# version 16
# baseline (speedup 1.0000x reference)
import math
import numpy as np
import ml_dtypes
from contextlib import ExitStack

import concourse.bass as bass
import concourse.mybir as mybir
from concourse.bass_utils import run_bass_kernel_spmd

F32 = mybir.dt.float32
BF16 = mybir.dt.bfloat16
AF = mybir.ActivationFunctionType
ALU = mybir.AluOpType
AX = mybir.AxisListType

D = 2048
H = 16
HD = 128
S = 2048
B = 4
OWN = 1024
NHID = 1024
NE = 8
SCL = 1.0 / math.sqrt(HD)
BIGM = 600.0
EXPSH = -4.0
EPS = 1e-5

ENGINES = ("tensor", "vector", "scalar", "gpsimd", "sync")


class Prog:

    def __init__(self, nc, ctx):
        self.nc = nc
        self.ctx = ctx
        self.ops = {e: [] for e in ENGINES}
        self.semh = {}
        self.semc = {}

    def sem(self, name):
        if name not in self.semh:
            self.semh[name] = self.ctx.enter_context(self.nc.semaphore(name))
            self.semc[name] = 0
        return self.semh[name]

    def val(self, name):
        self.sem(name)
        return self.semc[name]

    def op(self, eng, f, sem=None, n=1):
        if sem is None:
            self.ops[eng].append(f)
            return None
        h = self.sem(sem)
        self.semc[sem] += n
        self.ops[eng].append(lambda e, f=f, h=h, n=n: f(e).then_inc(h, n))
        return self.semc[sem]

    def wait(self, eng, sem, v=None):
        h = self.sem(sem)
        if v is None:
            v = self.semc[sem]
        if v > 0:
            self.ops[eng].append(lambda e, h=h, v=v: e.wait_ge(h, v))

    def dma(self, eng, out, in_, sem):
        return self.op(eng, lambda e, o=out, i=in_: e.dma_start(out=o, in_=i), sem, 16)

    def barrier(self, tag):
        vals = [(n, self.semc[n]) for n in list(self.semc) if self.semc[n] > 0]
        for eng in ENGINES:
            for n, v in vals:
                self.wait(eng, n, v)

    def finalize(self, block):
        def run(engine, ops):
            for f in ops:
                f(engine)
        block.tensor(lambda e: run(e, self.ops["tensor"]))
        block.vector(lambda e: run(e, self.ops["vector"]))
        block.scalar(lambda e: run(e, self.ops["scalar"]))
        block.gpsimd(lambda e: run(e, self.ops["gpsimd"]))
        block.sync(lambda e: run(e, self.ops["sync"]))


class Rot:

    def __init__(self, n):
        self.n = n
        self.free = [None] * n

    def acquire(self, p, eng, i):
        w = self.free[i % self.n]
        if w is not None:
            p.wait(eng, w[0], w[1])

    def release(self, i, sem, val):
        self.free[i % self.n] = (sem, val)


def build(debug=False, stage=3):
    nc = bass.Bass("TRN2", target_bir_lowering=False, debug=False)

    x_own = nc.dram_tensor("x_own", [OWN, D], F32, kind="ExternalInput").ap()
    x_oth = nc.dram_tensor("x_oth", [OWN, D], F32, kind="ExternalInput").ap()
    expb = nc.dram_tensor("expb", [128, 18], F32, kind="ExternalInput").ap()
    wqkv_t = nc.dram_tensor("wqkv_t", [H, 128, 3, 16, HD], F32, kind="ExternalInput").ap()
    bq_col = nc.dram_tensor("bq_col", [128, 48], F32, kind="ExternalInput").ap()
    b_qkv = nc.dram_tensor("b_qkv", [3 * D], F32, kind="ExternalInput").ap()
    w_outT = nc.dram_tensor("w_outT", [4, 128, 16, 512], F32, kind="ExternalInput").ap()
    b_out = nc.dram_tensor("b_out", [D], F32, kind="ExternalInput").ap()
    gate_wT = nc.dram_tensor("gate_wT", [D, NE], F32, kind="ExternalInput").ap()
    w1h = nc.dram_tensor("w1h", [NE, 8, 128, 16, 128], F32, kind="ExternalInput").ap()
    w3h = nc.dram_tensor("w3h", [NE, 8, 128, 16, 128], F32, kind="ExternalInput").ap()
    w2h = nc.dram_tensor("w2h", [NE, 128, 8, D], F32, kind="ExternalInput").ap()
    ws1h = nc.dram_tensor("ws1h", [8, 128, 16, 128], F32, kind="ExternalInput").ap()
    ws3h = nc.dram_tensor("ws3h", [8, 128, 16, 128], F32, kind="ExternalInput").ap()
    ws2h = nc.dram_tensor("ws2h", [128, 8, D], F32, kind="ExternalInput").ap()
    out_c = nc.dram_tensor("out", [OWN, D], F32, kind="ExternalOutput").ap()
    if debug:
        dbg_xn = nc.dram_tensor("dbg_xn", [128, 2 * 16 * OWN], BF16, kind="ExternalOutput").ap()
        dbg_attn = nc.dram_tensor("dbg_attn", [128, 16 * OWN], BF16, kind="ExternalOutput").ap()
        dbg_w = nc.dram_tensor("dbg_w", [128, 64], BF16, kind="ExternalOutput").ap()
        dbg_R = nc.dram_tensor("dbg_R", [128, 8 * D], F32, kind="ExternalOutput").ap()

    bf = ml_dtypes.bfloat16
    ident_d = nc.inline_tensor(np.eye(128, dtype=np.float32).astype(bf), name="identc").ap()
    ibig_d = nc.inline_tensor((BIGM * np.eye(128, dtype=np.float32)).astype(bf), name="ibigc").ap()
    tri = np.zeros((4, 128, 512), dtype=np.float32)
    for j2 in range(4):
        for k in range(128):
            tri[j2, k, j2 * 128 + k:] = 1.0
    tri_d = nc.inline_tensor(tri.astype(bf), name="tric").ap()
    selm = np.zeros((8, 8 * 128), dtype=np.float32)
    for e in range(8):
        selm[e, e * 128:(e + 1) * 128] = 1.0
    sel_d = nc.inline_tensor(selm.astype(bf), name="selc").ap()
    ones_d = nc.inline_tensor(np.ones((1, 128), dtype=np.float32).astype(bf), name="ones1c").ap()

    ctx = ExitStack()
    with ctx:
        p = Prog(nc, ctx)
        sb = lambda *a, **k: ctx.enter_context(nc.sbuf_tensor(*a, **k))
        xn_tok = sb("xn_tok", [128, 8 * D], BF16)
        w_sbp = sb("w_sbp", [128, 8 * 8], BF16)
        wf_sbp = sb("wf_sbp", [128, 8 * 8], F32)
        ident = sb("ident", [128, 128], BF16)
        ibig = sb("ibig", [128, 128], BF16)
        ones1 = sb("ones1", [1, 128], BF16)
        bqc_sb = sb("bqc_sb", [128, 48], F32)
        gw_sb = sb("gw_sb", [128, 16 * 8], BF16)
        expb_sb = sb("expb_sb", [128, 18], F32)
        s_xnF = ExitStack()
        ctx.enter_context(s_xnF)
        xn_own = s_xnF.enter_context(nc.sbuf_tensor("xn_own", [128, 16 * OWN], BF16, side="right"))
        s_attn = ExitStack()
        ctx.enter_context(s_attn)
        attn = s_attn.enter_context(nc.sbuf_tensor("attn", [128, 16 * OWN], BF16, side="right"))
        bor_sb = s_attn.enter_context(nc.sbuf_tensor("bor_sb", [1, D], BF16, side="right"))
        s_xnoth = ExitStack()
        ctx.enter_context(s_xnoth)
        xn_oth = s_xnoth.enter_context(nc.sbuf_tensor("xn_oth", [128, 16 * OWN], BF16, side="right"))
        tri_sb = s_xnoth.enter_context(nc.sbuf_tensor("tri_sb", [128, 4 * 512], BF16))

        p.dma("sync", ident[:], ident_d[:, :], "D_c")
        p.dma("sync", ibig[:], ibig_d[:, :], "D_c")
        p.dma("sync", tri_sb[:].rearrange("p (m q) -> p m q", m=4),
              tri_d[:, :, :].rearrange("m p q -> p m q"), "D_c")
        p.dma("sync", ones1[:], ones_d[:, :], "D_c")
        p.dma("sync", bqc_sb[:], bq_col[:, :], "D_c")
        p.dma("sync", expb_sb[:], expb[:, :], "D_c")
        p.dma("gpsimd", bor_sb[:], b_out[:].rearrange("(a n) -> a n", a=1), "D_cg")
        p.dma("gpsimd", gw_sb[:].rearrange("p (c e) -> p c e", c=16),
              gate_wT.rearrange("(c p) e -> p c e", p=128), "D_cg")
        cst_v = p.val("D_c")
        cstg_v = p.val("D_cg")

        with ExitStack() as ph0:
            sb0 = lambda *a, **k: ph0.enter_context(nc.sbuf_tensor(*a, **k))
            ps0 = [ph0.enter_context(nc.psum_tensor(f"p0_{i}", [128, 1024], BF16)) for i in range(6)]
            xt = [sb0(f"xt{i}", [128, D], F32) for i in range(3)]
            sqj = sb0("sqj", [128, D], BF16)
            sqj2 = sb0("sqj2", [128, D], BF16)
            st1 = [sb0(f"st1{i}", [128, 8], F32) for i in range(3)]
            xntk = [sb0(f"xntk{i}", [128, D], BF16) for i in range(3)]
            xc = [sb0(f"xc{i}", [128, D], F32) for i in range(2)]
            rcol = [sb0(f"rcol{i}", [128, 2], F32) for i in range(3)]
            p0rot = Rot(6)
            xtk_cnt = [0]

            for tt in range(16):
                own = tt < 8
                src = (x_own if own else x_oth)
                dst = (xn_own if own else xn_oth)
                ttl = tt % 8
                b3 = tt % 3
                ss = tt % 3
                xcs = tt % 2
                if tt >= 3:
                    p.wait("sync", "lnA", tt - 3 + 1)
                xsem = f"D_x{tt % 3}"
                xv = p.dma("sync", xt[b3][:], src[ttl * 128:(ttl + 1) * 128, :], xsem)
                p.wait("vector", xsem, xv)
                if tt >= 3:
                    p.wait("vector", "lnA", tt - 3 + 1)
                p.wait("scalar", xsem, xv)
                if tt >= 3:
                    p.wait("scalar", "lnA", tt - 3 + 1)
                v1 = p.op("scalar", lambda e, b=b3, s=ss: e.activation(
                    sqj2[:], xt[b][:], AF.Copy, bias=0.0, scale=1.0,
                    accum_out=st1[s][:, 0:1]), "v1")
                s2v = p.op("scalar", lambda e, b=b3, s=ss: e.activation(
                    sqj[:], xt[b][:], AF.Square, bias=0.0, scale=1.0,
                    accum_out=st1[s][:, 1:2]), "sq")
                p.wait("vector", "sq", s2v)
                ev = p.op("vector", lambda e, s=ss: e.tensor_scalar(
                    st1[s][:, 3:4], st1[s][:, 1:2], 1.0 / D, EPS, ALU.mult, ALU.add), "v2")
                p.wait("vector", "v1", v1)
                mv = p.op("vector", lambda e, s=ss: e.tensor_scalar(
                    st1[s][:, 2:3], st1[s][:, 0:1], -1.0 / D, None, ALU.mult, ALU.bypass), "v2")
                p.wait("vector", "v2", mv)
                m2v = p.op("vector", lambda e, s=ss: e.tensor_scalar(
                    st1[s][:, 4:5], st1[s][:, 2:3], st1[s][:, 2:3], None, ALU.mult, ALU.bypass), "v2")
                p.wait("vector", "v2", m2v)
                vpev = p.op("vector", lambda e, s=ss: e.tensor_sub(
                    st1[s][:, 5:6], st1[s][:, 3:4], st1[s][:, 4:5]), "v2")
                p.wait("scalar", "v2", vpev)
                sdv = p.op("scalar", lambda e, s=ss: e.activation(
                    st1[s][:, 6:7], st1[s][:, 5:6], AF.Sqrt, bias=0.0, scale=1.0), "s2")
                if tt >= 2:
                    p.wait("vector", "lnA", tt - 2 + 1)
                xcv = p.op("vector", lambda e, b=b3, s=ss, xs=xcs: e.tensor_scalar(
                    xc[xs][:], xt[b][:], st1[s][:, 2:3], None, ALU.add, ALU.bypass), "xcv")
                p.wait("vector", "s2", sdv)
                rsv = p.op("vector", lambda e, s=ss: e.reciprocal(rcol[s][:, 0:1], st1[s][:, 6:7]), "v3")
                p.wait("scalar", "v3", rsv)
                p.wait("scalar", "xcv", xcv)
                if tt >= 3:
                    p.wait("scalar", "xt", 16 * (tt - 2))
                if tt >= 3:
                    p.wait("scalar", "xtk", max(0, xtk_cnt[0] - 2))
                lnv = p.op("scalar", lambda e, s=ss, xs=xcs: e.activation(
                    xntk[s][:], xc[xs][:], AF.Copy, bias=0.0, scale=rcol[s][:, 0:1]), "lnA")
                if own:
                    p.wait("vector", "lnA", lnv)
                    p.op("vector", lambda e, s=ss, ttl=ttl: e.tensor_copy(
                        xn_tok[:, ttl * D:(ttl + 1) * D], xntk[s][:]), "xtk")
                    xtk_cnt[0] += 1
                p.wait("tensor", "lnA", lnv)
                if tt == 0:
                    p.wait("tensor", "D_c", cst_v)
                for dc in range(16):
                    p0rot.acquire(p, "tensor", tt * 16 + dc)
                    tv = p.op("tensor", lambda e, s=ss, dc=dc, r=(tt * 16 + dc) % 6: e.transpose(
                        ps0[r][:, 0:128], xntk[s][:, dc * 128:(dc + 1) * 128], ident[:]), "xt")
                    eng = "vector" if dc % 2 == 0 else "scalar"
                    p.wait(eng, "xt", tv)
                    if eng == "vector":
                        ev2 = p.op(eng, lambda e, d=dst, dc=dc, ttl=ttl, r=(tt * 16 + dc) % 6: e.tensor_copy(
                            d[:, dc * OWN + ttl * 128: dc * OWN + (ttl + 1) * 128], ps0[r][:, 0:128]), "xe")
                        p0rot.release(tt * 16 + dc, "xe", ev2)
                    else:
                        ev2 = p.op(eng, lambda e, d=dst, dc=dc, ttl=ttl, r=(tt * 16 + dc) % 6: e.activation(
                            d[:, dc * OWN + ttl * 128: dc * OWN + (ttl + 1) * 128], ps0[r][:, 0:128],
                            AF.Copy, bias=0.0, scale=1.0), "xe2")
                        p0rot.release(tt * 16 + dc, "xe2", ev2)

        p.barrier("b01")
        if debug:
            p.dma("sync", dbg_xn[:, 0:16 * OWN], xn_own[:], "D_sd")
            p.dma("sync", dbg_xn[:, 16 * OWN:], xn_oth[:], "D_sd")

        if stage >= 1:
         with ExitStack() as ph1:
            sb1 = lambda *a, **k: ph1.enter_context(nc.sbuf_tensor(*a, **k))
            pP = [ph1.enter_context(nc.psum_tensor(f"pP{i}", [128, 1024], F32)) for i in range(2)]
            pA = [ph1.enter_context(nc.psum_tensor(f"pA{i}", [128, 512], F32)) for i in range(2)]
            pT = [ph1.enter_context(nc.psum_tensor(f"pT{i}", [128, 1024], BF16)) for i in range(2)]
            qkvw = [sb1("qkvw0", [128, 3 * 16 * 128], BF16)]
            K_sb = [sb1("K0", [128, S], BF16)]
            Q_sb = [sb1(f"Q{i}", [128, OWN], BF16) for i in range(2)]
            V_sb = [sb1("V0", [128, S], BF16)]
            VT_sb = [sb1(f"VT{i}", [128, 16 * 129], BF16) for i in range(2)]
            PT_sb = [sb1(f"PT{i}", [128, 16 * 512], BF16) for i in range(2)]
            AT_sb = [sb1(f"AT{i}", [128, 128], BF16) for i in range(2)]
            rc_sb = [sb1(f"rc{i}", [128, 1], F32) for i in range(2)]
            rS = Rot(2)
            rQp = Rot(4)
            rAp = Rot(2)
            rTp = Rot(2)
            rAT = Rot(2)

            for i in range(2):
                p.op("vector", lambda e, i=i: e.memset(
                    VT_sb[i][:].rearrange("p (kc c) -> p kc c", c=129)[:, :, 128:129], 1.0), "vte")

            def xn_rhs(dc, c0, n):
                if c0 < OWN:
                    assert c0 + n <= OWN
                    return xn_own[:, dc * OWN + c0: dc * OWN + c0 + n]
                assert c0 + n <= 2 * OWN
                return xn_oth[:, dc * OWN + c0 - OWN: dc * OWN + c0 - OWN + n]

            sched = {0: [(kc, 'diag', kc) for kc in range(4)] + [(kc, 'data', None) for kc in range(8, 16)],
                     1: [(kc, 'plain', None) for kc in range(4)] + [(kc, 'diag', kc - 4) for kc in range(4, 8)]
                        + [(kc, 'data', None) for kc in range(8, 16)]}

            qknt = 0
            unit = -1
            avdone = {}
            ptv_maps = {}
            pending_post = []

            def emit_qkv(h, b):
                nonlocal qknt
                wv = wq_v[h]
                p.wait("tensor", "D_wq0", wv)
                p.wait("tensor", "pt", p.val("pt"))
                for which, nchunks, dstl in ((1, 4, 'K'), (2, 4, 'V'), (0, 2, 'Q')):
                    for c4 in range(nchunks):
                        rQp.acquire(p, "tensor", qknt)
                        t2, hf = (qknt // 2) % 2, qknt % 2
                        for dc in range(16):
                            p.op("tensor", lambda e, b=b, w=which, dc=dc, c4=c4, t2=t2, hf=hf: e.matmul(
                                pP[t2][:, hf * 512:(hf + 1) * 512],
                                qkvw[0][:, (w * 16 + dc) * 128:(w * 16 + dc + 1) * 128],
                                xn_rhs(dc, c4 * 512, 512),
                                start=(dc == 0), stop=(dc == 15)),
                                "mmQ" if dc == 15 else None, 1)
                        mv = p.val("mmQ")
                        p.wait("vector", "mmQ", mv)
                        bcol = bqc_sb[:, which * 16 + h: which * 16 + h + 1]
                        if dstl == 'K':
                            dst = K_sb[0][:, c4 * 512:(c4 + 1) * 512]
                        elif dstl == 'V':
                            dst = V_sb[0][:, c4 * 512:(c4 + 1) * 512]
                        else:
                            dst = Q_sb[b][:, c4 * 512:(c4 + 1) * 512]
                        ev = p.op("vector", lambda e, dst=dst, t2=t2, hf=hf, bcol=bcol: e.tensor_scalar(
                            dst, pP[t2][:, hf * 512:(hf + 1) * 512], bcol, None, ALU.add, ALU.bypass), "kqv")
                        rQp.release(qknt, "kqv", ev)
                        qknt += 1
                relw[h] = p.val("mmQ")
                kv = p.val("kqv")
                p.wait("tensor", "kqv", kv)
                if h >= 2:
                    p.wait("vector", "av", av_after.get(h - 2, 0))
                for kc in range(16):
                    rTp.acquire(p, "tensor", trn[0])
                    tv = p.op("tensor", lambda e, b=b, kc=kc, r=trn[0] % 2: e.transpose(
                        pT[r][:, 0:128], V_sb[0][:, kc * 128:(kc + 1) * 128], ident[:]), "trp")
                    p.wait("vector", "trp", tv)
                    ev = p.op("vector", lambda e, b=b, kc=kc, r=trn[0] % 2: e.tensor_copy(
                        VT_sb[b][:, kc * 129: kc * 129 + 128], pT[r][:, 0:128]), "vte")
                    rTp.release(trn[0], "vte", ev)
                    trn[0] += 1
                relvs[h] = p.val("trp")
                vte_after[h] = p.val("vte")

            def emit_scores(h, qb, b, u):
                ptb = u % 2
                if u >= 2:
                    p.wait("scalar", "av", avdone[u - 2])
                ptv_map = {}
                p.wait("tensor", "kqv", p.val("kqv"))
                chunks = sched[qb]
                for pi in range(0, len(chunks), 2):
                    pair = chunks[pi:pi + 2]
                    rS.acquire(p, "tensor", sck[0])
                    pr = sck[0] % 2
                    smv = None
                    for hf, (kc, kind, j2) in enumerate(pair):
                        fin = (hf == 1 and kind != 'diag')
                        smv = p.op("tensor", lambda e, b=b, kc=kc, qb=qb, pr=pr, hf=hf,
                             last=(kind != 'diag'): e.matmul(
                            pP[pr][:, hf * 512:(hf + 1) * 512], K_sb[0][:, kc * 128:(kc + 1) * 128],
                            Q_sb[b][:, qb * 512:(qb + 1) * 512],
                            start=True, stop=last), "mmS" if fin else None)
                        if kind == 'diag':
                            smv = p.op("tensor", lambda e, j2=j2, pr=pr, hf=hf: e.matmul(
                                pP[pr][:, hf * 512:(hf + 1) * 512], ibig[:],
                                tri_sb[:, j2 * 512:(j2 + 1) * 512],
                                start=False, stop=True), "mmS" if hf == 1 else None)
                    p.wait("scalar", "mmS", smv)
                    kc0, kind0, _ = pair[0]
                    if kind0 == 'diag':
                        bias = expb_sb[:, 17:18]
                    elif kind0 == 'plain':
                        bias = expb_sb[:, 16:17]
                    else:
                        bias = expb_sb[:, qb * 8 + (kc0 - 8): qb * 8 + (kc0 - 8) + 1]
                    ev = p.op("scalar", lambda e, ptb=ptb, kc0=kc0, pr=pr, bias=bias: e.activation(
                        PT_sb[ptb][:, kc0 * 512:(kc0 + 2) * 512], pP[pr][:], AF.Exp,
                        bias=bias, scale=SCL), "pt")
                    rS.release(sck[0], "pt", ev)
                    for (kc, _, _) in pair:
                        ptv_map[kc] = ev
                    sck[0] += 1
                ptv_maps[u] = ptv_map

            def post_av(h, qb, ql, av_idx, avv):
                pr = av_idx % 2
                p.wait("vector", "av", avv)
                rcv = p.op("vector", lambda e, pr=pr, r=av_idx % 2: e.reciprocal(
                    rc_sb[r][:], pA[pr][:, 128:129]), "rc")
                p.wait("scalar", "rc", rcv)
                rAT.acquire(p, "scalar", av_idx)
                atv = p.op("scalar", lambda e, pr=pr, r=av_idx % 2: e.activation(
                    AT_sb[r][:], pA[pr][:, 0:128], AF.Copy, bias=0.0, scale=rc_sb[r][:, 0:1]), "at")
                rAp.release(av_idx, "at", atv)
                p.wait("tensor", "at", atv)
                rTp.acquire(p, "tensor", trn[0])
                tv = p.op("tensor", lambda e, r=av_idx % 2, tr=trn[0] % 2: e.transpose(
                    pT[tr][:, 0:128], AT_sb[r][:], ident[:]), "trp")
                rAT.release(av_idx, "trp", tv)
                p.wait("scalar", "trp", tv)
                col = h * OWN + qb * 512 + ql * 128
                ev = p.op("scalar", lambda e, col=col, tr=trn[0] % 2: e.activation(
                    attn[:, col:col + 128], pT[tr][:, 0:128],
                    AF.Copy, bias=0.0, scale=1.0), "ao")
                rTp.release(trn[0], "ao", ev)
                trn[0] += 1

            def emit_av(h, qb, b, u):
                ptb = u % 2
                p.wait("tensor", "vte", vte_after[h])
                pend = pending_post
                for ql in range(4):
                    rAp.acquire(p, "tensor", avk[0])
                    pr = avk[0] % 2
                    chunks = [c[0] for c in sched[qb]]
                    avv = None
                    for i, kc in enumerate(chunks):
                        if ql == 0:
                            p.wait("tensor", "pt", ptv_maps[u][kc])
                        avv = p.op("tensor", lambda e, ptb=ptb, kc=kc, ql=ql, b=b, pr=pr,
                             st=(i == 0), sp=(i == len(chunks) - 1): e.matmul(
                            pA[pr][:, 0:129], PT_sb[ptb][:, kc * 512 + ql * 128: kc * 512 + (ql + 1) * 128],
                            VT_sb[b][:, kc * 129: kc * 129 + 129],
                            start=st, stop=sp), "av" if i == len(chunks) - 1 else None)
                    if pend:
                        post_av(*pend.pop(0))
                    pend.append((h, qb, ql, avk[0], avv))
                    avk[0] += 1
                avdone[u] = p.val("av")

            wq_v = {}

            def emit_wq_dma(h):
                if h >= 1:
                    p.wait("gpsimd", "mmQ", relw[h - 1])
                wq_v[h] = p.dma("gpsimd",
                                qkvw[0][:].rearrange("p (w dc hd) -> p w dc hd", w=3, dc=16),
                                wqkv_t[h], "D_wq0")

            sck = [0]
            avk = [0]
            trn = [0]
            relw = {}
            relvs = {}
            vte_after = {}
            ms_after = {}
            av_after = {}
            emit_wq_dma(0)
            for h in range(H):
                b = h % 2
                if h >= 1:
                    p.wait("vector", "mmS", ms_after.get(h - 1, 0))
                    p.wait("vector", "trp", relvs.get(h - 1, 0))
                if h >= 2:
                    p.wait("vector", "mmS", ms_after.get(h - 2, 0))
                emit_qkv(h, b)
                if h + 1 < H:
                    emit_wq_dma(h + 1)
                for qb in range(2):
                    unit += 1
                    emit_scores(h, qb, b, unit)
                    emit_av(h, qb, b, unit)
                ms_after[h] = p.val("mmS")
                av_after[h] = p.val("av")
            while pending_post:
                post_av(*pending_post.pop(0))

        s_xnoth.close()
        p.barrier("b12")
        if debug and stage >= 1:
            p.dma("sync", dbg_attn[:], attn[:], "D_sd")

        R = sb("R", [128, 8 * D], F32)
        wT_sb = sb("wT_sb", [8, OWN], BF16)
        if stage >= 2:
         with ExitStack() as ph2a:
            sb2 = lambda *a, **k: ph2a.enter_context(nc.sbuf_tensor(*a, **k))
            pR = [ph2a.enter_context(nc.psum_tensor(f"pR{i}", [128, 512], F32)) for i in range(2)]
            pGm = ph2a.enter_context(nc.psum_tensor("pGm", [128, 512], F32))
            pW8 = ph2a.enter_context(nc.psum_tensor("pW8", [8, 512], F32))
            wo_sb = [sb2(f"wo{i}", [128, 16 * 512], BF16) for i in range(2)]
            sg_sb = sb2("sg", [128, 8 * 8], F32)
            tg_sb = sb2("tg", [128, 8 * 8 * 3], F32)
            rg_sb = sb2("rg", [128, 8 * 2], F32)
            w_sb = w_sbp
            rR = Rot(2)
            rwo = Rot(2)

            for tt in range(8):
                for dc in range(16):
                    p.op("tensor", lambda e, tt=tt, dc=dc: e.matmul(
                        pGm[:, tt * 8:(tt + 1) * 8],
                        xn_own[:, dc * OWN + tt * 128: dc * OWN + (tt + 1) * 128],
                        gw_sb[:, dc * 8:(dc + 1) * 8],
                        start=(dc == 0), stop=(dc == 15)),
                        "mmG" if (tt == 7 and dc == 15) else None, 1)
            gv = p.val("mmG")
            p.wait("scalar", "mmG", gv)
            p.op("scalar", lambda e: e.activation(
                sg_sb[:], pGm[:, 0:64], AF.Sigmoid, bias=0.0, scale=1.0), "sgm")
            sgv = p.val("sgm")
            p.wait("vector", "sgm", sgv)
            for tt in range(8):
                p.op("vector", lambda e, tt=tt: e.reduce_max(
                    rg_sb[:, tt * 2: tt * 2 + 1], sg_sb[:, tt * 8:(tt + 1) * 8], axis=AX.X), "g1")
            p.wait("vector", "g1", p.val("g1"))
            for tt in range(8):
                p.op("vector", lambda e, tt=tt: e.tensor_scalar(
                    tg_sb[:, tt * 8:(tt + 1) * 8], sg_sb[:, tt * 8:(tt + 1) * 8],
                    rg_sb[:, tt * 2: tt * 2 + 1], -1e30, ALU.is_ge, ALU.mult), "g1")
            p.wait("vector", "g1", p.val("g1"))
            for tt in range(8):
                p.op("vector", lambda e, tt=tt: e.tensor_add(
                    tg_sb[:, 64 + tt * 8: 64 + (tt + 1) * 8], tg_sb[:, tt * 8:(tt + 1) * 8],
                    sg_sb[:, tt * 8:(tt + 1) * 8]), "g1")
            p.wait("vector", "g1", p.val("g1"))
            for tt in range(8):
                p.op("vector", lambda e, tt=tt: e.reduce_max(
                    rg_sb[:, tt * 2 + 1: tt * 2 + 2], tg_sb[:, 64 + tt * 8: 64 + (tt + 1) * 8], axis=AX.X), "g1")
            p.wait("vector", "g1", p.val("g1"))
            for tt in range(8):
                p.op("vector", lambda e, tt=tt: e.tensor_scalar(
                    tg_sb[:, 128 + tt * 8: 128 + (tt + 1) * 8], sg_sb[:, tt * 8:(tt + 1) * 8],
                    rg_sb[:, tt * 2 + 1: tt * 2 + 2], None, ALU.is_ge, ALU.bypass), "g1")
            p.wait("vector", "g1", p.val("g1"))
            for tt in range(8):
                p.op("vector", lambda e, tt=tt: e.tensor_mul(
                    w_sb[:, tt * 8:(tt + 1) * 8], tg_sb[:, 128 + tt * 8: 128 + (tt + 1) * 8],
                    sg_sb[:, tt * 8:(tt + 1) * 8]), "g2")
            p.wait("vector", "g2", p.val("g2"))
            p.op("vector", lambda e: e.tensor_copy(wf_sbp[:], w_sbp[:]), "g2")
            g2v = p.val("g2")
            if debug:
                p.wait("sync", "g2", g2v)
                p.dma("sync", dbg_w[:], w_sb[:], "D_sd")
            p.wait("tensor", "g2", g2v)
            for half in range(2):
                for i in range(4):
                    tt = half * 4 + i
                    p.op("tensor", lambda e, tt=tt, i=i: e.matmul(
                        pW8[0:8, i * 128:(i + 1) * 128], w_sb[:, tt * 8:(tt + 1) * 8],
                        ident[:], start=True, stop=True),
                        "mmW" if i == 3 else None, 1)
                wv = p.val("mmW")
                p.wait("scalar", "mmW", wv)
                sv = p.op("scalar", lambda e, half=half: e.activation(
                    wT_sb[0:8, half * 512:(half + 1) * 512], pW8[0:8, 0:512],
                    AF.Copy, bias=0.0, scale=1.0), "wTe")
                p.wait("tensor", "wTe", sv)

            ork = [0]
            wo_v = {}

            def emit_wo_dma(dc):
                if dc >= 2:
                    p.wait("gpsimd", "mmO", 8 * (dc - 1))
                wo_v[dc] = p.dma("gpsimd",
                                 wo_sb[dc % 2][:].rearrange("p (dcin n) -> p dcin n", dcin=16),
                                 w_outT[dc], f"D_wo{dc % 2}")

            emit_wo_dma(0)
            emit_wo_dma(1)
            for dc in range(4):
                if dc + 2 < 4:
                    emit_wo_dma(dc + 2)
                p.wait("tensor", f"D_wo{dc % 2}", wo_v[dc])
                for tt in range(8):
                    rR.acquire(p, "tensor", ork[0])
                    pr = ork[0] % 2
                    p.op("tensor", lambda e, dc=dc, pr=pr: e.matmul(
                        pR[pr][:], ones1[0:1, :], bor_sb[0:1, dc * 512:(dc + 1) * 512],
                        start=True, stop=False), None)
                    for dcin in range(16):
                        p.op("tensor", lambda e, dcin=dcin, tt=tt, dc=dc, pr=pr: e.matmul(
                            pR[pr][:], attn[:, dcin * OWN + tt * 128: dcin * OWN + (tt + 1) * 128],
                            wo_sb[dc % 2][:, dcin * 512:(dcin + 1) * 512],
                            start=False, stop=(dcin == 15)),
                            "mmO" if dcin == 15 else None, 1)
                    ov = p.val("mmO")
                    p.wait("scalar", "mmO", ov)
                    ev = p.op("scalar", lambda e, tt=tt, dc=dc, pr=pr: e.activation(
                        R[:, tt * D + dc * 512: tt * D + (dc + 1) * 512], pR[pr][:],
                        AF.Copy, bias=0.0, scale=1.0), "rop")
                    rR.release(ork[0], "rop", ev)
                    ork[0] += 1

        s_attn.close()
        p.barrier("b2ab")

        CAP = 384
        if stage >= 3:
         with ExitStack() as ph2o:
            sbo = lambda *a, **k: ph2o.enter_context(nc.sbuf_tensor(*a, **k))
            w13 = [sbo(f"w13_{i}", [128, 2 * 16 * 128], BF16) for i in range(2)]
            w2s = sbo("w2s", [128, 8 * D], BF16)
            ts1 = [sbo(f"ts1{i}", [128, 512], F32) for i in range(2)]
            mm3_after = {}
            mm2_after = {}
            w13_v = {}
            w2_v = {}

            def w13_src(ee, hc, which):
                if ee == 0:
                    return (ws1h if which == 0 else ws3h)[hc]
                e = ee - 1
                return (w1h if which == 0 else w3h)[e, hc]

            def w2_src(ee):
                if ee == 0:
                    return ws2h[:, :, :]
                return w2h[ee - 1]

            def emit_w13_dma(ee, hc):
                idx = ee * 8 + hc
                if idx >= 2:
                    p.wait("gpsimd", "mm3", mm3_after[idx - 2])
                for which in range(2):
                    v = p.dma("gpsimd",
                              w13[idx % 2][:, which * 2048:(which + 1) * 2048].rearrange(
                                  "p (dc h) -> p dc h", dc=16),
                              w13_src(ee, hc, which),
                              f"D_w13{idx % 2}")
                w13_v[idx] = v

            def emit_w2_chunk(ee, c):
                if ee >= 1:
                    p.wait("gpsimd", "mm2", mm2_after[ee - 1])
                w2_v[ee] = p.dma("gpsimd",
                                 w2s[:, c * 2 * D:(c + 1) * 2 * D].rearrange(
                                     "p (hc n) -> p hc n", hc=2),
                                 w2_src(ee)[:, c * 2:(c + 1) * 2, :],
                                 "D_w2")

            emit_w13_dma(0, 0)
            emit_w13_dma(0, 1)

            with ExitStack() as ph2s:
                sbs = lambda *a, **k: ph2s.enter_context(nc.sbuf_tensor(*a, **k))
                pG1 = [ph2s.enter_context(nc.psum_tensor(f"pG1{i}", [128, 512], F32)) for i in range(2)]
                pG3 = [ph2s.enter_context(nc.psum_tensor(f"pG3{i}", [128, 512], F32)) for i in range(2)]
                pR2 = [ph2s.enter_context(nc.psum_tensor(f"pR2{i}", [128, 512], F32)) for i in range(2)]
                Hsh = sbs("Hsh", [128, 8 * OWN], BF16)
                rG1 = Rot(2)
                rG3 = Rot(2)
                rR2 = Rot(2)
                rTmp = Rot(2)
                gck = [0]
                w2ck = [0]
                for hc in range(8):
                    idx = hc
                    wb13 = idx % 2
                    p.wait("tensor", f"D_w13{idx % 2}", w13_v[idx])
                    for tc in range(2):
                        rG1.acquire(p, "tensor", gck[0])
                        rG3.acquire(p, "tensor", gck[0])
                        pr = gck[0] % 2
                        for dc in range(16):
                            p.op("tensor", lambda e, wb13=wb13, dc=dc, tc=tc, pr=pr: e.matmul(
                                pG1[pr][:], w13[wb13][:, dc * 128:(dc + 1) * 128],
                                xn_own[:, dc * OWN + tc * 512: dc * OWN + (tc + 1) * 512],
                                start=(dc == 0), stop=(dc == 15)),
                                "mm1" if dc == 15 else None, 1)
                        m1v = p.val("mm1")
                        for dc in range(16):
                            p.op("tensor", lambda e, wb13=wb13, dc=dc, tc=tc, pr=pr: e.matmul(
                                pG3[pr][:], w13[wb13][:, 2048 + dc * 128:2048 + (dc + 1) * 128],
                                xn_own[:, dc * OWN + tc * 512: dc * OWN + (tc + 1) * 512],
                                start=(dc == 0), stop=(dc == 15)),
                                "mm3" if dc == 15 else None, 1)
                        m3v = p.val("mm3")
                        rTmp.acquire(p, "scalar", gck[0])
                        tb = gck[0] % 2
                        p.wait("scalar", "mm1", m1v)
                        s1v = p.op("scalar", lambda e, pr=pr, tb=tb: e.activation(
                            ts1[tb][:], pG1[pr][:], AF.Silu, bias=0.0, scale=1.0), "si")
                        rG1.release(gck[0], "si", s1v)
                        p.wait("vector", "mm3", m3v)
                        p.wait("vector", "si", s1v)
                        hv = p.op("vector", lambda e, pr=pr, tb=tb, hc=hc, tc=tc: e.tensor_mul(
                            Hsh[:, hc * OWN + tc * 512: hc * OWN + (tc + 1) * 512],
                            ts1[tb][:], pG3[pr][:]), "hh")
                        rG3.release(gck[0], "hh", hv)
                        rTmp.release(gck[0], "hh", hv)
                        gck[0] += 1
                    mm3_after[idx] = p.val("mm3")
                    if idx + 2 < 72:
                        emit_w13_dma(*divmod(idx + 2, 8))
                    if hc % 2 == 0:
                        emit_w2_chunk(0, hc // 2)
                hhv = p.val("hh")
                p.wait("tensor", "D_w2", w2_v[0])
                p.wait("tensor", "hh", hhv)
                for tt in range(8):
                    for dc in range(4):
                        rR2.acquire(p, "tensor", w2ck[0])
                        pr = w2ck[0] % 2
                        for hc in range(8):
                            p.op("tensor", lambda e, hc=hc, tt=tt, dc=dc, pr=pr: e.matmul(
                                pR2[pr][:], Hsh[:, hc * OWN + tt * 128: hc * OWN + (tt + 1) * 128],
                                w2s[:, hc * D + dc * 512: hc * D + (dc + 1) * 512],
                                start=(hc == 0), stop=(hc == 7)),
                                "mm2" if hc == 7 else None, 1)
                        m2v = p.val("mm2")
                        p.wait("vector", "mm2", m2v)
                        av = p.op("vector", lambda e, tt=tt, dc=dc, pr=pr: e.tensor_add(
                            R[:, tt * D + dc * 512: tt * D + (dc + 1) * 512],
                            R[:, tt * D + dc * 512: tt * D + (dc + 1) * 512], pR2[pr][:]), "rad")
                        rR2.release(w2ck[0], "rad", av)
                        w2ck[0] += 1
                mm2_after[0] = p.val("mm2")
                for c in range(4):
                    emit_w2_chunk(1, c)

            s_xnF.close()
            p.barrier("b2bc")

            iota_cap_d = nc.inline_tensor(
                np.tile(np.arange(1, CAP + 1, dtype=np.float32), (128, 1)), name="iotacapc")
            identf_d = nc.inline_tensor(np.eye(8, dtype=np.float32), name="identfc")
            with ExitStack() as ph2e:
                sbe = lambda *a, **k: ph2e.enter_context(nc.sbuf_tensor(*a, **k))
                pG1e = [ph2e.enter_context(nc.psum_tensor(f"pG1eb{i}", [128, 512], F32)) for i in range(2)]
                pXS = pG1e
                pG3e = [ph2e.enter_context(nc.psum_tensor(f"pG3eb{i}", [128, 512], F32)) for i in range(2)]
                pT2 = [ph2e.enter_context(nc.psum_tensor(f"pT2{i}", [128, 1024], BF16)) for i in range(2)]
                pD = [ph2e.enter_context(nc.psum_tensor(f"pD{i}", [128, 512], F32)) for i in range(2)]
                xs_sb = sbe("xs_sb", [128, 16 * CAP], BF16)
                Hsp = sbe("Hsp", [128, 8 * CAP], BF16)
                rout = sbe("rout", [128, 3 * D], BF16)
                P_sb = sbe("P_sb", [128, 8 * CAP], BF16)
                PTs = [sbe(f"PTs{i}", [128, 3 * OWN], BF16) for i in range(2)]
                posm = sbe("posm", [128, OWN], F32)
                P01 = [sbe(f"P01_{i}", [128, CAP], BF16) for i in range(2)]
                poscol = sbe("poscol", [128, 8 * 8], F32)
                mcol_sb = sbe("mcol_sb", [128, 8 * 8], F32)
                iota_cap = sbe("iota_cap", [128, CAP], F32)
                identf = sbe("identf", [8, 8], F32)
                p.dma("sync", iota_cap[:], iota_cap_d[:, :], "D_c2")
                p.dma("sync", identf[:], identf_d[:, :], "D_c2")
                ic_v = p.val("D_c2")

                wtv = p.val("wTe")
                p.wait("vector", "wTe", wtv)
                mv = p.op("vector", lambda e: e.tensor_scalar(
                    posm[32:40, :], wT_sb[0:8, :], 0.0, None, ALU.is_gt, ALU.bypass), "pv")
                p.wait("vector", "pv", mv)
                cv3 = p.op("vector", lambda e: e.tensor_tensor_scan(
                    posm[0:8, :], posm[32:40, :], posm[32:40, :], 0.0, ALU.add, ALU.bypass), "pv")
                p.op("vector", lambda e: e.tensor_scalar(
                    mcol_sb[:], wf_sbp[:], 0.0, None, ALU.is_gt, ALU.bypass), "pv")
                p.wait("tensor", "pv", p.val("pv"))
                p.wait("tensor", "D_c2", ic_v)
                for tc in range(8):
                    if tc >= 1:
                        p.wait("tensor", "pce", tc)
                    tv = p.op("tensor", lambda e, tc=tc: e.transpose(
                        pXS[0][:, 0:8], posm[0:8, tc * 128:(tc + 1) * 128], identf[:]), "ptr")
                    p.wait("vector", "ptr", tv)
                    pcv = p.op("vector", lambda e, tc=tc: e.tensor_copy(
                        poscol[:, tc * 8:(tc + 1) * 8], pXS[0][:, 0:8]), "pce")
                p.wait("tensor", "pce", p.val("pce"))
                p.wait("tensor", "D_c2", ic_v)

                rXS = Rot(2)
                rG1e_shared = rXS
                rPTrot = Rot(2)
                ptk = [0]
                poolk = [0]
                rG1e = Rot(2)
                rG3e = Rot(2)
                rD = Rot(2)
                rTmpE = Rot(2)
                xsk = [0]
                gck = [0]
                dk = [0]
                mmxs_after = {}
                mm2b_after = {}
                mmsc_after = {}

                pb_after = {}
                ptb_after = {}

                def emit_ptbuild(ex):
                    ptbuf = ex % 2
                    if ex >= 2:
                        p.wait("vector", "mmSC", mmsc_after[ex - 2])
                    for tc in range(8):
                        if ptk[0] >= 2:
                            p.wait("vector", "pttr", p.val("pttr") - 1)
                        pv01 = p.op("vector", lambda e, tc=tc, ex=ex, pb2=tc % 2: e.tensor_scalar(
                            P01[pb2][:], iota_cap[:],
                            poscol[:, tc * 8 + ex: tc * 8 + ex + 1],
                            mcol_sb[:, tc * 8 + ex: tc * 8 + ex + 1],
                            ALU.is_equal, ALU.mult), "p01")
                        p.wait("tensor", "p01", pv01)
                        for sc in range(3):
                            rPTrot.acquire(p, "tensor", ptk[0])
                            tv2 = p.op("tensor", lambda e, sc=sc, r=ptk[0] % 2, pb2=tc % 2: e.transpose(
                                pT2[r][:, 0:128], P01[pb2][:, sc * 128:(sc + 1) * 128], ident[:]), "pttr")
                            p.wait("scalar", "pttr", tv2)
                            cpv = p.op("scalar", lambda e, sc=sc, tc=tc, r=ptk[0] % 2, ptbuf=ptbuf: e.activation(
                                PTs[ptbuf][:, sc * OWN + tc * 128:(sc * OWN + (tc + 1) * 128)],
                                pT2[r][:, 0:128], AF.Copy, bias=0.0, scale=1.0), "ptb")
                            rPTrot.release(ptk[0], "ptb", cpv)
                            ptk[0] += 1
                    ptb_after[ex] = p.val("ptb")

                def emit_pbuild(ex):
                    if ex >= 1:
                        p.wait("vector", "mmXS", mmxs_after[ex - 1])
                    p.wait("vector", "pce", p.val("pce"))
                    for tc in range(8):
                        p.op("vector", lambda e, tc=tc, ex=ex: e.tensor_scalar(
                            P_sb[:, tc * CAP:(tc + 1) * CAP], iota_cap[:],
                            poscol[:, tc * 8 + ex: tc * 8 + ex + 1],
                            wf_sbp[:, 8 * tc + ex: 8 * tc + ex + 1],
                            ALU.is_equal, ALU.mult), "pb")
                    pb_after[ex] = p.val("pb")

                emit_pbuild(0)
                emit_ptbuild(0)
                for ex in range(8):
                    ee = ex + 1
                    p.wait("tensor", "pb", pb_after[ex])
                    if ex >= 1:
                        p.wait("vector", "mm1", p.val("mm1"))
                    for dc in range(16):
                        rXS.acquire(p, "tensor", poolk[0])
                        prx = poolk[0] % 2
                        for tc in range(8):
                            p.op("tensor", lambda e, dc=dc, tc=tc, prx=prx: e.matmul(
                                pXS[prx][:, 0:CAP], xn_tok[:, tc * D + dc * 128: tc * D + (dc + 1) * 128],
                                P_sb[:, tc * CAP:(tc + 1) * CAP],
                                start=(tc == 0), stop=(tc == 7)),
                                "mmXS" if tc == 7 else None, 1)
                        xv = p.val("mmXS")
                        p.wait("scalar", "mmXS", xv)
                        ev = p.op("scalar", lambda e, dc=dc, prx=prx: e.activation(
                            xs_sb[:, dc * CAP:(dc + 1) * CAP], pXS[prx][:, 0:CAP],
                            AF.Copy, bias=0.0, scale=1.0), "xse")
                        rXS.release(poolk[0], "xse", ev)
                        poolk[0] += 1
                    mmxs_after[ex] = p.val("mmXS")
                    xse_base = p.val("xse") - 16
                    ptbv = ptb_after[ex]
                    for hc in range(8):
                        idx = ee * 8 + hc
                        wb13 = idx % 2
                        p.wait("tensor", f"D_w13{idx % 2}", w13_v[idx])
                        rXS.acquire(p, "tensor", poolk[0])
                        rG3e.acquire(p, "tensor", gck[0])
                        prg = poolk[0] % 2
                        prg3 = gck[0] % 2
                        for dc in range(16):
                            if hc == 0:
                                p.wait("tensor", "xse", xse_base + dc + 1)
                            p.op("tensor", lambda e, wb13=wb13, dc=dc, prg=prg: e.matmul(
                                pG1e[prg][:, 0:CAP], w13[wb13][:, dc * 128:(dc + 1) * 128],
                                xs_sb[:, dc * CAP:(dc + 1) * CAP],
                                start=(dc == 0), stop=(dc == 15)),
                                "mm1" if dc == 15 else None, 1)
                        m1v = p.val("mm1")
                        for dc in range(16):
                            p.op("tensor", lambda e, wb13=wb13, dc=dc, prg3=prg3: e.matmul(
                                pG3e[prg3][:, 0:CAP], w13[wb13][:, 2048 + dc * 128:2048 + (dc + 1) * 128],
                                xs_sb[:, dc * CAP:(dc + 1) * CAP],
                                start=(dc == 0), stop=(dc == 15)),
                                "mm3" if dc == 15 else None, 1)
                        m3v = p.val("mm3")
                        rTmpE.acquire(p, "scalar", gck[0])
                        tb = gck[0] % 2
                        p.wait("scalar", "mm1", m1v)
                        if ex >= 1 and hc == 0:
                            p.wait("scalar", "mm2", mm2b_after[ex - 1])
                        s1v = p.op("scalar", lambda e, prg=prg, tb=tb: e.activation(
                            ts1[tb][:, 0:CAP], pG1e[prg][:, 0:CAP], AF.Silu, bias=0.0, scale=1.0), "si")
                        rXS.release(poolk[0], "si", s1v)
                        p.wait("vector", "mm3", m3v)
                        p.wait("vector", "si", s1v)
                        if ex >= 1 and hc == 0:
                            p.wait("vector", "mm2", mm2b_after[ex - 1])
                        hv = p.op("vector", lambda e, prg3=prg3, tb=tb, hc=hc: e.tensor_mul(
                            Hsp[:, hc * CAP:(hc + 1) * CAP],
                            ts1[tb][:, 0:CAP], pG3e[prg3][:, 0:CAP]), "hh")
                        rG3e.release(gck[0], "hh", hv)
                        rTmpE.release(gck[0], "hh", hv)
                        gck[0] += 1
                        poolk[0] += 1
                        mm3_after[idx] = p.val("mm3")
                        if idx + 2 < 72:
                            emit_w13_dma(*divmod(idx + 2, 8))

                    hhv = p.val("hh")
                    if ex + 1 < 8:
                        emit_pbuild(ex + 1)
                        emit_ptbuild(ex + 1)
                    p.wait("tensor", "D_w2", w2_v[ee])
                    p.wait("tensor", "hh", hhv)
                    if ex >= 1:
                        p.wait("scalar", "mmSC", mmsc_after[ex - 1])
                    for sc in range(3):
                        for dc in range(4):
                            rD.acquire(p, "tensor", dk[0])
                            prd = dk[0] % 2
                            for hc in range(8):
                                p.op("tensor", lambda e, hc=hc, sc=sc, dc=dc, prd=prd: e.matmul(
                                    pD[prd][:], Hsp[:, hc * CAP + sc * 128: hc * CAP + sc * 128 + 128],
                                    w2s[:, hc * D + dc * 512: hc * D + (dc + 1) * 512],
                                    start=(hc == 0), stop=(hc == 7)),
                                    "mm2" if hc == 7 else None, 1)
                            m2v = p.val("mm2")
                            p.wait("scalar", "mm2", m2v)
                            rv = p.op("scalar", lambda e, sc=sc, dc=dc, prd=prd: e.activation(
                                rout[:, sc * D + dc * 512: sc * D + (dc + 1) * 512], pD[prd][:],
                                AF.Copy, bias=0.0, scale=1.0), "rte")
                            rD.release(dk[0], "rte", rv)
                            dk[0] += 1
                    mm2b_after[ex] = p.val("mm2")
                    mm2_after[ee] = p.val("mm2")
                    if ee + 1 <= 8:
                        for c in range(4):
                            emit_w2_chunk(ee + 1, c)
                    rtev = p.val("rte")
                    p.wait("tensor", "rte", rtev)
                    p.wait("tensor", "ptb", ptbv)
                    for tt in range(8):
                        for dc in range(4):
                            rD.acquire(p, "tensor", dk[0])
                            prd = dk[0] % 2
                            for sc in range(3):
                                p.op("tensor", lambda e, sc=sc, tt=tt, dc=dc, prd=prd, ptbuf=ex % 2: e.matmul(
                                    pD[prd][:], PTs[ptbuf][:, sc * OWN + tt * 128: sc * OWN + (tt + 1) * 128],
                                    rout[:, sc * D + dc * 512: sc * D + (dc + 1) * 512],
                                    start=(sc == 0), stop=(sc == 2)),
                                    "mmSC" if sc == 2 else None, 1)
                            scv = p.val("mmSC")
                            p.wait("vector", "mmSC", scv)
                            av = p.op("vector", lambda e, tt=tt, dc=dc, prd=prd: e.tensor_add(
                                R[:, tt * D + dc * 512: tt * D + (dc + 1) * 512],
                                R[:, tt * D + dc * 512: tt * D + (dc + 1) * 512], pD[prd][:]), "rad")
                            rD.release(dk[0], "rad", av)
                            dk[0] += 1
                    mmsc_after[ex] = p.val("mmSC")

        if debug and stage >= 2:
            p.wait("sync", "rad", p.val("rad"))
            p.dma("sync", dbg_R[:], R[:], "D_sd")
        p.barrier("b2f")

        if stage < 2:
            p.op("vector", lambda e: e.memset(R[:], 0.0), "rad")
        with ExitStack() as phf:
            sbf = lambda *a, **k: phf.enter_context(nc.sbuf_tensor(*a, **k))
            xrl = [sbf(f"xrl{i}", [128, D], F32) for i in range(2)]
            osb = [sbf(f"osb{i}", [128, D], F32) for i in range(2)]
            rX = Rot(2)
            rO = Rot(2)
            p.wait("vector", "rad", p.val("rad"))
            for tt in range(8):
                rX.acquire(p, "sync", tt)
                xv = p.dma("sync", xrl[tt % 2][:], x_own[tt * 128:(tt + 1) * 128, :], f"D_xr{tt % 2}")
                p.wait("vector", f"D_xr{tt % 2}", xv)
                rO.acquire(p, "vector", tt)
                fv = p.op("vector", lambda e, tt=tt: e.tensor_add(
                    osb[tt % 2][:], R[:, tt * D:(tt + 1) * D], xrl[tt % 2][:]), "fin")
                rX.release(tt, "fin", fv)
                p.wait("sync", "fin", fv)
                sv = p.dma("sync", out_c[tt * 128:(tt + 1) * 128, :], osb[tt % 2][:], f"D_st{tt % 2}")
                rO.release(tt, f"D_st{tt % 2}", sv)
            p.wait("sync", "D_st0", p.val("D_st0"))
            p.wait("sync", "D_st1", p.val("D_st1"))
            if debug:
                p.wait("sync", "D_sd", p.val("D_sd"))

        with nc.Block() as block:
            p.finalize(block)
    return nc


_NC_CACHE = {}


def get_nc(debug=False):
    key = bool(debug)
    if key not in _NC_CACHE:
        _NC_CACHE[key] = build(debug)
    return _NC_CACHE[key]


def prep_inputs(L0_attn_x, L0_attn_w_qkv, L0_attn_b_qkv, L0_attn_w_out, L0_attn_b_out,
                L0_ffn_gate_w, L0_ffn_w1, L0_ffn_w2, L0_ffn_w3,
                L0_ffn_ws1, L0_ffn_ws2, L0_ffn_ws3):
    x = np.ascontiguousarray(np.asarray(L0_attn_x, dtype=np.float32))
    w_qkv = np.asarray(L0_attn_w_qkv, dtype=np.float32)
    b_qkv = np.ascontiguousarray(np.asarray(L0_attn_b_qkv, dtype=np.float32))
    w_out = np.asarray(L0_attn_w_out, dtype=np.float32)
    b_outv = np.ascontiguousarray(np.asarray(L0_attn_b_out, dtype=np.float32))
    gate_w = np.asarray(L0_ffn_gate_w, dtype=np.float32)
    w1 = np.asarray(L0_ffn_w1, dtype=np.float32)
    w2 = np.asarray(L0_ffn_w2, dtype=np.float32)
    w3 = np.asarray(L0_ffn_w3, dtype=np.float32)
    ws1 = np.asarray(L0_ffn_ws1, dtype=np.float32)
    ws2 = np.asarray(L0_ffn_ws2, dtype=np.float32)
    ws3 = np.asarray(L0_ffn_ws3, dtype=np.float32)

    shared = {
        "wqkv_t": np.ascontiguousarray(
            w_qkv.reshape(3, H, HD, 16, 128).transpose(1, 4, 0, 3, 2)),
        "bq_col": np.ascontiguousarray(b_qkv.reshape(3, H, HD).transpose(2, 0, 1).reshape(128, 48)),
        "b_qkv": b_qkv,
        "w_outT": np.ascontiguousarray(w_out.T.reshape(16, 128, 4, 512).transpose(2, 1, 0, 3)),
        "b_out": b_outv,
        "gate_wT": np.ascontiguousarray(gate_w.T),
        "w1h": np.ascontiguousarray(
            w1.transpose(0, 2, 1).reshape(NE, 16, 128, 8, 128).transpose(0, 3, 2, 1, 4)),
        "w3h": np.ascontiguousarray(
            w3.transpose(0, 2, 1).reshape(NE, 16, 128, 8, 128).transpose(0, 3, 2, 1, 4)),
        "w2h": np.ascontiguousarray(
            w2.transpose(0, 2, 1).reshape(NE, 8, 128, D).transpose(0, 2, 1, 3)),
        "ws1h": np.ascontiguousarray(
            ws1.T.reshape(16, 128, 8, 128).transpose(2, 1, 0, 3)),
        "ws3h": np.ascontiguousarray(
            ws3.T.reshape(16, 128, 8, 128).transpose(2, 1, 0, 3)),
        "ws2h": np.ascontiguousarray(ws2.T.reshape(8, 128, D).transpose(1, 0, 2)),
    }
    in_maps = []
    for c in range(8):
        b, j = divmod(c, 2)
        m = dict(shared)
        m["x_own"] = np.ascontiguousarray(x[b, j * OWN:(j + 1) * OWN, :])
        m["x_oth"] = np.ascontiguousarray(x[b, (1 - j) * OWN:(2 - j) * OWN, :])
        bias = np.full((128, 18), EXPSH, dtype=np.float32)
        if j == 0:
            bias[:, 0:16] = -1e30
        bias[:, 16] = EXPSH
        bias[:, 17] = EXPSH - BIGM * SCL
        m["expb"] = bias
        in_maps.append(m)
    return in_maps


def kernel(L0_attn_x, L0_attn_w_qkv, L0_attn_b_qkv, L0_attn_w_out, L0_attn_b_out,
           L0_ffn_gate_w, L0_ffn_w1, L0_ffn_w2, L0_ffn_w3,
           L0_ffn_ws1, L0_ffn_ws2, L0_ffn_ws3):
    in_maps = prep_inputs(L0_attn_x, L0_attn_w_qkv, L0_attn_b_qkv, L0_attn_w_out,
                          L0_attn_b_out, L0_ffn_gate_w, L0_ffn_w1, L0_ffn_w2,
                          L0_ffn_w3, L0_ffn_ws1, L0_ffn_ws2, L0_ffn_ws3)
    nc = get_nc(debug=False)
    res = run_bass_kernel_spmd(nc, in_maps, core_ids=list(range(8)))
    out = np.empty((B, S, D), dtype=np.float32)
    for c in range(8):
        b, j = divmod(c, 2)
        out[b, j * OWN:(j + 1) * OWN, :] = res.results[c]["out"]
    return out


# version 17
# speedup vs baseline: 1.0213x; 1.0213x over previous
import math
import numpy as np
import ml_dtypes
from contextlib import ExitStack

import concourse.bass as bass
import concourse.mybir as mybir
from concourse.bass_utils import run_bass_kernel_spmd

F32 = mybir.dt.float32
BF16 = mybir.dt.bfloat16
AF = mybir.ActivationFunctionType
ALU = mybir.AluOpType
AX = mybir.AxisListType

D = 2048
H = 16
HD = 128
S = 2048
B = 4
OWN = 1024
NHID = 1024
NE = 8
SCL = 1.0 / math.sqrt(HD)
BIGM = 600.0
EXPSH = -4.0
EPS = 1e-5

ENGINES = ("tensor", "vector", "scalar", "gpsimd", "sync")


class Prog:

    def __init__(self, nc, ctx):
        self.nc = nc
        self.ctx = ctx
        self.ops = {e: [] for e in ENGINES}
        self.semh = {}
        self.semc = {}

    def sem(self, name):
        if name not in self.semh:
            self.semh[name] = self.ctx.enter_context(self.nc.semaphore(name))
            self.semc[name] = 0
        return self.semh[name]

    def val(self, name):
        self.sem(name)
        return self.semc[name]

    def op(self, eng, f, sem=None, n=1):
        if sem is None:
            self.ops[eng].append(f)
            return None
        h = self.sem(sem)
        self.semc[sem] += n
        self.ops[eng].append(lambda e, f=f, h=h, n=n: f(e).then_inc(h, n))
        return self.semc[sem]

    def wait(self, eng, sem, v=None):
        h = self.sem(sem)
        if v is None:
            v = self.semc[sem]
        if v > 0:
            self.ops[eng].append(lambda e, h=h, v=v: e.wait_ge(h, v))

    def dma(self, eng, out, in_, sem):
        return self.op(eng, lambda e, o=out, i=in_: e.dma_start(out=o, in_=i), sem, 16)

    def barrier(self, tag):
        vals = [(n, self.semc[n]) for n in list(self.semc) if self.semc[n] > 0]
        for eng in ENGINES:
            for n, v in vals:
                self.wait(eng, n, v)

    def finalize(self, block):
        def run(engine, ops):
            for f in ops:
                f(engine)
        block.tensor(lambda e: run(e, self.ops["tensor"]))
        block.vector(lambda e: run(e, self.ops["vector"]))
        block.scalar(lambda e: run(e, self.ops["scalar"]))
        block.gpsimd(lambda e: run(e, self.ops["gpsimd"]))
        block.sync(lambda e: run(e, self.ops["sync"]))


class Rot:

    def __init__(self, n):
        self.n = n
        self.free = [None] * n

    def acquire(self, p, eng, i):
        w = self.free[i % self.n]
        if w is not None:
            p.wait(eng, w[0], w[1])

    def release(self, i, sem, val):
        self.free[i % self.n] = (sem, val)


def build(debug=False, stage=3):
    nc = bass.Bass("TRN2", target_bir_lowering=False, debug=False)

    x_own = nc.dram_tensor("x_own", [OWN, D], F32, kind="ExternalInput").ap()
    x_oth = nc.dram_tensor("x_oth", [OWN, D], F32, kind="ExternalInput").ap()
    expb = nc.dram_tensor("expb", [128, 18], F32, kind="ExternalInput").ap()
    wqkv_t = nc.dram_tensor("wqkv_t", [H, 128, 3, 16, HD], F32, kind="ExternalInput").ap()
    bq_col = nc.dram_tensor("bq_col", [128, 48], F32, kind="ExternalInput").ap()
    b_qkv = nc.dram_tensor("b_qkv", [3 * D], F32, kind="ExternalInput").ap()
    w_outT = nc.dram_tensor("w_outT", [4, 128, 16, 512], F32, kind="ExternalInput").ap()
    b_out = nc.dram_tensor("b_out", [D], F32, kind="ExternalInput").ap()
    gate_wT = nc.dram_tensor("gate_wT", [D, NE], F32, kind="ExternalInput").ap()
    w1h = nc.dram_tensor("w1h", [NE, 8, 128, 16, 128], F32, kind="ExternalInput").ap()
    w3h = nc.dram_tensor("w3h", [NE, 8, 128, 16, 128], F32, kind="ExternalInput").ap()
    w2h = nc.dram_tensor("w2h", [NE, 128, 8, D], F32, kind="ExternalInput").ap()
    ws1h = nc.dram_tensor("ws1h", [8, 128, 16, 128], F32, kind="ExternalInput").ap()
    ws3h = nc.dram_tensor("ws3h", [8, 128, 16, 128], F32, kind="ExternalInput").ap()
    ws2h = nc.dram_tensor("ws2h", [128, 8, D], F32, kind="ExternalInput").ap()
    out_c = nc.dram_tensor("out", [OWN, D], F32, kind="ExternalOutput").ap()
    if debug:
        dbg_xn = nc.dram_tensor("dbg_xn", [128, 2 * 16 * OWN], BF16, kind="ExternalOutput").ap()
        dbg_attn = nc.dram_tensor("dbg_attn", [128, 16 * OWN], BF16, kind="ExternalOutput").ap()
        dbg_w = nc.dram_tensor("dbg_w", [128, 64], BF16, kind="ExternalOutput").ap()
        dbg_R = nc.dram_tensor("dbg_R", [128, 8 * D], F32, kind="ExternalOutput").ap()

    bf = ml_dtypes.bfloat16
    ident_d = nc.inline_tensor(np.eye(128, dtype=np.float32).astype(bf), name="identc").ap()
    ibig_d = nc.inline_tensor((BIGM * np.eye(128, dtype=np.float32)).astype(bf), name="ibigc").ap()
    tri = np.zeros((4, 128, 512), dtype=np.float32)
    for j2 in range(4):
        for k in range(128):
            tri[j2, k, j2 * 128 + k:] = 1.0
    tri_d = nc.inline_tensor(tri.astype(bf), name="tric").ap()
    selm = np.zeros((8, 8 * 128), dtype=np.float32)
    for e in range(8):
        selm[e, e * 128:(e + 1) * 128] = 1.0
    sel_d = nc.inline_tensor(selm.astype(bf), name="selc").ap()
    ones_d = nc.inline_tensor(np.ones((1, 128), dtype=np.float32).astype(bf), name="ones1c").ap()

    ctx = ExitStack()
    with ctx:
        p = Prog(nc, ctx)
        sb = lambda *a, **k: ctx.enter_context(nc.sbuf_tensor(*a, **k))
        xn_tok = sb("xn_tok", [128, 8 * D], BF16)
        w_sbp = sb("w_sbp", [128, 8 * 8], BF16)
        wf_sbp = sb("wf_sbp", [128, 8 * 8], F32)
        ident = sb("ident", [128, 128], BF16)
        ibig = sb("ibig", [128, 128], BF16)
        ones1 = sb("ones1", [1, 128], BF16)
        bqc_sb = sb("bqc_sb", [128, 48], F32)
        gw_sb = sb("gw_sb", [128, 16 * 8], BF16)
        expb_sb = sb("expb_sb", [128, 18], F32)
        s_xnF = ExitStack()
        ctx.enter_context(s_xnF)
        xn_own = s_xnF.enter_context(nc.sbuf_tensor("xn_own", [128, 16 * OWN], BF16, side="right"))
        s_attn = ExitStack()
        ctx.enter_context(s_attn)
        attn = s_attn.enter_context(nc.sbuf_tensor("attn", [128, 16 * OWN], BF16, side="right"))
        bor_sb = s_attn.enter_context(nc.sbuf_tensor("bor_sb", [1, D], BF16, side="right"))
        s_xnoth = ExitStack()
        ctx.enter_context(s_xnoth)
        xn_oth = s_xnoth.enter_context(nc.sbuf_tensor("xn_oth", [128, 16 * OWN], BF16, side="right"))
        tri_sb = s_xnoth.enter_context(nc.sbuf_tensor("tri_sb", [128, 4 * 512], BF16))

        p.dma("sync", ident[:], ident_d[:, :], "D_c")
        p.dma("sync", ibig[:], ibig_d[:, :], "D_c")
        p.dma("sync", tri_sb[:].rearrange("p (m q) -> p m q", m=4),
              tri_d[:, :, :].rearrange("m p q -> p m q"), "D_c")
        p.dma("sync", ones1[:], ones_d[:, :], "D_c")
        p.dma("sync", bqc_sb[:], bq_col[:, :], "D_c")
        p.dma("sync", expb_sb[:], expb[:, :], "D_c")
        p.dma("gpsimd", bor_sb[:], b_out[:].rearrange("(a n) -> a n", a=1), "D_cg")
        p.dma("gpsimd", gw_sb[:].rearrange("p (c e) -> p c e", c=16),
              gate_wT.rearrange("(c p) e -> p c e", p=128), "D_cg")
        cst_v = p.val("D_c")
        cstg_v = p.val("D_cg")

        with ExitStack() as ph0:
            sb0 = lambda *a, **k: ph0.enter_context(nc.sbuf_tensor(*a, **k))
            ps0 = [ph0.enter_context(nc.psum_tensor(f"p0_{i}", [128, 1024], BF16)) for i in range(6)]
            xt = [sb0(f"xt{i}", [128, D], F32) for i in range(3)]
            sqj = sb0("sqj", [128, D], BF16)
            sqj2 = sb0("sqj2", [128, D], BF16)
            st1 = [sb0(f"st1{i}", [128, 8], F32) for i in range(3)]
            xntk = [sb0(f"xntk{i}", [128, D], BF16) for i in range(3)]
            xc = [sb0(f"xc{i}", [128, D], F32) for i in range(3)]
            epscol = sb0("epscol", [128, 1], F32)
            rcol = [sb0(f"rcol{i}", [128, 2], F32) for i in range(3)]
            p0rot = Rot(6)
            xtk_cnt = [0]
            p.op("vector", lambda e: e.memset(epscol[:], EPS), "v2")

            for tt in range(16):
                own = tt < 8
                src = (x_own if own else x_oth)
                dst = (xn_own if own else xn_oth)
                ttl = tt % 8
                b3 = tt % 3
                ss = tt % 3
                xcs = tt % 3
                if tt >= 3:
                    p.wait("sync", "lnA", tt - 3 + 1)
                xsem = f"D_x{tt % 3}"
                xv = p.dma("sync", xt[b3][:], src[ttl * 128:(ttl + 1) * 128, :], xsem)
                p.wait("vector", xsem, xv)
                if tt >= 3:
                    p.wait("vector", "lnA", tt - 3 + 1)
                p.wait("scalar", xsem, xv)
                if tt >= 3:
                    p.wait("scalar", "lnA", tt - 3 + 1)
                v1 = p.op("scalar", lambda e, b=b3, s=ss: e.activation(
                    sqj2[:], xt[b][:], AF.Copy, bias=0.0, scale=1.0,
                    accum_out=st1[s][:, 0:1]), "v1")
                s2v = p.op("scalar", lambda e, b=b3, s=ss: e.activation(
                    sqj[:], xt[b][:], AF.Square, bias=0.0, scale=1.0,
                    accum_out=st1[s][:, 1:2]), "sq")
                p.wait("vector", "v1", v1)
                mv = p.op("vector", lambda e, s=ss: e.tensor_scalar(
                    st1[s][:, 2:3], st1[s][:, 0:1], -1.0 / D, None, ALU.mult, ALU.bypass), "v2")
                p.wait("vector", "v2", mv)
                m2v = p.op("vector", lambda e, s=ss: e.tensor_scalar(
                    st1[s][:, 4:5], st1[s][:, 2:3], st1[s][:, 2:3], None, ALU.mult, ALU.bypass), "v2")
                p.wait("vector", "sq", s2v)
                p.wait("vector", "v2", m2v)
                vpev = p.op("vector", lambda e, s=ss: e.scalar_tensor_tensor(
                    st1[s][:, 5:6], st1[s][:, 1:2], 1.0 / D, st1[s][:, 4:5],
                    ALU.mult, ALU.subtract), "v2")
                p.wait("scalar", "v2", vpev)
                sdv = p.op("scalar", lambda e, s=ss: e.activation(
                    st1[s][:, 6:7], st1[s][:, 5:6], AF.Sqrt, bias=epscol[:, 0:1], scale=1.0), "s2")
                if tt >= 2:
                    p.wait("vector", "lnA", tt - 2 + 1)
                xcv = p.op("vector", lambda e, b=b3, s=ss, xs=xcs: e.tensor_scalar(
                    xc[xs][:], xt[b][:], st1[s][:, 2:3], None, ALU.add, ALU.bypass), "xcv")
                p.wait("vector", "s2", sdv)
                rsv = p.op("vector", lambda e, s=ss: e.reciprocal(rcol[s][:, 0:1], st1[s][:, 6:7]), "v3")
                p.wait("scalar", "v3", rsv)
                p.wait("scalar", "xcv", xcv)
                if tt >= 3:
                    p.wait("scalar", "xt", 16 * (tt - 2))
                if tt >= 3:
                    p.wait("scalar", "xtk", max(0, xtk_cnt[0] - 2))
                lnv = p.op("scalar", lambda e, s=ss, xs=xcs: e.activation(
                    xntk[s][:], xc[xs][:], AF.Copy, bias=0.0, scale=rcol[s][:, 0:1]), "lnA")
                if own:
                    p.wait("vector", "lnA", lnv)
                    p.op("vector", lambda e, s=ss, ttl=ttl: e.tensor_copy(
                        xn_tok[:, ttl * D:(ttl + 1) * D], xntk[s][:]), "xtk")
                    xtk_cnt[0] += 1
                p.wait("tensor", "lnA", lnv)
                if tt == 0:
                    p.wait("tensor", "D_c", cst_v)
                for dc in range(16):
                    p0rot.acquire(p, "tensor", tt * 16 + dc)
                    tv = p.op("tensor", lambda e, s=ss, dc=dc, r=(tt * 16 + dc) % 6: e.transpose(
                        ps0[r][:, 0:128], xntk[s][:, dc * 128:(dc + 1) * 128], ident[:]), "xt")
                    eng = "vector" if dc % 2 == 0 else "scalar"
                    p.wait(eng, "xt", tv)
                    if eng == "vector":
                        ev2 = p.op(eng, lambda e, d=dst, dc=dc, ttl=ttl, r=(tt * 16 + dc) % 6: e.tensor_copy(
                            d[:, dc * OWN + ttl * 128: dc * OWN + (ttl + 1) * 128], ps0[r][:, 0:128]), "xe")
                        p0rot.release(tt * 16 + dc, "xe", ev2)
                    else:
                        ev2 = p.op(eng, lambda e, d=dst, dc=dc, ttl=ttl, r=(tt * 16 + dc) % 6: e.activation(
                            d[:, dc * OWN + ttl * 128: dc * OWN + (ttl + 1) * 128], ps0[r][:, 0:128],
                            AF.Copy, bias=0.0, scale=1.0), "xe2")
                        p0rot.release(tt * 16 + dc, "xe2", ev2)

        p.barrier("b01")
        if debug:
            p.dma("sync", dbg_xn[:, 0:16 * OWN], xn_own[:], "D_sd")
            p.dma("sync", dbg_xn[:, 16 * OWN:], xn_oth[:], "D_sd")

        if stage >= 1:
         with ExitStack() as ph1:
            sb1 = lambda *a, **k: ph1.enter_context(nc.sbuf_tensor(*a, **k))
            pP = [ph1.enter_context(nc.psum_tensor(f"pP{i}", [128, 1024], F32)) for i in range(2)]
            pA = [ph1.enter_context(nc.psum_tensor(f"pA{i}", [128, 512], F32)) for i in range(2)]
            pT = [ph1.enter_context(nc.psum_tensor(f"pT{i}", [128, 1024], BF16)) for i in range(2)]
            qkvw = [sb1("qkvw0", [128, 3 * 16 * 128], BF16)]
            K_sb = [sb1("K0", [128, S], BF16)]
            Q_sb = [sb1(f"Q{i}", [128, OWN], BF16) for i in range(2)]
            V_sb = [sb1("V0", [128, S], BF16)]
            VT_sb = [sb1(f"VT{i}", [128, 16 * 129], BF16) for i in range(2)]
            PT_sb = [sb1(f"PT{i}", [128, 16 * 512], BF16) for i in range(2)]
            AT_sb = [sb1(f"AT{i}", [128, 128], BF16) for i in range(2)]
            rc_sb = [sb1(f"rc{i}", [128, 1], F32) for i in range(2)]
            rS = Rot(2)
            rQp = Rot(4)
            rAp = Rot(2)
            rTp = Rot(2)
            rAT = Rot(2)

            for i in range(2):
                p.op("vector", lambda e, i=i: e.memset(
                    VT_sb[i][:].rearrange("p (kc c) -> p kc c", c=129)[:, :, 128:129], 1.0), "vte")

            def xn_rhs(dc, c0, n):
                if c0 < OWN:
                    assert c0 + n <= OWN
                    return xn_own[:, dc * OWN + c0: dc * OWN + c0 + n]
                assert c0 + n <= 2 * OWN
                return xn_oth[:, dc * OWN + c0 - OWN: dc * OWN + c0 - OWN + n]

            sched = {0: [(kc, 'diag', kc) for kc in range(4)] + [(kc, 'data', None) for kc in range(8, 16)],
                     1: [(kc, 'plain', None) for kc in range(4)] + [(kc, 'diag', kc - 4) for kc in range(4, 8)]
                        + [(kc, 'data', None) for kc in range(8, 16)]}

            qknt = 0
            unit = -1
            avdone = {}
            ptv_maps = {}
            pending_post = []

            def emit_qkv(h, b):
                nonlocal qknt
                wv = wq_v[h]
                p.wait("tensor", "D_wq0", wv)
                p.wait("tensor", "pt", p.val("pt"))
                for which, nchunks, dstl in ((1, 4, 'K'), (2, 4, 'V'), (0, 2, 'Q')):
                    for c4 in range(nchunks):
                        rQp.acquire(p, "tensor", qknt)
                        t2, hf = (qknt // 2) % 2, qknt % 2
                        for dc in range(16):
                            p.op("tensor", lambda e, b=b, w=which, dc=dc, c4=c4, t2=t2, hf=hf: e.matmul(
                                pP[t2][:, hf * 512:(hf + 1) * 512],
                                qkvw[0][:, (w * 16 + dc) * 128:(w * 16 + dc + 1) * 128],
                                xn_rhs(dc, c4 * 512, 512),
                                start=(dc == 0), stop=(dc == 15)),
                                "mmQ" if dc == 15 else None, 1)
                        mv = p.val("mmQ")
                        p.wait("vector", "mmQ", mv)
                        bcol = bqc_sb[:, which * 16 + h: which * 16 + h + 1]
                        if dstl == 'K':
                            dst = K_sb[0][:, c4 * 512:(c4 + 1) * 512]
                        elif dstl == 'V':
                            dst = V_sb[0][:, c4 * 512:(c4 + 1) * 512]
                        else:
                            dst = Q_sb[b][:, c4 * 512:(c4 + 1) * 512]
                        ev = p.op("vector", lambda e, dst=dst, t2=t2, hf=hf, bcol=bcol: e.tensor_scalar(
                            dst, pP[t2][:, hf * 512:(hf + 1) * 512], bcol, None, ALU.add, ALU.bypass), "kqv")
                        rQp.release(qknt, "kqv", ev)
                        qknt += 1
                relw[h] = p.val("mmQ")
                kv = p.val("kqv")
                p.wait("tensor", "kqv", kv)
                if h >= 2:
                    p.wait("vector", "av", av_after.get(h - 2, 0))
                for kc in range(16):
                    rTp.acquire(p, "tensor", trn[0])
                    tv = p.op("tensor", lambda e, b=b, kc=kc, r=trn[0] % 2: e.transpose(
                        pT[r][:, 0:128], V_sb[0][:, kc * 128:(kc + 1) * 128], ident[:]), "trp")
                    p.wait("vector", "trp", tv)
                    ev = p.op("vector", lambda e, b=b, kc=kc, r=trn[0] % 2: e.tensor_copy(
                        VT_sb[b][:, kc * 129: kc * 129 + 128], pT[r][:, 0:128]), "vte")
                    rTp.release(trn[0], "vte", ev)
                    trn[0] += 1
                relvs[h] = p.val("trp")
                vte_after[h] = p.val("vte")

            def emit_scores(h, qb, b, u):
                ptb = u % 2
                if u >= 2:
                    p.wait("scalar", "av", avdone[u - 2])
                ptv_map = {}
                p.wait("tensor", "kqv", p.val("kqv"))
                chunks = sched[qb]
                for pi in range(0, len(chunks), 2):
                    pair = chunks[pi:pi + 2]
                    rS.acquire(p, "tensor", sck[0])
                    pr = sck[0] % 2
                    smv = None
                    for hf, (kc, kind, j2) in enumerate(pair):
                        fin = (hf == 1 and kind != 'diag')
                        smv = p.op("tensor", lambda e, b=b, kc=kc, qb=qb, pr=pr, hf=hf,
                             last=(kind != 'diag'): e.matmul(
                            pP[pr][:, hf * 512:(hf + 1) * 512], K_sb[0][:, kc * 128:(kc + 1) * 128],
                            Q_sb[b][:, qb * 512:(qb + 1) * 512],
                            start=True, stop=last), "mmS" if fin else None)
                        if kind == 'diag':
                            smv = p.op("tensor", lambda e, j2=j2, pr=pr, hf=hf: e.matmul(
                                pP[pr][:, hf * 512:(hf + 1) * 512], ibig[:],
                                tri_sb[:, j2 * 512:(j2 + 1) * 512],
                                start=False, stop=True), "mmS" if hf == 1 else None)
                    p.wait("scalar", "mmS", smv)
                    kc0, kind0, _ = pair[0]
                    if kind0 == 'diag':
                        bias = expb_sb[:, 17:18]
                    elif kind0 == 'plain':
                        bias = expb_sb[:, 16:17]
                    else:
                        bias = expb_sb[:, qb * 8 + (kc0 - 8): qb * 8 + (kc0 - 8) + 1]
                    ev = p.op("scalar", lambda e, ptb=ptb, kc0=kc0, pr=pr, bias=bias: e.activation(
                        PT_sb[ptb][:, kc0 * 512:(kc0 + 2) * 512], pP[pr][:], AF.Exp,
                        bias=bias, scale=SCL), "pt")
                    rS.release(sck[0], "pt", ev)
                    for (kc, _, _) in pair:
                        ptv_map[kc] = ev
                    sck[0] += 1
                ptv_maps[u] = ptv_map

            def post_av(h, qb, ql, av_idx, avv):
                pr = av_idx % 2
                p.wait("vector", "av", avv)
                rcv = p.op("vector", lambda e, pr=pr, r=av_idx % 2: e.reciprocal(
                    rc_sb[r][:], pA[pr][:, 128:129]), "rc")
                p.wait("scalar", "rc", rcv)
                rAT.acquire(p, "scalar", av_idx)
                atv = p.op("scalar", lambda e, pr=pr, r=av_idx % 2: e.activation(
                    AT_sb[r][:], pA[pr][:, 0:128], AF.Copy, bias=0.0, scale=rc_sb[r][:, 0:1]), "at")
                rAp.release(av_idx, "at", atv)
                p.wait("tensor", "at", atv)
                rTp.acquire(p, "tensor", trn[0])
                tv = p.op("tensor", lambda e, r=av_idx % 2, tr=trn[0] % 2: e.transpose(
                    pT[tr][:, 0:128], AT_sb[r][:], ident[:]), "trp")
                rAT.release(av_idx, "trp", tv)
                p.wait("scalar", "trp", tv)
                col = h * OWN + qb * 512 + ql * 128
                ev = p.op("scalar", lambda e, col=col, tr=trn[0] % 2: e.activation(
                    attn[:, col:col + 128], pT[tr][:, 0:128],
                    AF.Copy, bias=0.0, scale=1.0), "ao")
                rTp.release(trn[0], "ao", ev)
                trn[0] += 1

            def emit_av(h, qb, b, u):
                ptb = u % 2
                p.wait("tensor", "vte", vte_after[h])
                pend = pending_post
                for ql in range(4):
                    rAp.acquire(p, "tensor", avk[0])
                    pr = avk[0] % 2
                    chunks = [c[0] for c in sched[qb]]
                    avv = None
                    for i, kc in enumerate(chunks):
                        if ql == 0:
                            p.wait("tensor", "pt", ptv_maps[u][kc])
                        avv = p.op("tensor", lambda e, ptb=ptb, kc=kc, ql=ql, b=b, pr=pr,
                             st=(i == 0), sp=(i == len(chunks) - 1): e.matmul(
                            pA[pr][:, 0:129], PT_sb[ptb][:, kc * 512 + ql * 128: kc * 512 + (ql + 1) * 128],
                            VT_sb[b][:, kc * 129: kc * 129 + 129],
                            start=st, stop=sp), "av" if i == len(chunks) - 1 else None)
                    if pend:
                        post_av(*pend.pop(0))
                    pend.append((h, qb, ql, avk[0], avv))
                    avk[0] += 1
                avdone[u] = p.val("av")

            wq_v = {}

            def emit_wq_dma(h):
                if h >= 1:
                    p.wait("gpsimd", "mmQ", relw[h - 1])
                wq_v[h] = p.dma("gpsimd",
                                qkvw[0][:].rearrange("p (w dc hd) -> p w dc hd", w=3, dc=16),
                                wqkv_t[h], "D_wq0")

            sck = [0]
            avk = [0]
            trn = [0]
            relw = {}
            relvs = {}
            vte_after = {}
            ms_after = {}
            av_after = {}
            emit_wq_dma(0)
            for h in range(H):
                b = h % 2
                if h >= 1:
                    p.wait("vector", "mmS", ms_after.get(h - 1, 0))
                    p.wait("vector", "trp", relvs.get(h - 1, 0))
                if h >= 2:
                    p.wait("vector", "mmS", ms_after.get(h - 2, 0))
                emit_qkv(h, b)
                if h + 1 < H:
                    emit_wq_dma(h + 1)
                for qb in range(2):
                    unit += 1
                    emit_scores(h, qb, b, unit)
                    emit_av(h, qb, b, unit)
                ms_after[h] = p.val("mmS")
                av_after[h] = p.val("av")
            while pending_post:
                post_av(*pending_post.pop(0))

        s_xnoth.close()
        p.barrier("b12")
        if debug and stage >= 1:
            p.dma("sync", dbg_attn[:], attn[:], "D_sd")

        R = sb("R", [128, 8 * D], F32)
        wT_sb = sb("wT_sb", [8, OWN], BF16)
        if stage >= 2:
         with ExitStack() as ph2a:
            sb2 = lambda *a, **k: ph2a.enter_context(nc.sbuf_tensor(*a, **k))
            pR = [ph2a.enter_context(nc.psum_tensor(f"pR{i}", [128, 512], F32)) for i in range(2)]
            pGm = ph2a.enter_context(nc.psum_tensor("pGm", [128, 512], F32))
            pW8 = ph2a.enter_context(nc.psum_tensor("pW8", [8, 512], F32))
            wo_sb = [sb2(f"wo{i}", [128, 16 * 512], BF16) for i in range(2)]
            sg_sb = sb2("sg", [128, 8 * 8], F32)
            tg_sb = sb2("tg", [128, 8 * 8 * 3], F32)
            rg_sb = sb2("rg", [128, 8 * 2], F32)
            w_sb = w_sbp
            rR = Rot(2)
            rwo = Rot(2)

            for tt in range(8):
                for dc in range(16):
                    p.op("tensor", lambda e, tt=tt, dc=dc: e.matmul(
                        pGm[:, tt * 8:(tt + 1) * 8],
                        xn_own[:, dc * OWN + tt * 128: dc * OWN + (tt + 1) * 128],
                        gw_sb[:, dc * 8:(dc + 1) * 8],
                        start=(dc == 0), stop=(dc == 15)),
                        "mmG" if (tt == 7 and dc == 15) else None, 1)
            gv = p.val("mmG")
            p.wait("scalar", "mmG", gv)
            p.op("scalar", lambda e: e.activation(
                sg_sb[:], pGm[:, 0:64], AF.Sigmoid, bias=0.0, scale=1.0), "sgm")
            sgv = p.val("sgm")
            p.wait("vector", "sgm", sgv)
            for tt in range(8):
                p.op("vector", lambda e, tt=tt: e.reduce_max(
                    rg_sb[:, tt * 2: tt * 2 + 1], sg_sb[:, tt * 8:(tt + 1) * 8], axis=AX.X), "g1")
            p.wait("vector", "g1", p.val("g1"))
            for tt in range(8):
                p.op("vector", lambda e, tt=tt: e.tensor_scalar(
                    tg_sb[:, tt * 8:(tt + 1) * 8], sg_sb[:, tt * 8:(tt + 1) * 8],
                    rg_sb[:, tt * 2: tt * 2 + 1], -1e30, ALU.is_ge, ALU.mult), "g1")
            p.wait("vector", "g1", p.val("g1"))
            for tt in range(8):
                p.op("vector", lambda e, tt=tt: e.tensor_add(
                    tg_sb[:, 64 + tt * 8: 64 + (tt + 1) * 8], tg_sb[:, tt * 8:(tt + 1) * 8],
                    sg_sb[:, tt * 8:(tt + 1) * 8]), "g1")
            p.wait("vector", "g1", p.val("g1"))
            for tt in range(8):
                p.op("vector", lambda e, tt=tt: e.reduce_max(
                    rg_sb[:, tt * 2 + 1: tt * 2 + 2], tg_sb[:, 64 + tt * 8: 64 + (tt + 1) * 8], axis=AX.X), "g1")
            p.wait("vector", "g1", p.val("g1"))
            for tt in range(8):
                p.op("vector", lambda e, tt=tt: e.tensor_scalar(
                    tg_sb[:, 128 + tt * 8: 128 + (tt + 1) * 8], sg_sb[:, tt * 8:(tt + 1) * 8],
                    rg_sb[:, tt * 2 + 1: tt * 2 + 2], None, ALU.is_ge, ALU.bypass), "g1")
            p.wait("vector", "g1", p.val("g1"))
            for tt in range(8):
                p.op("vector", lambda e, tt=tt: e.tensor_mul(
                    w_sb[:, tt * 8:(tt + 1) * 8], tg_sb[:, 128 + tt * 8: 128 + (tt + 1) * 8],
                    sg_sb[:, tt * 8:(tt + 1) * 8]), "g2")
            p.wait("vector", "g2", p.val("g2"))
            p.op("vector", lambda e: e.tensor_copy(wf_sbp[:], w_sbp[:]), "g2")
            g2v = p.val("g2")
            if debug:
                p.wait("sync", "g2", g2v)
                p.dma("sync", dbg_w[:], w_sb[:], "D_sd")
            p.wait("tensor", "g2", g2v)
            for half in range(2):
                for i in range(4):
                    tt = half * 4 + i
                    p.op("tensor", lambda e, tt=tt, i=i: e.matmul(
                        pW8[0:8, i * 128:(i + 1) * 128], w_sb[:, tt * 8:(tt + 1) * 8],
                        ident[:], start=True, stop=True),
                        "mmW" if i == 3 else None, 1)
                wv = p.val("mmW")
                p.wait("scalar", "mmW", wv)
                sv = p.op("scalar", lambda e, half=half: e.activation(
                    wT_sb[0:8, half * 512:(half + 1) * 512], pW8[0:8, 0:512],
                    AF.Copy, bias=0.0, scale=1.0), "wTe")
                p.wait("tensor", "wTe", sv)

            ork = [0]
            wo_v = {}

            def emit_wo_dma(dc):
                if dc >= 2:
                    p.wait("gpsimd", "mmO", 8 * (dc - 1))
                wo_v[dc] = p.dma("gpsimd",
                                 wo_sb[dc % 2][:].rearrange("p (dcin n) -> p dcin n", dcin=16),
                                 w_outT[dc], f"D_wo{dc % 2}")

            emit_wo_dma(0)
            emit_wo_dma(1)
            for dc in range(4):
                if dc + 2 < 4:
                    emit_wo_dma(dc + 2)
                p.wait("tensor", f"D_wo{dc % 2}", wo_v[dc])
                for tt in range(8):
                    rR.acquire(p, "tensor", ork[0])
                    pr = ork[0] % 2
                    p.op("tensor", lambda e, dc=dc, pr=pr: e.matmul(
                        pR[pr][:], ones1[0:1, :], bor_sb[0:1, dc * 512:(dc + 1) * 512],
                        start=True, stop=False), None)
                    for dcin in range(16):
                        p.op("tensor", lambda e, dcin=dcin, tt=tt, dc=dc, pr=pr: e.matmul(
                            pR[pr][:], attn[:, dcin * OWN + tt * 128: dcin * OWN + (tt + 1) * 128],
                            wo_sb[dc % 2][:, dcin * 512:(dcin + 1) * 512],
                            start=False, stop=(dcin == 15)),
                            "mmO" if dcin == 15 else None, 1)
                    ov = p.val("mmO")
                    p.wait("scalar", "mmO", ov)
                    ev = p.op("scalar", lambda e, tt=tt, dc=dc, pr=pr: e.activation(
                        R[:, tt * D + dc * 512: tt * D + (dc + 1) * 512], pR[pr][:],
                        AF.Copy, bias=0.0, scale=1.0), "rop")
                    rR.release(ork[0], "rop", ev)
                    ork[0] += 1

        s_attn.close()
        p.barrier("b2ab")

        CAP = 384
        if stage >= 3:
         with ExitStack() as ph2o:
            sbo = lambda *a, **k: ph2o.enter_context(nc.sbuf_tensor(*a, **k))
            w13 = [sbo(f"w13_{i}", [128, 2 * 16 * 128], BF16) for i in range(2)]
            w2s = sbo("w2s", [128, 8 * D], BF16)
            ts1 = [sbo(f"ts1{i}", [128, 512], F32) for i in range(2)]
            mm3_after = {}
            mm2_after = {}
            w13_v = {}
            w2_v = {}

            def w13_src(ee, hc, which):
                if ee == 0:
                    return (ws1h if which == 0 else ws3h)[hc]
                e = ee - 1
                return (w1h if which == 0 else w3h)[e, hc]

            def w2_src(ee):
                if ee == 0:
                    return ws2h[:, :, :]
                return w2h[ee - 1]

            def emit_w13_dma(ee, hc):
                idx = ee * 8 + hc
                if idx >= 2:
                    p.wait("gpsimd", "mm3", mm3_after[idx - 2])
                for which in range(2):
                    v = p.dma("gpsimd",
                              w13[idx % 2][:, which * 2048:(which + 1) * 2048].rearrange(
                                  "p (dc h) -> p dc h", dc=16),
                              w13_src(ee, hc, which),
                              f"D_w13{idx % 2}")
                w13_v[idx] = v

            def emit_w2_chunk(ee, c):
                if ee >= 1:
                    p.wait("gpsimd", "mm2", mm2_after[ee - 1])
                w2_v[ee] = p.dma("gpsimd",
                                 w2s[:, c * 2 * D:(c + 1) * 2 * D].rearrange(
                                     "p (hc n) -> p hc n", hc=2),
                                 w2_src(ee)[:, c * 2:(c + 1) * 2, :],
                                 "D_w2")

            emit_w13_dma(0, 0)
            emit_w13_dma(0, 1)

            with ExitStack() as ph2s:
                sbs = lambda *a, **k: ph2s.enter_context(nc.sbuf_tensor(*a, **k))
                pG1 = [ph2s.enter_context(nc.psum_tensor(f"pG1{i}", [128, 512], F32)) for i in range(2)]
                pG3 = [ph2s.enter_context(nc.psum_tensor(f"pG3{i}", [128, 512], F32)) for i in range(2)]
                pR2 = [ph2s.enter_context(nc.psum_tensor(f"pR2{i}", [128, 512], F32)) for i in range(2)]
                Hsh = sbs("Hsh", [128, 8 * OWN], BF16)
                rG1 = Rot(2)
                rG3 = Rot(2)
                rR2 = Rot(2)
                rTmp = Rot(2)
                gck = [0]
                w2ck = [0]
                for hc in range(8):
                    idx = hc
                    wb13 = idx % 2
                    p.wait("tensor", f"D_w13{idx % 2}", w13_v[idx])
                    for tc in range(2):
                        rG1.acquire(p, "tensor", gck[0])
                        rG3.acquire(p, "tensor", gck[0])
                        pr = gck[0] % 2
                        for dc in range(16):
                            p.op("tensor", lambda e, wb13=wb13, dc=dc, tc=tc, pr=pr: e.matmul(
                                pG1[pr][:], w13[wb13][:, dc * 128:(dc + 1) * 128],
                                xn_own[:, dc * OWN + tc * 512: dc * OWN + (tc + 1) * 512],
                                start=(dc == 0), stop=(dc == 15)),
                                "mm1" if dc == 15 else None, 1)
                        m1v = p.val("mm1")
                        for dc in range(16):
                            p.op("tensor", lambda e, wb13=wb13, dc=dc, tc=tc, pr=pr: e.matmul(
                                pG3[pr][:], w13[wb13][:, 2048 + dc * 128:2048 + (dc + 1) * 128],
                                xn_own[:, dc * OWN + tc * 512: dc * OWN + (tc + 1) * 512],
                                start=(dc == 0), stop=(dc == 15)),
                                "mm3" if dc == 15 else None, 1)
                        m3v = p.val("mm3")
                        rTmp.acquire(p, "scalar", gck[0])
                        tb = gck[0] % 2
                        p.wait("scalar", "mm1", m1v)
                        s1v = p.op("scalar", lambda e, pr=pr, tb=tb: e.activation(
                            ts1[tb][:], pG1[pr][:], AF.Silu, bias=0.0, scale=1.0), "si")
                        rG1.release(gck[0], "si", s1v)
                        p.wait("vector", "mm3", m3v)
                        p.wait("vector", "si", s1v)
                        hv = p.op("vector", lambda e, pr=pr, tb=tb, hc=hc, tc=tc: e.tensor_mul(
                            Hsh[:, hc * OWN + tc * 512: hc * OWN + (tc + 1) * 512],
                            ts1[tb][:], pG3[pr][:]), "hh")
                        rG3.release(gck[0], "hh", hv)
                        rTmp.release(gck[0], "hh", hv)
                        gck[0] += 1
                    mm3_after[idx] = p.val("mm3")
                    if idx + 2 < 72:
                        emit_w13_dma(*divmod(idx + 2, 8))
                    if hc % 2 == 0:
                        emit_w2_chunk(0, hc // 2)
                hhv = p.val("hh")
                p.wait("tensor", "D_w2", w2_v[0])
                p.wait("tensor", "hh", hhv)
                for tt in range(8):
                    for dc in range(4):
                        rR2.acquire(p, "tensor", w2ck[0])
                        pr = w2ck[0] % 2
                        for hc in range(8):
                            p.op("tensor", lambda e, hc=hc, tt=tt, dc=dc, pr=pr: e.matmul(
                                pR2[pr][:], Hsh[:, hc * OWN + tt * 128: hc * OWN + (tt + 1) * 128],
                                w2s[:, hc * D + dc * 512: hc * D + (dc + 1) * 512],
                                start=(hc == 0), stop=(hc == 7)),
                                "mm2" if hc == 7 else None, 1)
                        m2v = p.val("mm2")
                        p.wait("vector", "mm2", m2v)
                        av = p.op("vector", lambda e, tt=tt, dc=dc, pr=pr: e.tensor_add(
                            R[:, tt * D + dc * 512: tt * D + (dc + 1) * 512],
                            R[:, tt * D + dc * 512: tt * D + (dc + 1) * 512], pR2[pr][:]), "rad")
                        rR2.release(w2ck[0], "rad", av)
                        w2ck[0] += 1
                mm2_after[0] = p.val("mm2")
                for c in range(4):
                    emit_w2_chunk(1, c)

            s_xnF.close()
            p.barrier("b2bc")

            iota_cap_d = nc.inline_tensor(
                np.tile(np.arange(1, CAP + 1, dtype=np.float32), (128, 1)), name="iotacapc")
            identf_d = nc.inline_tensor(np.eye(8, dtype=np.float32), name="identfc")
            with ExitStack() as ph2e:
                sbe = lambda *a, **k: ph2e.enter_context(nc.sbuf_tensor(*a, **k))
                pG1e = [ph2e.enter_context(nc.psum_tensor(f"pG1eb{i}", [128, 512], F32)) for i in range(2)]
                pXS = pG1e
                pG3e = [ph2e.enter_context(nc.psum_tensor(f"pG3eb{i}", [128, 512], F32)) for i in range(2)]
                pT2 = [ph2e.enter_context(nc.psum_tensor(f"pT2{i}", [128, 1024], BF16)) for i in range(2)]
                pD = [ph2e.enter_context(nc.psum_tensor(f"pD{i}", [128, 512], F32)) for i in range(2)]
                xs_sb = sbe("xs_sb", [128, 16 * CAP], BF16)
                Hsp = sbe("Hsp", [128, 8 * CAP], BF16)
                rout = sbe("rout", [128, 3 * D], BF16)
                P_sb = sbe("P_sb", [128, 8 * CAP], BF16)
                PTs = [sbe(f"PTs{i}", [128, 3 * OWN], BF16) for i in range(2)]
                posm = sbe("posm", [128, OWN], F32)
                P01 = [sbe(f"P01_{i}", [128, CAP], BF16) for i in range(2)]
                poscol = sbe("poscol", [128, 8 * 8], F32)
                mcol_sb = sbe("mcol_sb", [128, 8 * 8], F32)
                iota_cap = sbe("iota_cap", [128, CAP], F32)
                identf = sbe("identf", [8, 8], F32)
                p.dma("sync", iota_cap[:], iota_cap_d[:, :], "D_c2")
                p.dma("sync", identf[:], identf_d[:, :], "D_c2")
                ic_v = p.val("D_c2")

                wtv = p.val("wTe")
                p.wait("vector", "wTe", wtv)
                mv = p.op("vector", lambda e: e.tensor_scalar(
                    posm[32:40, :], wT_sb[0:8, :], 0.0, None, ALU.is_gt, ALU.bypass), "pv")
                p.wait("vector", "pv", mv)
                cv3 = p.op("vector", lambda e: e.tensor_tensor_scan(
                    posm[0:8, :], posm[32:40, :], posm[32:40, :], 0.0, ALU.add, ALU.bypass), "pv")
                p.op("vector", lambda e: e.tensor_scalar(
                    mcol_sb[:], wf_sbp[:], 0.0, None, ALU.is_gt, ALU.bypass), "pv")
                p.wait("tensor", "pv", p.val("pv"))
                p.wait("tensor", "D_c2", ic_v)
                for tc in range(8):
                    if tc >= 1:
                        p.wait("tensor", "pce", tc)
                    tv = p.op("tensor", lambda e, tc=tc: e.transpose(
                        pXS[0][:, 0:8], posm[0:8, tc * 128:(tc + 1) * 128], identf[:]), "ptr")
                    p.wait("vector", "ptr", tv)
                    pcv = p.op("vector", lambda e, tc=tc: e.tensor_copy(
                        poscol[:, tc * 8:(tc + 1) * 8], pXS[0][:, 0:8]), "pce")
                p.wait("tensor", "pce", p.val("pce"))
                p.wait("tensor", "D_c2", ic_v)

                rXS = Rot(2)
                rG1e_shared = rXS
                rPTrot = Rot(2)
                ptk = [0]
                poolk = [0]
                rG1e = Rot(2)
                rG3e = Rot(2)
                rD = Rot(2)
                rTmpE = Rot(2)
                xsk = [0]
                gck = [0]
                dk = [0]
                mmxs_after = {}
                mm2b_after = {}
                mmsc_after = {}

                pb_after = {}
                ptb_after = {}

                def emit_ptbuild(ex):
                    ptbuf = ex % 2
                    if ex >= 2:
                        p.wait("vector", "mmSC", mmsc_after[ex - 2])
                    for tc in range(8):
                        if ptk[0] >= 2:
                            p.wait("vector", "pttr", p.val("pttr") - 1)
                        pv01 = p.op("vector", lambda e, tc=tc, ex=ex, pb2=tc % 2: e.tensor_scalar(
                            P01[pb2][:], iota_cap[:],
                            poscol[:, tc * 8 + ex: tc * 8 + ex + 1],
                            mcol_sb[:, tc * 8 + ex: tc * 8 + ex + 1],
                            ALU.is_equal, ALU.mult), "p01")
                        p.wait("tensor", "p01", pv01)
                        for sc in range(3):
                            rPTrot.acquire(p, "tensor", ptk[0])
                            tv2 = p.op("tensor", lambda e, sc=sc, r=ptk[0] % 2, pb2=tc % 2: e.transpose(
                                pT2[r][:, 0:128], P01[pb2][:, sc * 128:(sc + 1) * 128], ident[:]), "pttr")
                            p.wait("scalar", "pttr", tv2)
                            cpv = p.op("scalar", lambda e, sc=sc, tc=tc, r=ptk[0] % 2, ptbuf=ptbuf: e.activation(
                                PTs[ptbuf][:, sc * OWN + tc * 128:(sc * OWN + (tc + 1) * 128)],
                                pT2[r][:, 0:128], AF.Copy, bias=0.0, scale=1.0), "ptb")
                            rPTrot.release(ptk[0], "ptb", cpv)
                            ptk[0] += 1
                    ptb_after[ex] = p.val("ptb")

                def emit_pbuild(ex):
                    if ex >= 1:
                        p.wait("vector", "mmXS", mmxs_after[ex - 1])
                    p.wait("vector", "pce", p.val("pce"))
                    for tc in range(8):
                        p.op("vector", lambda e, tc=tc, ex=ex: e.tensor_scalar(
                            P_sb[:, tc * CAP:(tc + 1) * CAP], iota_cap[:],
                            poscol[:, tc * 8 + ex: tc * 8 + ex + 1],
                            wf_sbp[:, 8 * tc + ex: 8 * tc + ex + 1],
                            ALU.is_equal, ALU.mult), "pb")
                    pb_after[ex] = p.val("pb")

                emit_pbuild(0)
                emit_ptbuild(0)
                for ex in range(8):
                    ee = ex + 1
                    p.wait("tensor", "pb", pb_after[ex])
                    if ex >= 1:
                        p.wait("vector", "mm1", p.val("mm1"))
                    for dc in range(16):
                        rXS.acquire(p, "tensor", poolk[0])
                        prx = poolk[0] % 2
                        for tc in range(8):
                            p.op("tensor", lambda e, dc=dc, tc=tc, prx=prx: e.matmul(
                                pXS[prx][:, 0:CAP], xn_tok[:, tc * D + dc * 128: tc * D + (dc + 1) * 128],
                                P_sb[:, tc * CAP:(tc + 1) * CAP],
                                start=(tc == 0), stop=(tc == 7)),
                                "mmXS" if tc == 7 else None, 1)
                        xv = p.val("mmXS")
                        p.wait("scalar", "mmXS", xv)
                        ev = p.op("scalar", lambda e, dc=dc, prx=prx: e.activation(
                            xs_sb[:, dc * CAP:(dc + 1) * CAP], pXS[prx][:, 0:CAP],
                            AF.Copy, bias=0.0, scale=1.0), "xse")
                        rXS.release(poolk[0], "xse", ev)
                        poolk[0] += 1
                    mmxs_after[ex] = p.val("mmXS")
                    xse_base = p.val("xse") - 16
                    ptbv = ptb_after[ex]
                    for hc in range(8):
                        idx = ee * 8 + hc
                        wb13 = idx % 2
                        p.wait("tensor", f"D_w13{idx % 2}", w13_v[idx])
                        rXS.acquire(p, "tensor", poolk[0])
                        rG3e.acquire(p, "tensor", gck[0])
                        prg = poolk[0] % 2
                        prg3 = gck[0] % 2
                        for dc in range(16):
                            if hc == 0:
                                p.wait("tensor", "xse", xse_base + dc + 1)
                            p.op("tensor", lambda e, wb13=wb13, dc=dc, prg=prg: e.matmul(
                                pG1e[prg][:, 0:CAP], w13[wb13][:, dc * 128:(dc + 1) * 128],
                                xs_sb[:, dc * CAP:(dc + 1) * CAP],
                                start=(dc == 0), stop=(dc == 15)),
                                "mm1" if dc == 15 else None, 1)
                        m1v = p.val("mm1")
                        for dc in range(16):
                            p.op("tensor", lambda e, wb13=wb13, dc=dc, prg3=prg3: e.matmul(
                                pG3e[prg3][:, 0:CAP], w13[wb13][:, 2048 + dc * 128:2048 + (dc + 1) * 128],
                                xs_sb[:, dc * CAP:(dc + 1) * CAP],
                                start=(dc == 0), stop=(dc == 15)),
                                "mm3" if dc == 15 else None, 1)
                        m3v = p.val("mm3")
                        rTmpE.acquire(p, "scalar", gck[0])
                        tb = gck[0] % 2
                        p.wait("scalar", "mm1", m1v)
                        if ex >= 1 and hc == 0:
                            p.wait("scalar", "mm2", mm2b_after[ex - 1])
                        s1v = p.op("scalar", lambda e, prg=prg, tb=tb: e.activation(
                            ts1[tb][:, 0:CAP], pG1e[prg][:, 0:CAP], AF.Silu, bias=0.0, scale=1.0), "si")
                        rXS.release(poolk[0], "si", s1v)
                        p.wait("vector", "mm3", m3v)
                        p.wait("vector", "si", s1v)
                        if ex >= 1 and hc == 0:
                            p.wait("vector", "mm2", mm2b_after[ex - 1])
                        hv = p.op("vector", lambda e, prg3=prg3, tb=tb, hc=hc: e.tensor_mul(
                            Hsp[:, hc * CAP:(hc + 1) * CAP],
                            ts1[tb][:, 0:CAP], pG3e[prg3][:, 0:CAP]), "hh")
                        rG3e.release(gck[0], "hh", hv)
                        rTmpE.release(gck[0], "hh", hv)
                        gck[0] += 1
                        poolk[0] += 1
                        mm3_after[idx] = p.val("mm3")
                        if idx + 2 < 72:
                            emit_w13_dma(*divmod(idx + 2, 8))

                    hhv = p.val("hh")
                    if ex + 1 < 8:
                        emit_pbuild(ex + 1)
                        emit_ptbuild(ex + 1)
                    p.wait("tensor", "D_w2", w2_v[ee])
                    p.wait("tensor", "hh", hhv)
                    if ex >= 1:
                        p.wait("scalar", "mmSC", mmsc_after[ex - 1])
                    for sc in range(3):
                        for dc in range(4):
                            rD.acquire(p, "tensor", dk[0])
                            prd = dk[0] % 2
                            for hc in range(8):
                                p.op("tensor", lambda e, hc=hc, sc=sc, dc=dc, prd=prd: e.matmul(
                                    pD[prd][:], Hsp[:, hc * CAP + sc * 128: hc * CAP + sc * 128 + 128],
                                    w2s[:, hc * D + dc * 512: hc * D + (dc + 1) * 512],
                                    start=(hc == 0), stop=(hc == 7)),
                                    "mm2" if hc == 7 else None, 1)
                            m2v = p.val("mm2")
                            p.wait("scalar", "mm2", m2v)
                            rv = p.op("scalar", lambda e, sc=sc, dc=dc, prd=prd: e.activation(
                                rout[:, sc * D + dc * 512: sc * D + (dc + 1) * 512], pD[prd][:],
                                AF.Copy, bias=0.0, scale=1.0), "rte")
                            rD.release(dk[0], "rte", rv)
                            dk[0] += 1
                    mm2b_after[ex] = p.val("mm2")
                    mm2_after[ee] = p.val("mm2")
                    if ee + 1 <= 8:
                        for c in range(4):
                            emit_w2_chunk(ee + 1, c)
                    rtev = p.val("rte")
                    p.wait("tensor", "rte", rtev)
                    p.wait("tensor", "ptb", ptbv)
                    for tt in range(8):
                        for dc in range(4):
                            rD.acquire(p, "tensor", dk[0])
                            prd = dk[0] % 2
                            for sc in range(3):
                                p.op("tensor", lambda e, sc=sc, tt=tt, dc=dc, prd=prd, ptbuf=ex % 2: e.matmul(
                                    pD[prd][:], PTs[ptbuf][:, sc * OWN + tt * 128: sc * OWN + (tt + 1) * 128],
                                    rout[:, sc * D + dc * 512: sc * D + (dc + 1) * 512],
                                    start=(sc == 0), stop=(sc == 2)),
                                    "mmSC" if sc == 2 else None, 1)
                            scv = p.val("mmSC")
                            p.wait("vector", "mmSC", scv)
                            av = p.op("vector", lambda e, tt=tt, dc=dc, prd=prd: e.tensor_add(
                                R[:, tt * D + dc * 512: tt * D + (dc + 1) * 512],
                                R[:, tt * D + dc * 512: tt * D + (dc + 1) * 512], pD[prd][:]), "rad")
                            rD.release(dk[0], "rad", av)
                            dk[0] += 1
                    mmsc_after[ex] = p.val("mmSC")

        if debug and stage >= 2:
            p.wait("sync", "rad", p.val("rad"))
            p.dma("sync", dbg_R[:], R[:], "D_sd")
        p.barrier("b2f")

        if stage < 2:
            p.op("vector", lambda e: e.memset(R[:], 0.0), "rad")
        with ExitStack() as phf:
            sbf = lambda *a, **k: phf.enter_context(nc.sbuf_tensor(*a, **k))
            xrl = [sbf(f"xrl{i}", [128, D], F32) for i in range(2)]
            osb = [sbf(f"osb{i}", [128, D], F32) for i in range(2)]
            rX = Rot(2)
            rO = Rot(2)
            p.wait("vector", "rad", p.val("rad"))
            for tt in range(8):
                rX.acquire(p, "sync", tt)
                xv = p.dma("sync", xrl[tt % 2][:], x_own[tt * 128:(tt + 1) * 128, :], f"D_xr{tt % 2}")
                p.wait("vector", f"D_xr{tt % 2}", xv)
                rO.acquire(p, "vector", tt)
                fv = p.op("vector", lambda e, tt=tt: e.tensor_add(
                    osb[tt % 2][:], R[:, tt * D:(tt + 1) * D], xrl[tt % 2][:]), "fin")
                rX.release(tt, "fin", fv)
                p.wait("sync", "fin", fv)
                sv = p.dma("sync", out_c[tt * 128:(tt + 1) * 128, :], osb[tt % 2][:], f"D_st{tt % 2}")
                rO.release(tt, f"D_st{tt % 2}", sv)
            p.wait("sync", "D_st0", p.val("D_st0"))
            p.wait("sync", "D_st1", p.val("D_st1"))
            if debug:
                p.wait("sync", "D_sd", p.val("D_sd"))

        with nc.Block() as block:
            p.finalize(block)
    return nc


_NC_CACHE = {}


def get_nc(debug=False):
    key = bool(debug)
    if key not in _NC_CACHE:
        _NC_CACHE[key] = build(debug)
    return _NC_CACHE[key]


def prep_inputs(L0_attn_x, L0_attn_w_qkv, L0_attn_b_qkv, L0_attn_w_out, L0_attn_b_out,
                L0_ffn_gate_w, L0_ffn_w1, L0_ffn_w2, L0_ffn_w3,
                L0_ffn_ws1, L0_ffn_ws2, L0_ffn_ws3):
    x = np.ascontiguousarray(np.asarray(L0_attn_x, dtype=np.float32))
    w_qkv = np.asarray(L0_attn_w_qkv, dtype=np.float32)
    b_qkv = np.ascontiguousarray(np.asarray(L0_attn_b_qkv, dtype=np.float32))
    w_out = np.asarray(L0_attn_w_out, dtype=np.float32)
    b_outv = np.ascontiguousarray(np.asarray(L0_attn_b_out, dtype=np.float32))
    gate_w = np.asarray(L0_ffn_gate_w, dtype=np.float32)
    w1 = np.asarray(L0_ffn_w1, dtype=np.float32)
    w2 = np.asarray(L0_ffn_w2, dtype=np.float32)
    w3 = np.asarray(L0_ffn_w3, dtype=np.float32)
    ws1 = np.asarray(L0_ffn_ws1, dtype=np.float32)
    ws2 = np.asarray(L0_ffn_ws2, dtype=np.float32)
    ws3 = np.asarray(L0_ffn_ws3, dtype=np.float32)

    shared = {
        "wqkv_t": np.ascontiguousarray(
            w_qkv.reshape(3, H, HD, 16, 128).transpose(1, 4, 0, 3, 2)),
        "bq_col": np.ascontiguousarray(b_qkv.reshape(3, H, HD).transpose(2, 0, 1).reshape(128, 48)),
        "b_qkv": b_qkv,
        "w_outT": np.ascontiguousarray(w_out.T.reshape(16, 128, 4, 512).transpose(2, 1, 0, 3)),
        "b_out": b_outv,
        "gate_wT": np.ascontiguousarray(gate_w.T),
        "w1h": np.ascontiguousarray(
            w1.transpose(0, 2, 1).reshape(NE, 16, 128, 8, 128).transpose(0, 3, 2, 1, 4)),
        "w3h": np.ascontiguousarray(
            w3.transpose(0, 2, 1).reshape(NE, 16, 128, 8, 128).transpose(0, 3, 2, 1, 4)),
        "w2h": np.ascontiguousarray(
            w2.transpose(0, 2, 1).reshape(NE, 8, 128, D).transpose(0, 2, 1, 3)),
        "ws1h": np.ascontiguousarray(
            ws1.T.reshape(16, 128, 8, 128).transpose(2, 1, 0, 3)),
        "ws3h": np.ascontiguousarray(
            ws3.T.reshape(16, 128, 8, 128).transpose(2, 1, 0, 3)),
        "ws2h": np.ascontiguousarray(ws2.T.reshape(8, 128, D).transpose(1, 0, 2)),
    }
    in_maps = []
    for c in range(8):
        b, j = divmod(c, 2)
        m = dict(shared)
        m["x_own"] = np.ascontiguousarray(x[b, j * OWN:(j + 1) * OWN, :])
        m["x_oth"] = np.ascontiguousarray(x[b, (1 - j) * OWN:(2 - j) * OWN, :])
        bias = np.full((128, 18), EXPSH, dtype=np.float32)
        if j == 0:
            bias[:, 0:16] = -1e30
        bias[:, 16] = EXPSH
        bias[:, 17] = EXPSH - BIGM * SCL
        m["expb"] = bias
        in_maps.append(m)
    return in_maps


def kernel(L0_attn_x, L0_attn_w_qkv, L0_attn_b_qkv, L0_attn_w_out, L0_attn_b_out,
           L0_ffn_gate_w, L0_ffn_w1, L0_ffn_w2, L0_ffn_w3,
           L0_ffn_ws1, L0_ffn_ws2, L0_ffn_ws3):
    in_maps = prep_inputs(L0_attn_x, L0_attn_w_qkv, L0_attn_b_qkv, L0_attn_w_out,
                          L0_attn_b_out, L0_ffn_gate_w, L0_ffn_w1, L0_ffn_w2,
                          L0_ffn_w3, L0_ffn_ws1, L0_ffn_ws2, L0_ffn_ws3)
    nc = get_nc(debug=False)
    res = run_bass_kernel_spmd(nc, in_maps, core_ids=list(range(8)))
    out = np.empty((B, S, D), dtype=np.float32)
    for c in range(8):
        b, j = divmod(c, 2)
        out[b, j * OWN:(j + 1) * OWN, :] = res.results[c]["out"]
    return out


# version 19
# speedup vs baseline: 1.0293x; 1.0078x over previous
import math
import numpy as np
import ml_dtypes
from contextlib import ExitStack

import concourse.bass as bass
import concourse.mybir as mybir
from concourse.bass_utils import run_bass_kernel_spmd

F32 = mybir.dt.float32
BF16 = mybir.dt.bfloat16
AF = mybir.ActivationFunctionType
ALU = mybir.AluOpType
AX = mybir.AxisListType

D = 2048
H = 16
HD = 128
S = 2048
B = 4
OWN = 1024
NHID = 1024
NE = 8
SCL = 1.0 / math.sqrt(HD)
BIGM = 600.0
EXPSH = -4.0
EPS = 1e-5

ENGINES = ("tensor", "vector", "scalar", "gpsimd", "sync")


class Prog:

    def __init__(self, nc, ctx):
        self.nc = nc
        self.ctx = ctx
        self.ops = {e: [] for e in ENGINES}
        self.semh = {}
        self.semc = {}

    def sem(self, name):
        if name not in self.semh:
            self.semh[name] = self.ctx.enter_context(self.nc.semaphore(name))
            self.semc[name] = 0
        return self.semh[name]

    def val(self, name):
        self.sem(name)
        return self.semc[name]

    def op(self, eng, f, sem=None, n=1):
        if sem is None:
            self.ops[eng].append(f)
            return None
        h = self.sem(sem)
        self.semc[sem] += n
        self.ops[eng].append(lambda e, f=f, h=h, n=n: f(e).then_inc(h, n))
        return self.semc[sem]

    def wait(self, eng, sem, v=None):
        h = self.sem(sem)
        if v is None:
            v = self.semc[sem]
        if v > 0:
            self.ops[eng].append(lambda e, h=h, v=v: e.wait_ge(h, v))

    def dma(self, eng, out, in_, sem):
        return self.op(eng, lambda e, o=out, i=in_: e.dma_start(out=o, in_=i), sem, 16)

    def barrier(self, tag):
        vals = [(n, self.semc[n]) for n in list(self.semc) if self.semc[n] > 0]
        for eng in ENGINES:
            for n, v in vals:
                self.wait(eng, n, v)

    def finalize(self, block):
        def run(engine, ops):
            for f in ops:
                f(engine)
        block.tensor(lambda e: run(e, self.ops["tensor"]))
        block.vector(lambda e: run(e, self.ops["vector"]))
        block.scalar(lambda e: run(e, self.ops["scalar"]))
        block.gpsimd(lambda e: run(e, self.ops["gpsimd"]))
        block.sync(lambda e: run(e, self.ops["sync"]))


class Rot:

    def __init__(self, n):
        self.n = n
        self.free = [None] * n

    def acquire(self, p, eng, i):
        w = self.free[i % self.n]
        if w is not None:
            p.wait(eng, w[0], w[1])

    def release(self, i, sem, val):
        self.free[i % self.n] = (sem, val)


def build(debug=False, stage=3):
    nc = bass.Bass("TRN2", target_bir_lowering=False, debug=False)

    x_own = nc.dram_tensor("x_own", [OWN, D], F32, kind="ExternalInput").ap()
    x_oth = nc.dram_tensor("x_oth", [OWN, D], F32, kind="ExternalInput").ap()
    expb = nc.dram_tensor("expb", [128, 18], F32, kind="ExternalInput").ap()
    wqkv_t = nc.dram_tensor("wqkv_t", [H, 128, 3, 16, HD], F32, kind="ExternalInput").ap()
    bq_col = nc.dram_tensor("bq_col", [128, 48], F32, kind="ExternalInput").ap()
    b_qkv = nc.dram_tensor("b_qkv", [3 * D], F32, kind="ExternalInput").ap()
    w_outT = nc.dram_tensor("w_outT", [4, 128, 16, 512], F32, kind="ExternalInput").ap()
    b_out = nc.dram_tensor("b_out", [D], F32, kind="ExternalInput").ap()
    gate_wT = nc.dram_tensor("gate_wT", [D, NE], F32, kind="ExternalInput").ap()
    w1h = nc.dram_tensor("w1h", [NE, 8, 128, 16, 128], F32, kind="ExternalInput").ap()
    w3h = nc.dram_tensor("w3h", [NE, 8, 128, 16, 128], F32, kind="ExternalInput").ap()
    w2h = nc.dram_tensor("w2h", [NE, 128, 8, D], F32, kind="ExternalInput").ap()
    ws1h = nc.dram_tensor("ws1h", [8, 128, 16, 128], F32, kind="ExternalInput").ap()
    ws3h = nc.dram_tensor("ws3h", [8, 128, 16, 128], F32, kind="ExternalInput").ap()
    ws2h = nc.dram_tensor("ws2h", [128, 8, D], F32, kind="ExternalInput").ap()
    out_c = nc.dram_tensor("out", [OWN, D], F32, kind="ExternalOutput").ap()
    if debug:
        dbg_xn = nc.dram_tensor("dbg_xn", [128, 2 * 16 * OWN], BF16, kind="ExternalOutput").ap()
        dbg_attn = nc.dram_tensor("dbg_attn", [128, 16 * OWN], BF16, kind="ExternalOutput").ap()
        dbg_w = nc.dram_tensor("dbg_w", [128, 64], BF16, kind="ExternalOutput").ap()
        dbg_R = nc.dram_tensor("dbg_R", [128, 8 * D], F32, kind="ExternalOutput").ap()

    bf = ml_dtypes.bfloat16
    ident_d = nc.inline_tensor(np.eye(128, dtype=np.float32).astype(bf), name="identc").ap()
    ibig_d = nc.inline_tensor((BIGM * np.eye(128, dtype=np.float32)).astype(bf), name="ibigc").ap()
    tri = np.zeros((4, 128, 512), dtype=np.float32)
    for j2 in range(4):
        for k in range(128):
            tri[j2, k, j2 * 128 + k:] = 1.0
    tri_d = nc.inline_tensor(tri.astype(bf), name="tric").ap()
    selm = np.zeros((8, 8 * 128), dtype=np.float32)
    for e in range(8):
        selm[e, e * 128:(e + 1) * 128] = 1.0
    sel_d = nc.inline_tensor(selm.astype(bf), name="selc").ap()
    ones_d = nc.inline_tensor(np.ones((1, 128), dtype=np.float32).astype(bf), name="ones1c").ap()

    ctx = ExitStack()
    with ctx:
        p = Prog(nc, ctx)
        sb = lambda *a, **k: ctx.enter_context(nc.sbuf_tensor(*a, **k))
        xn_tok = sb("xn_tok", [128, 8 * D], BF16)
        w_sbp = sb("w_sbp", [128, 8 * 8], BF16)
        wf_sbp = sb("wf_sbp", [128, 8 * 8], F32)
        ident = sb("ident", [128, 128], BF16)
        ibig = sb("ibig", [128, 128], BF16)
        ones1 = sb("ones1", [1, 128], BF16)
        bqc_sb = sb("bqc_sb", [128, 48], F32)
        gw_sb = sb("gw_sb", [128, 16 * 8], BF16)
        expb_sb = sb("expb_sb", [128, 18], F32)
        s_xnF = ExitStack()
        ctx.enter_context(s_xnF)
        xn_own = s_xnF.enter_context(nc.sbuf_tensor("xn_own", [128, 16 * OWN], BF16, side="right"))
        s_attn = ExitStack()
        ctx.enter_context(s_attn)
        attn = s_attn.enter_context(nc.sbuf_tensor("attn", [128, 16 * OWN], BF16, side="right"))
        bor_sb = s_attn.enter_context(nc.sbuf_tensor("bor_sb", [1, D], BF16, side="right"))
        s_xnoth = ExitStack()
        ctx.enter_context(s_xnoth)
        xn_oth = s_xnoth.enter_context(nc.sbuf_tensor("xn_oth", [128, 16 * OWN], BF16, side="right"))
        tri_sb = s_xnoth.enter_context(nc.sbuf_tensor("tri_sb", [128, 4 * 512], BF16))

        p.dma("sync", ident[:], ident_d[:, :], "D_c")
        p.dma("sync", ibig[:], ibig_d[:, :], "D_c")
        p.dma("sync", tri_sb[:].rearrange("p (m q) -> p m q", m=4),
              tri_d[:, :, :].rearrange("m p q -> p m q"), "D_c")
        p.dma("sync", ones1[:], ones_d[:, :], "D_c")
        p.dma("sync", bqc_sb[:], bq_col[:, :], "D_c")
        p.dma("sync", expb_sb[:], expb[:, :], "D_c")
        p.dma("gpsimd", bor_sb[:], b_out[:].rearrange("(a n) -> a n", a=1), "D_cg")
        p.dma("gpsimd", gw_sb[:].rearrange("p (c e) -> p c e", c=16),
              gate_wT.rearrange("(c p) e -> p c e", p=128), "D_cg")
        cst_v = p.val("D_c")
        cstg_v = p.val("D_cg")

        with ExitStack() as ph0:
            sb0 = lambda *a, **k: ph0.enter_context(nc.sbuf_tensor(*a, **k))
            ps0 = [ph0.enter_context(nc.psum_tensor(f"p0_{i}", [128, 1024], BF16)) for i in range(6)]
            xt = [sb0(f"xt{i}", [128, D], F32) for i in range(3)]
            sqj = sb0("sqj", [128, D], BF16)
            sqj2 = sb0("sqj2", [128, D], BF16)
            st1 = [sb0(f"st1{i}", [128, 8], F32) for i in range(3)]
            xntk = [sb0(f"xntk{i}", [128, D], BF16) for i in range(3)]
            xc = [sb0(f"xc{i}", [128, D], F32) for i in range(3)]
            epscol = sb0("epscol", [128, 1], F32)
            rcol = [sb0(f"rcol{i}", [128, 2], F32) for i in range(3)]
            p0rot = Rot(6)
            xtk_cnt = [0]
            pend0 = []
            p.op("vector", lambda e: e.memset(epscol[:], EPS), "v2")

            def emit_tile_transposes(tt, ss, ttl, dst, lnv):
                p.wait("tensor", "lnA", lnv)
                if tt == 0:
                    p.wait("tensor", "D_c", cst_v)
                for dc in range(16):
                    p0rot.acquire(p, "tensor", tt * 16 + dc)
                    tv = p.op("tensor", lambda e, s=ss, dc=dc, r=(tt * 16 + dc) % 6: e.transpose(
                        ps0[r][:, 0:128], xntk[s][:, dc * 128:(dc + 1) * 128], ident[:]), "xt")
                    eng = "vector" if dc % 2 == 0 else "scalar"
                    p.wait(eng, "xt", tv)
                    if eng == "vector":
                        ev2 = p.op(eng, lambda e, d=dst, dc=dc, ttl=ttl, r=(tt * 16 + dc) % 6: e.tensor_copy(
                            d[:, dc * OWN + ttl * 128: dc * OWN + (ttl + 1) * 128], ps0[r][:, 0:128]), "xe")
                        p0rot.release(tt * 16 + dc, "xe", ev2)
                    else:
                        ev2 = p.op(eng, lambda e, d=dst, dc=dc, ttl=ttl, r=(tt * 16 + dc) % 6: e.activation(
                            d[:, dc * OWN + ttl * 128: dc * OWN + (ttl + 1) * 128], ps0[r][:, 0:128],
                            AF.Copy, bias=0.0, scale=1.0), "xe2")
                        p0rot.release(tt * 16 + dc, "xe2", ev2)

            for tt in range(16):
                own = tt < 8
                src = (x_own if own else x_oth)
                dst = (xn_own if own else xn_oth)
                ttl = tt % 8
                b3 = tt % 3
                ss = tt % 3
                xcs = tt % 3
                if tt >= 3:
                    p.wait("sync", "lnA", tt - 3 + 1)
                xsem = f"D_x{tt % 3}"
                xv = p.dma("sync", xt[b3][:], src[ttl * 128:(ttl + 1) * 128, :], xsem)
                p.wait("vector", xsem, xv)
                if tt >= 3:
                    p.wait("vector", "lnA", tt - 3 + 1)
                p.wait("scalar", xsem, xv)
                if tt >= 3:
                    p.wait("scalar", "lnA", tt - 3 + 1)
                v1 = p.op("scalar", lambda e, b=b3, s=ss: e.activation(
                    sqj2[:], xt[b][:], AF.Copy, bias=0.0, scale=1.0,
                    accum_out=st1[s][:, 0:1]), "v1")
                s2v = p.op("scalar", lambda e, b=b3, s=ss: e.activation(
                    sqj[:], xt[b][:], AF.Square, bias=0.0, scale=1.0,
                    accum_out=st1[s][:, 1:2]), "sq")
                p.wait("vector", "v1", v1)
                mv = p.op("vector", lambda e, s=ss: e.tensor_scalar(
                    st1[s][:, 2:3], st1[s][:, 0:1], -1.0 / D, None, ALU.mult, ALU.bypass), "v2")
                p.wait("vector", "v2", mv)
                m2v = p.op("vector", lambda e, s=ss: e.tensor_scalar(
                    st1[s][:, 4:5], st1[s][:, 2:3], st1[s][:, 2:3], None, ALU.mult, ALU.bypass), "v2")
                p.wait("vector", "sq", s2v)
                p.wait("vector", "v2", m2v)
                vpev = p.op("vector", lambda e, s=ss: e.scalar_tensor_tensor(
                    st1[s][:, 5:6], st1[s][:, 1:2], 1.0 / D, st1[s][:, 4:5],
                    ALU.mult, ALU.subtract), "v2")
                p.wait("scalar", "v2", vpev)
                sdv = p.op("scalar", lambda e, s=ss: e.activation(
                    st1[s][:, 6:7], st1[s][:, 5:6], AF.Sqrt, bias=epscol[:, 0:1], scale=1.0), "s2")
                if tt >= 2:
                    p.wait("vector", "lnA", tt - 2 + 1)
                xcv = p.op("vector", lambda e, b=b3, s=ss, xs=xcs: e.tensor_scalar(
                    xc[xs][:], xt[b][:], st1[s][:, 2:3], None, ALU.add, ALU.bypass), "xcv")
                p.wait("vector", "s2", sdv)
                rsv = p.op("vector", lambda e, s=ss: e.reciprocal(rcol[s][:, 0:1], st1[s][:, 6:7]), "v3")
                p.wait("scalar", "v3", rsv)
                p.wait("scalar", "xcv", xcv)
                if tt >= 3:
                    p.wait("scalar", "xt", 16 * (tt - 2))
                if tt >= 3:
                    p.wait("scalar", "xtk", max(0, xtk_cnt[0] - 2))
                lnv = p.op("scalar", lambda e, s=ss, xs=xcs: e.activation(
                    xntk[s][:], xc[xs][:], AF.Copy, bias=0.0, scale=rcol[s][:, 0:1]), "lnA")
                if own:
                    p.wait("vector", "lnA", lnv)
                    p.op("vector", lambda e, s=ss, ttl=ttl: e.tensor_copy(
                        xn_tok[:, ttl * D:(ttl + 1) * D], xntk[s][:]), "xtk")
                    xtk_cnt[0] += 1
                pend0.append((tt, ss, ttl, dst, lnv))
                if len(pend0) > 1:
                    emit_tile_transposes(*pend0.pop(0))
                if tt == 15:
                    while pend0:
                        emit_tile_transposes(*pend0.pop(0))

        p.barrier("b01")
        if debug:
            p.dma("sync", dbg_xn[:, 0:16 * OWN], xn_own[:], "D_sd")
            p.dma("sync", dbg_xn[:, 16 * OWN:], xn_oth[:], "D_sd")

        if stage >= 1:
         with ExitStack() as ph1:
            sb1 = lambda *a, **k: ph1.enter_context(nc.sbuf_tensor(*a, **k))
            pP = [ph1.enter_context(nc.psum_tensor(f"pP{i}", [128, 1024], F32)) for i in range(2)]
            pA = [ph1.enter_context(nc.psum_tensor(f"pA{i}", [128, 512], F32)) for i in range(2)]
            pT = [ph1.enter_context(nc.psum_tensor(f"pT{i}", [128, 1024], BF16)) for i in range(2)]
            qkvw = [sb1("qkvw0", [128, 3 * 16 * 128], BF16)]
            K_sb = [sb1("K0", [128, S], BF16)]
            Q_sb = [sb1(f"Q{i}", [128, OWN], BF16) for i in range(2)]
            V_sb = [sb1("V0", [128, S], BF16)]
            VT_sb = [sb1(f"VT{i}", [128, 16 * 129], BF16) for i in range(2)]
            PT_sb = [sb1(f"PT{i}", [128, 16 * 512], BF16) for i in range(2)]
            AT_sb = [sb1(f"AT{i}", [128, 128], BF16) for i in range(2)]
            rc_sb = [sb1(f"rc{i}", [128, 1], F32) for i in range(2)]
            rS = Rot(2)
            rQp = Rot(4)
            rAp = Rot(2)
            rTp = Rot(2)
            rAT = Rot(2)

            for i in range(2):
                p.op("vector", lambda e, i=i: e.memset(
                    VT_sb[i][:].rearrange("p (kc c) -> p kc c", c=129)[:, :, 128:129], 1.0), "vte")

            def xn_rhs(dc, c0, n):
                if c0 < OWN:
                    assert c0 + n <= OWN
                    return xn_own[:, dc * OWN + c0: dc * OWN + c0 + n]
                assert c0 + n <= 2 * OWN
                return xn_oth[:, dc * OWN + c0 - OWN: dc * OWN + c0 - OWN + n]

            sched = {0: [(kc, 'diag', kc) for kc in range(4)] + [(kc, 'data', None) for kc in range(8, 16)],
                     1: [(kc, 'plain', None) for kc in range(4)] + [(kc, 'diag', kc - 4) for kc in range(4, 8)]
                        + [(kc, 'data', None) for kc in range(8, 16)]}

            qknt = 0
            unit = -1
            avdone = {}
            ptv_maps = {}
            pending_post = []

            def emit_qkv(h, b):
                nonlocal qknt
                wv = wq_v[h]
                p.wait("tensor", "D_wq0", wv)
                p.wait("tensor", "pt", p.val("pt"))
                for which, nchunks, dstl in ((1, 4, 'K'), (2, 4, 'V'), (0, 2, 'Q')):
                    for c4 in range(nchunks):
                        rQp.acquire(p, "tensor", qknt)
                        t2, hf = (qknt // 2) % 2, qknt % 2
                        for dc in range(16):
                            p.op("tensor", lambda e, b=b, w=which, dc=dc, c4=c4, t2=t2, hf=hf: e.matmul(
                                pP[t2][:, hf * 512:(hf + 1) * 512],
                                qkvw[0][:, (w * 16 + dc) * 128:(w * 16 + dc + 1) * 128],
                                xn_rhs(dc, c4 * 512, 512),
                                start=(dc == 0), stop=(dc == 15)),
                                "mmQ" if dc == 15 else None, 1)
                        mv = p.val("mmQ")
                        p.wait("vector", "mmQ", mv)
                        bcol = bqc_sb[:, which * 16 + h: which * 16 + h + 1]
                        if dstl == 'K':
                            dst = K_sb[0][:, c4 * 512:(c4 + 1) * 512]
                        elif dstl == 'V':
                            dst = V_sb[0][:, c4 * 512:(c4 + 1) * 512]
                        else:
                            dst = Q_sb[b][:, c4 * 512:(c4 + 1) * 512]
                        ev = p.op("vector", lambda e, dst=dst, t2=t2, hf=hf, bcol=bcol: e.tensor_scalar(
                            dst, pP[t2][:, hf * 512:(hf + 1) * 512], bcol, None, ALU.add, ALU.bypass), "kqv")
                        rQp.release(qknt, "kqv", ev)
                        qknt += 1
                relw[h] = p.val("mmQ")
                kv = p.val("kqv")
                p.wait("tensor", "kqv", kv)
                if h >= 2:
                    p.wait("vector", "av", av_after.get(h - 2, 0))
                for kc in range(16):
                    rTp.acquire(p, "tensor", trn[0])
                    tv = p.op("tensor", lambda e, b=b, kc=kc, r=trn[0] % 2: e.transpose(
                        pT[r][:, 0:128], V_sb[0][:, kc * 128:(kc + 1) * 128], ident[:]), "trp")
                    p.wait("vector", "trp", tv)
                    ev = p.op("vector", lambda e, b=b, kc=kc, r=trn[0] % 2: e.tensor_copy(
                        VT_sb[b][:, kc * 129: kc * 129 + 128], pT[r][:, 0:128]), "vte")
                    rTp.release(trn[0], "vte", ev)
                    trn[0] += 1
                relvs[h] = p.val("trp")
                vte_after[h] = p.val("vte")

            def emit_scores(h, qb, b, u):
                ptb = u % 2
                if u >= 2:
                    p.wait("scalar", "av", avdone[u - 2])
                ptv_map = {}
                p.wait("tensor", "kqv", p.val("kqv"))
                chunks = sched[qb]
                for pi in range(0, len(chunks), 2):
                    pair = chunks[pi:pi + 2]
                    rS.acquire(p, "tensor", sck[0])
                    pr = sck[0] % 2
                    smv = None
                    for hf, (kc, kind, j2) in enumerate(pair):
                        fin = (hf == 1 and kind != 'diag')
                        smv = p.op("tensor", lambda e, b=b, kc=kc, qb=qb, pr=pr, hf=hf,
                             last=(kind != 'diag'): e.matmul(
                            pP[pr][:, hf * 512:(hf + 1) * 512], K_sb[0][:, kc * 128:(kc + 1) * 128],
                            Q_sb[b][:, qb * 512:(qb + 1) * 512],
                            start=True, stop=last), "mmS" if fin else None)
                        if kind == 'diag':
                            smv = p.op("tensor", lambda e, j2=j2, pr=pr, hf=hf: e.matmul(
                                pP[pr][:, hf * 512:(hf + 1) * 512], ibig[:],
                                tri_sb[:, j2 * 512:(j2 + 1) * 512],
                                start=False, stop=True), "mmS" if hf == 1 else None)
                    p.wait("scalar", "mmS", smv)
                    kc0, kind0, _ = pair[0]
                    if kind0 == 'diag':
                        bias = expb_sb[:, 17:18]
                    elif kind0 == 'plain':
                        bias = expb_sb[:, 16:17]
                    else:
                        bias = expb_sb[:, qb * 8 + (kc0 - 8): qb * 8 + (kc0 - 8) + 1]
                    ev = p.op("scalar", lambda e, ptb=ptb, kc0=kc0, pr=pr, bias=bias: e.activation(
                        PT_sb[ptb][:, kc0 * 512:(kc0 + 2) * 512], pP[pr][:], AF.Exp,
                        bias=bias, scale=SCL), "pt")
                    rS.release(sck[0], "pt", ev)
                    for (kc, _, _) in pair:
                        ptv_map[kc] = ev
                    sck[0] += 1
                ptv_maps[u] = ptv_map

            def post_av(h, qb, ql, av_idx, avv):
                pr = av_idx % 2
                p.wait("vector", "av", avv)
                rcv = p.op("vector", lambda e, pr=pr, r=av_idx % 2: e.reciprocal(
                    rc_sb[r][:], pA[pr][:, 128:129]), "rc")
                p.wait("scalar", "rc", rcv)
                rAT.acquire(p, "scalar", av_idx)
                atv = p.op("scalar", lambda e, pr=pr, r=av_idx % 2: e.activation(
                    AT_sb[r][:], pA[pr][:, 0:128], AF.Copy, bias=0.0, scale=rc_sb[r][:, 0:1]), "at")
                rAp.release(av_idx, "at", atv)
                p.wait("tensor", "at", atv)
                rTp.acquire(p, "tensor", trn[0])
                tv = p.op("tensor", lambda e, r=av_idx % 2, tr=trn[0] % 2: e.transpose(
                    pT[tr][:, 0:128], AT_sb[r][:], ident[:]), "trp")
                rAT.release(av_idx, "trp", tv)
                p.wait("scalar", "trp", tv)
                col = h * OWN + qb * 512 + ql * 128
                ev = p.op("scalar", lambda e, col=col, tr=trn[0] % 2: e.activation(
                    attn[:, col:col + 128], pT[tr][:, 0:128],
                    AF.Copy, bias=0.0, scale=1.0), "ao")
                rTp.release(trn[0], "ao", ev)
                trn[0] += 1

            def emit_av(h, qb, b, u):
                ptb = u % 2
                p.wait("tensor", "vte", vte_after[h])
                pend = pending_post
                for ql in range(4):
                    rAp.acquire(p, "tensor", avk[0])
                    pr = avk[0] % 2
                    chunks = [c[0] for c in sched[qb]]
                    avv = None
                    for i, kc in enumerate(chunks):
                        if ql == 0:
                            p.wait("tensor", "pt", ptv_maps[u][kc])
                        avv = p.op("tensor", lambda e, ptb=ptb, kc=kc, ql=ql, b=b, pr=pr,
                             st=(i == 0), sp=(i == len(chunks) - 1): e.matmul(
                            pA[pr][:, 0:129], PT_sb[ptb][:, kc * 512 + ql * 128: kc * 512 + (ql + 1) * 128],
                            VT_sb[b][:, kc * 129: kc * 129 + 129],
                            start=st, stop=sp), "av" if i == len(chunks) - 1 else None)
                    if pend:
                        post_av(*pend.pop(0))
                    pend.append((h, qb, ql, avk[0], avv))
                    avk[0] += 1
                avdone[u] = p.val("av")

            wq_v = {}

            def emit_wq_dma(h):
                if h >= 1:
                    p.wait("gpsimd", "mmQ", relw[h - 1])
                wq_v[h] = p.dma("gpsimd",
                                qkvw[0][:].rearrange("p (w dc hd) -> p w dc hd", w=3, dc=16),
                                wqkv_t[h], "D_wq0")

            sck = [0]
            avk = [0]
            trn = [0]
            relw = {}
            relvs = {}
            vte_after = {}
            ms_after = {}
            av_after = {}
            emit_wq_dma(0)
            for h in range(H):
                b = h % 2
                if h >= 1:
                    p.wait("vector", "mmS", ms_after.get(h - 1, 0))
                    p.wait("vector", "trp", relvs.get(h - 1, 0))
                if h >= 2:
                    p.wait("vector", "mmS", ms_after.get(h - 2, 0))
                emit_qkv(h, b)
                if h + 1 < H:
                    emit_wq_dma(h + 1)
                for qb in range(2):
                    unit += 1
                    emit_scores(h, qb, b, unit)
                    emit_av(h, qb, b, unit)
                ms_after[h] = p.val("mmS")
                av_after[h] = p.val("av")
            while pending_post:
                post_av(*pending_post.pop(0))

        s_xnoth.close()
        p.barrier("b12")
        if debug and stage >= 1:
            p.dma("sync", dbg_attn[:], attn[:], "D_sd")

        R = sb("R", [128, 8 * D], F32)
        wT_sb = sb("wT_sb", [8, OWN], BF16)
        if stage >= 2:
         with ExitStack() as ph2a:
            sb2 = lambda *a, **k: ph2a.enter_context(nc.sbuf_tensor(*a, **k))
            pR = [ph2a.enter_context(nc.psum_tensor(f"pR{i}", [128, 512], F32)) for i in range(2)]
            pGm = ph2a.enter_context(nc.psum_tensor("pGm", [128, 512], F32))
            pW8 = ph2a.enter_context(nc.psum_tensor("pW8", [8, 512], F32))
            wo_sb = [sb2(f"wo{i}", [128, 16 * 512], BF16) for i in range(2)]
            sg_sb = sb2("sg", [128, 8 * 8], F32)
            tg_sb = sb2("tg", [128, 8 * 8 * 3], F32)
            rg_sb = sb2("rg", [128, 8 * 2], F32)
            w_sb = w_sbp
            rR = Rot(2)
            rwo = Rot(2)

            for tt in range(8):
                for dc in range(16):
                    p.op("tensor", lambda e, tt=tt, dc=dc: e.matmul(
                        pGm[:, tt * 8:(tt + 1) * 8],
                        xn_own[:, dc * OWN + tt * 128: dc * OWN + (tt + 1) * 128],
                        gw_sb[:, dc * 8:(dc + 1) * 8],
                        start=(dc == 0), stop=(dc == 15)),
                        "mmG" if (tt == 7 and dc == 15) else None, 1)
            gv = p.val("mmG")
            p.wait("scalar", "mmG", gv)
            p.op("scalar", lambda e: e.activation(
                sg_sb[:], pGm[:, 0:64], AF.Sigmoid, bias=0.0, scale=1.0), "sgm")
            sgv = p.val("sgm")
            p.wait("vector", "sgm", sgv)
            for tt in range(8):
                p.op("vector", lambda e, tt=tt: e.reduce_max(
                    rg_sb[:, tt * 2: tt * 2 + 1], sg_sb[:, tt * 8:(tt + 1) * 8], axis=AX.X), "g1")
            p.wait("vector", "g1", p.val("g1"))
            for tt in range(8):
                p.op("vector", lambda e, tt=tt: e.tensor_scalar(
                    tg_sb[:, tt * 8:(tt + 1) * 8], sg_sb[:, tt * 8:(tt + 1) * 8],
                    rg_sb[:, tt * 2: tt * 2 + 1], -1e30, ALU.is_ge, ALU.mult), "g1")
            p.wait("vector", "g1", p.val("g1"))
            for tt in range(8):
                p.op("vector", lambda e, tt=tt: e.tensor_add(
                    tg_sb[:, 64 + tt * 8: 64 + (tt + 1) * 8], tg_sb[:, tt * 8:(tt + 1) * 8],
                    sg_sb[:, tt * 8:(tt + 1) * 8]), "g1")
            p.wait("vector", "g1", p.val("g1"))
            for tt in range(8):
                p.op("vector", lambda e, tt=tt: e.reduce_max(
                    rg_sb[:, tt * 2 + 1: tt * 2 + 2], tg_sb[:, 64 + tt * 8: 64 + (tt + 1) * 8], axis=AX.X), "g1")
            p.wait("vector", "g1", p.val("g1"))
            for tt in range(8):
                p.op("vector", lambda e, tt=tt: e.tensor_scalar(
                    tg_sb[:, 128 + tt * 8: 128 + (tt + 1) * 8], sg_sb[:, tt * 8:(tt + 1) * 8],
                    rg_sb[:, tt * 2 + 1: tt * 2 + 2], None, ALU.is_ge, ALU.bypass), "g1")
            p.wait("vector", "g1", p.val("g1"))
            for tt in range(8):
                p.op("vector", lambda e, tt=tt: e.tensor_mul(
                    w_sb[:, tt * 8:(tt + 1) * 8], tg_sb[:, 128 + tt * 8: 128 + (tt + 1) * 8],
                    sg_sb[:, tt * 8:(tt + 1) * 8]), "g2")
            p.wait("vector", "g2", p.val("g2"))
            p.op("vector", lambda e: e.tensor_copy(wf_sbp[:], w_sbp[:]), "g2")
            g2v = p.val("g2")
            if debug:
                p.wait("sync", "g2", g2v)
                p.dma("sync", dbg_w[:], w_sb[:], "D_sd")
            p.wait("tensor", "g2", g2v)
            for half in range(2):
                for i in range(4):
                    tt = half * 4 + i
                    p.op("tensor", lambda e, tt=tt, i=i: e.matmul(
                        pW8[0:8, i * 128:(i + 1) * 128], w_sb[:, tt * 8:(tt + 1) * 8],
                        ident[:], start=True, stop=True),
                        "mmW" if i == 3 else None, 1)
                wv = p.val("mmW")
                p.wait("scalar", "mmW", wv)
                sv = p.op("scalar", lambda e, half=half: e.activation(
                    wT_sb[0:8, half * 512:(half + 1) * 512], pW8[0:8, 0:512],
                    AF.Copy, bias=0.0, scale=1.0), "wTe")
                p.wait("tensor", "wTe", sv)

            ork = [0]
            wo_v = {}

            def emit_wo_dma(dc):
                if dc >= 2:
                    p.wait("gpsimd", "mmO", 8 * (dc - 1))
                wo_v[dc] = p.dma("gpsimd",
                                 wo_sb[dc % 2][:].rearrange("p (dcin n) -> p dcin n", dcin=16),
                                 w_outT[dc], f"D_wo{dc % 2}")

            emit_wo_dma(0)
            emit_wo_dma(1)
            for dc in range(4):
                if dc + 2 < 4:
                    emit_wo_dma(dc + 2)
                p.wait("tensor", f"D_wo{dc % 2}", wo_v[dc])
                for tt in range(8):
                    rR.acquire(p, "tensor", ork[0])
                    pr = ork[0] % 2
                    p.op("tensor", lambda e, dc=dc, pr=pr: e.matmul(
                        pR[pr][:], ones1[0:1, :], bor_sb[0:1, dc * 512:(dc + 1) * 512],
                        start=True, stop=False), None)
                    for dcin in range(16):
                        p.op("tensor", lambda e, dcin=dcin, tt=tt, dc=dc, pr=pr: e.matmul(
                            pR[pr][:], attn[:, dcin * OWN + tt * 128: dcin * OWN + (tt + 1) * 128],
                            wo_sb[dc % 2][:, dcin * 512:(dcin + 1) * 512],
                            start=False, stop=(dcin == 15)),
                            "mmO" if dcin == 15 else None, 1)
                    ov = p.val("mmO")
                    p.wait("scalar", "mmO", ov)
                    ev = p.op("scalar", lambda e, tt=tt, dc=dc, pr=pr: e.activation(
                        R[:, tt * D + dc * 512: tt * D + (dc + 1) * 512], pR[pr][:],
                        AF.Copy, bias=0.0, scale=1.0), "rop")
                    rR.release(ork[0], "rop", ev)
                    ork[0] += 1

        s_attn.close()
        p.barrier("b2ab")

        CAP = 384
        if stage >= 3:
         with ExitStack() as ph2o:
            sbo = lambda *a, **k: ph2o.enter_context(nc.sbuf_tensor(*a, **k))
            w13 = [sbo(f"w13_{i}", [128, 2 * 16 * 128], BF16) for i in range(2)]
            w2s = sbo("w2s", [128, 8 * D], BF16)
            ts1 = [sbo(f"ts1{i}", [128, 512], F32) for i in range(2)]
            mm3_after = {}
            mm2_after = {}
            w13_v = {}
            w2_v = {}

            def w13_src(ee, hc, which):
                if ee == 0:
                    return (ws1h if which == 0 else ws3h)[hc]
                e = ee - 1
                return (w1h if which == 0 else w3h)[e, hc]

            def w2_src(ee):
                if ee == 0:
                    return ws2h[:, :, :]
                return w2h[ee - 1]

            def emit_w13_dma(ee, hc):
                idx = ee * 8 + hc
                if idx >= 2:
                    p.wait("gpsimd", "mm3", mm3_after[idx - 2])
                for which in range(2):
                    v = p.dma("gpsimd",
                              w13[idx % 2][:, which * 2048:(which + 1) * 2048].rearrange(
                                  "p (dc h) -> p dc h", dc=16),
                              w13_src(ee, hc, which),
                              f"D_w13{idx % 2}")
                w13_v[idx] = v

            def emit_w2_chunk(ee, c):
                if ee >= 1:
                    p.wait("gpsimd", "mm2", mm2_after[ee - 1])
                w2_v[ee] = p.dma("gpsimd",
                                 w2s[:, c * 2 * D:(c + 1) * 2 * D].rearrange(
                                     "p (hc n) -> p hc n", hc=2),
                                 w2_src(ee)[:, c * 2:(c + 1) * 2, :],
                                 "D_w2")

            emit_w13_dma(0, 0)
            emit_w13_dma(0, 1)

            with ExitStack() as ph2s:
                sbs = lambda *a, **k: ph2s.enter_context(nc.sbuf_tensor(*a, **k))
                pG1 = [ph2s.enter_context(nc.psum_tensor(f"pG1{i}", [128, 512], F32)) for i in range(2)]
                pG3 = [ph2s.enter_context(nc.psum_tensor(f"pG3{i}", [128, 512], F32)) for i in range(2)]
                pR2 = [ph2s.enter_context(nc.psum_tensor(f"pR2{i}", [128, 512], F32)) for i in range(2)]
                Hsh = sbs("Hsh", [128, 8 * OWN], BF16)
                rG1 = Rot(2)
                rG3 = Rot(2)
                rR2 = Rot(2)
                rTmp = Rot(2)
                gck = [0]
                w2ck = [0]
                for hc in range(8):
                    idx = hc
                    wb13 = idx % 2
                    p.wait("tensor", f"D_w13{idx % 2}", w13_v[idx])
                    for tc in range(2):
                        rG1.acquire(p, "tensor", gck[0])
                        rG3.acquire(p, "tensor", gck[0])
                        pr = gck[0] % 2
                        for dc in range(16):
                            p.op("tensor", lambda e, wb13=wb13, dc=dc, tc=tc, pr=pr: e.matmul(
                                pG1[pr][:], w13[wb13][:, dc * 128:(dc + 1) * 128],
                                xn_own[:, dc * OWN + tc * 512: dc * OWN + (tc + 1) * 512],
                                start=(dc == 0), stop=(dc == 15)),
                                "mm1" if dc == 15 else None, 1)
                        m1v = p.val("mm1")
                        for dc in range(16):
                            p.op("tensor", lambda e, wb13=wb13, dc=dc, tc=tc, pr=pr: e.matmul(
                                pG3[pr][:], w13[wb13][:, 2048 + dc * 128:2048 + (dc + 1) * 128],
                                xn_own[:, dc * OWN + tc * 512: dc * OWN + (tc + 1) * 512],
                                start=(dc == 0), stop=(dc == 15)),
                                "mm3" if dc == 15 else None, 1)
                        m3v = p.val("mm3")
                        rTmp.acquire(p, "scalar", gck[0])
                        tb = gck[0] % 2
                        p.wait("scalar", "mm1", m1v)
                        s1v = p.op("scalar", lambda e, pr=pr, tb=tb: e.activation(
                            ts1[tb][:], pG1[pr][:], AF.Silu, bias=0.0, scale=1.0), "si")
                        rG1.release(gck[0], "si", s1v)
                        p.wait("vector", "mm3", m3v)
                        p.wait("vector", "si", s1v)
                        hv = p.op("vector", lambda e, pr=pr, tb=tb, hc=hc, tc=tc: e.tensor_mul(
                            Hsh[:, hc * OWN + tc * 512: hc * OWN + (tc + 1) * 512],
                            ts1[tb][:], pG3[pr][:]), "hh")
                        rG3.release(gck[0], "hh", hv)
                        rTmp.release(gck[0], "hh", hv)
                        gck[0] += 1
                    mm3_after[idx] = p.val("mm3")
                    if idx + 2 < 72:
                        emit_w13_dma(*divmod(idx + 2, 8))
                    if hc % 2 == 0:
                        emit_w2_chunk(0, hc // 2)
                hhv = p.val("hh")
                p.wait("tensor", "D_w2", w2_v[0])
                p.wait("tensor", "hh", hhv)
                for tt in range(8):
                    for dc in range(4):
                        rR2.acquire(p, "tensor", w2ck[0])
                        pr = w2ck[0] % 2
                        for hc in range(8):
                            p.op("tensor", lambda e, hc=hc, tt=tt, dc=dc, pr=pr: e.matmul(
                                pR2[pr][:], Hsh[:, hc * OWN + tt * 128: hc * OWN + (tt + 1) * 128],
                                w2s[:, hc * D + dc * 512: hc * D + (dc + 1) * 512],
                                start=(hc == 0), stop=(hc == 7)),
                                "mm2" if hc == 7 else None, 1)
                        m2v = p.val("mm2")
                        p.wait("vector", "mm2", m2v)
                        av = p.op("vector", lambda e, tt=tt, dc=dc, pr=pr: e.tensor_add(
                            R[:, tt * D + dc * 512: tt * D + (dc + 1) * 512],
                            R[:, tt * D + dc * 512: tt * D + (dc + 1) * 512], pR2[pr][:]), "rad")
                        rR2.release(w2ck[0], "rad", av)
                        w2ck[0] += 1
                mm2_after[0] = p.val("mm2")
                for c in range(4):
                    emit_w2_chunk(1, c)

            s_xnF.close()
            p.barrier("b2bc")

            iota_cap_d = nc.inline_tensor(
                np.tile(np.arange(1, CAP + 1, dtype=np.float32), (128, 1)), name="iotacapc")
            identf_d = nc.inline_tensor(np.eye(8, dtype=np.float32), name="identfc")
            with ExitStack() as ph2e:
                sbe = lambda *a, **k: ph2e.enter_context(nc.sbuf_tensor(*a, **k))
                pG1e = [ph2e.enter_context(nc.psum_tensor(f"pG1eb{i}", [128, 512], F32)) for i in range(2)]
                pXS = pG1e
                pG3e = [ph2e.enter_context(nc.psum_tensor(f"pG3eb{i}", [128, 512], F32)) for i in range(2)]
                pT2 = [ph2e.enter_context(nc.psum_tensor(f"pT2{i}", [128, 1024], BF16)) for i in range(2)]
                pD = [ph2e.enter_context(nc.psum_tensor(f"pD{i}", [128, 512], F32)) for i in range(2)]
                xs_sb = sbe("xs_sb", [128, 16 * CAP], BF16)
                Hsp = sbe("Hsp", [128, 8 * CAP], BF16)
                rout = sbe("rout", [128, 3 * D], BF16)
                P_sb = sbe("P_sb", [128, 8 * CAP], BF16)
                PTs = [sbe(f"PTs{i}", [128, 3 * OWN], BF16) for i in range(2)]
                posm = sbe("posm", [128, OWN], F32)
                P01 = [sbe(f"P01_{i}", [128, CAP], BF16) for i in range(2)]
                poscol = sbe("poscol", [128, 8 * 8], F32)
                mcol_sb = sbe("mcol_sb", [128, 8 * 8], F32)
                iota_cap = sbe("iota_cap", [128, CAP], F32)
                identf = sbe("identf", [8, 8], F32)
                p.dma("sync", iota_cap[:], iota_cap_d[:, :], "D_c2")
                p.dma("sync", identf[:], identf_d[:, :], "D_c2")
                ic_v = p.val("D_c2")

                wtv = p.val("wTe")
                p.wait("vector", "wTe", wtv)
                mv = p.op("vector", lambda e: e.tensor_scalar(
                    posm[32:40, :], wT_sb[0:8, :], 0.0, None, ALU.is_gt, ALU.bypass), "pv")
                p.wait("vector", "pv", mv)
                cv3 = p.op("vector", lambda e: e.tensor_tensor_scan(
                    posm[0:8, :], posm[32:40, :], posm[32:40, :], 0.0, ALU.add, ALU.bypass), "pv")
                p.op("vector", lambda e: e.tensor_scalar(
                    mcol_sb[:], wf_sbp[:], 0.0, None, ALU.is_gt, ALU.bypass), "pv")
                p.wait("tensor", "pv", p.val("pv"))
                p.wait("tensor", "D_c2", ic_v)
                for tc in range(8):
                    if tc >= 1:
                        p.wait("tensor", "pce", tc)
                    tv = p.op("tensor", lambda e, tc=tc: e.transpose(
                        pXS[0][:, 0:8], posm[0:8, tc * 128:(tc + 1) * 128], identf[:]), "ptr")
                    p.wait("vector", "ptr", tv)
                    pcv = p.op("vector", lambda e, tc=tc: e.tensor_copy(
                        poscol[:, tc * 8:(tc + 1) * 8], pXS[0][:, 0:8]), "pce")
                p.wait("tensor", "pce", p.val("pce"))
                p.wait("tensor", "D_c2", ic_v)

                rXS = Rot(2)
                rG1e_shared = rXS
                rPTrot = Rot(2)
                ptk = [0]
                poolk = [0]
                rG1e = Rot(2)
                rG3e = Rot(2)
                rD = Rot(2)
                rTmpE = Rot(2)
                xsk = [0]
                gck = [0]
                dk = [0]
                mmxs_after = {}
                mm2b_after = {}
                mmsc_after = {}

                pb_after = {}
                ptb_after = {}

                def emit_ptbuild(ex):
                    ptbuf = ex % 2
                    if ex >= 2:
                        p.wait("vector", "mmSC", mmsc_after[ex - 2])
                    for tc in range(8):
                        if ptk[0] >= 2:
                            p.wait("vector", "pttr", p.val("pttr") - 1)
                        pv01 = p.op("vector", lambda e, tc=tc, ex=ex, pb2=tc % 2: e.tensor_scalar(
                            P01[pb2][:], iota_cap[:],
                            poscol[:, tc * 8 + ex: tc * 8 + ex + 1],
                            mcol_sb[:, tc * 8 + ex: tc * 8 + ex + 1],
                            ALU.is_equal, ALU.mult), "p01")
                        p.wait("tensor", "p01", pv01)
                        for sc in range(3):
                            rPTrot.acquire(p, "tensor", ptk[0])
                            tv2 = p.op("tensor", lambda e, sc=sc, r=ptk[0] % 2, pb2=tc % 2: e.transpose(
                                pT2[r][:, 0:128], P01[pb2][:, sc * 128:(sc + 1) * 128], ident[:]), "pttr")
                            p.wait("scalar", "pttr", tv2)
                            cpv = p.op("scalar", lambda e, sc=sc, tc=tc, r=ptk[0] % 2, ptbuf=ptbuf: e.activation(
                                PTs[ptbuf][:, sc * OWN + tc * 128:(sc * OWN + (tc + 1) * 128)],
                                pT2[r][:, 0:128], AF.Copy, bias=0.0, scale=1.0), "ptb")
                            rPTrot.release(ptk[0], "ptb", cpv)
                            ptk[0] += 1
                    ptb_after[ex] = p.val("ptb")

                def emit_pbuild(ex):
                    if ex >= 1:
                        p.wait("vector", "mmXS", mmxs_after[ex - 1])
                    p.wait("vector", "pce", p.val("pce"))
                    for tc in range(8):
                        p.op("vector", lambda e, tc=tc, ex=ex: e.tensor_scalar(
                            P_sb[:, tc * CAP:(tc + 1) * CAP], iota_cap[:],
                            poscol[:, tc * 8 + ex: tc * 8 + ex + 1],
                            wf_sbp[:, 8 * tc + ex: 8 * tc + ex + 1],
                            ALU.is_equal, ALU.mult), "pb")
                    pb_after[ex] = p.val("pb")

                emit_pbuild(0)
                emit_ptbuild(0)
                for ex in range(8):
                    ee = ex + 1
                    p.wait("tensor", "pb", pb_after[ex])
                    if ex >= 1:
                        p.wait("vector", "mm1", p.val("mm1"))
                    for dc in range(16):
                        rXS.acquire(p, "tensor", poolk[0])
                        prx = poolk[0] % 2
                        for tc in range(8):
                            p.op("tensor", lambda e, dc=dc, tc=tc, prx=prx: e.matmul(
                                pXS[prx][:, 0:CAP], xn_tok[:, tc * D + dc * 128: tc * D + (dc + 1) * 128],
                                P_sb[:, tc * CAP:(tc + 1) * CAP],
                                start=(tc == 0), stop=(tc == 7)),
                                "mmXS" if tc == 7 else None, 1)
                        xv = p.val("mmXS")
                        p.wait("scalar", "mmXS", xv)
                        ev = p.op("scalar", lambda e, dc=dc, prx=prx: e.activation(
                            xs_sb[:, dc * CAP:(dc + 1) * CAP], pXS[prx][:, 0:CAP],
                            AF.Copy, bias=0.0, scale=1.0), "xse")
                        rXS.release(poolk[0], "xse", ev)
                        poolk[0] += 1
                    mmxs_after[ex] = p.val("mmXS")
                    xse_base = p.val("xse") - 16
                    ptbv = ptb_after[ex]
                    for hc in range(8):
                        idx = ee * 8 + hc
                        wb13 = idx % 2
                        p.wait("tensor", f"D_w13{idx % 2}", w13_v[idx])
                        rXS.acquire(p, "tensor", poolk[0])
                        rG3e.acquire(p, "tensor", gck[0])
                        prg = poolk[0] % 2
                        prg3 = gck[0] % 2
                        for dc in range(16):
                            if hc == 0:
                                p.wait("tensor", "xse", xse_base + dc + 1)
                            p.op("tensor", lambda e, wb13=wb13, dc=dc, prg=prg: e.matmul(
                                pG1e[prg][:, 0:CAP], w13[wb13][:, dc * 128:(dc + 1) * 128],
                                xs_sb[:, dc * CAP:(dc + 1) * CAP],
                                start=(dc == 0), stop=(dc == 15)),
                                "mm1" if dc == 15 else None, 1)
                        m1v = p.val("mm1")
                        for dc in range(16):
                            p.op("tensor", lambda e, wb13=wb13, dc=dc, prg3=prg3: e.matmul(
                                pG3e[prg3][:, 0:CAP], w13[wb13][:, 2048 + dc * 128:2048 + (dc + 1) * 128],
                                xs_sb[:, dc * CAP:(dc + 1) * CAP],
                                start=(dc == 0), stop=(dc == 15)),
                                "mm3" if dc == 15 else None, 1)
                        m3v = p.val("mm3")
                        rTmpE.acquire(p, "scalar", gck[0])
                        tb = gck[0] % 2
                        p.wait("scalar", "mm1", m1v)
                        if ex >= 1 and hc == 0:
                            p.wait("scalar", "mm2", mm2b_after[ex - 1])
                        s1v = p.op("scalar", lambda e, prg=prg, tb=tb: e.activation(
                            ts1[tb][:, 0:CAP], pG1e[prg][:, 0:CAP], AF.Silu, bias=0.0, scale=1.0), "si")
                        rXS.release(poolk[0], "si", s1v)
                        p.wait("vector", "mm3", m3v)
                        p.wait("vector", "si", s1v)
                        if ex >= 1 and hc == 0:
                            p.wait("vector", "mm2", mm2b_after[ex - 1])
                        hv = p.op("vector", lambda e, prg3=prg3, tb=tb, hc=hc: e.tensor_mul(
                            Hsp[:, hc * CAP:(hc + 1) * CAP],
                            ts1[tb][:, 0:CAP], pG3e[prg3][:, 0:CAP]), "hh")
                        rG3e.release(gck[0], "hh", hv)
                        rTmpE.release(gck[0], "hh", hv)
                        gck[0] += 1
                        poolk[0] += 1
                        mm3_after[idx] = p.val("mm3")
                        if idx + 2 < 72:
                            emit_w13_dma(*divmod(idx + 2, 8))

                    hhv = p.val("hh")
                    if ex + 1 < 8:
                        emit_pbuild(ex + 1)
                        emit_ptbuild(ex + 1)
                    p.wait("tensor", "D_w2", w2_v[ee])
                    p.wait("tensor", "hh", hhv)
                    if ex >= 1:
                        p.wait("scalar", "mmSC", mmsc_after[ex - 1])
                    for sc in range(3):
                        for dc in range(4):
                            rD.acquire(p, "tensor", dk[0])
                            prd = dk[0] % 2
                            for hc in range(8):
                                p.op("tensor", lambda e, hc=hc, sc=sc, dc=dc, prd=prd: e.matmul(
                                    pD[prd][:], Hsp[:, hc * CAP + sc * 128: hc * CAP + sc * 128 + 128],
                                    w2s[:, hc * D + dc * 512: hc * D + (dc + 1) * 512],
                                    start=(hc == 0), stop=(hc == 7)),
                                    "mm2" if hc == 7 else None, 1)
                            m2v = p.val("mm2")
                            p.wait("scalar", "mm2", m2v)
                            rv = p.op("scalar", lambda e, sc=sc, dc=dc, prd=prd: e.activation(
                                rout[:, sc * D + dc * 512: sc * D + (dc + 1) * 512], pD[prd][:],
                                AF.Copy, bias=0.0, scale=1.0), "rte")
                            rD.release(dk[0], "rte", rv)
                            dk[0] += 1
                    mm2b_after[ex] = p.val("mm2")
                    mm2_after[ee] = p.val("mm2")
                    if ee + 1 <= 8:
                        for c in range(4):
                            emit_w2_chunk(ee + 1, c)
                    rtev = p.val("rte")
                    p.wait("tensor", "rte", rtev)
                    p.wait("tensor", "ptb", ptbv)
                    for tt in range(8):
                        for dc in range(4):
                            rD.acquire(p, "tensor", dk[0])
                            prd = dk[0] % 2
                            for sc in range(3):
                                p.op("tensor", lambda e, sc=sc, tt=tt, dc=dc, prd=prd, ptbuf=ex % 2: e.matmul(
                                    pD[prd][:], PTs[ptbuf][:, sc * OWN + tt * 128: sc * OWN + (tt + 1) * 128],
                                    rout[:, sc * D + dc * 512: sc * D + (dc + 1) * 512],
                                    start=(sc == 0), stop=(sc == 2)),
                                    "mmSC" if sc == 2 else None, 1)
                            scv = p.val("mmSC")
                            p.wait("vector", "mmSC", scv)
                            av = p.op("vector", lambda e, tt=tt, dc=dc, prd=prd: e.tensor_add(
                                R[:, tt * D + dc * 512: tt * D + (dc + 1) * 512],
                                R[:, tt * D + dc * 512: tt * D + (dc + 1) * 512], pD[prd][:]), "rad")
                            rD.release(dk[0], "rad", av)
                            dk[0] += 1
                    mmsc_after[ex] = p.val("mmSC")

        if debug and stage >= 2:
            p.wait("sync", "rad", p.val("rad"))
            p.dma("sync", dbg_R[:], R[:], "D_sd")
        p.barrier("b2f")

        if stage < 2:
            p.op("vector", lambda e: e.memset(R[:], 0.0), "rad")
        with ExitStack() as phf:
            sbf = lambda *a, **k: phf.enter_context(nc.sbuf_tensor(*a, **k))
            xrl = [sbf(f"xrl{i}", [128, D], F32) for i in range(2)]
            osb = [sbf(f"osb{i}", [128, D], F32) for i in range(2)]
            rX = Rot(2)
            rO = Rot(2)
            p.wait("vector", "rad", p.val("rad"))
            for tt in range(8):
                rX.acquire(p, "sync", tt)
                xv = p.dma("sync", xrl[tt % 2][:], x_own[tt * 128:(tt + 1) * 128, :], f"D_xr{tt % 2}")
                p.wait("vector", f"D_xr{tt % 2}", xv)
                rO.acquire(p, "vector", tt)
                fv = p.op("vector", lambda e, tt=tt: e.tensor_add(
                    osb[tt % 2][:], R[:, tt * D:(tt + 1) * D], xrl[tt % 2][:]), "fin")
                rX.release(tt, "fin", fv)
                p.wait("sync", "fin", fv)
                sv = p.dma("sync", out_c[tt * 128:(tt + 1) * 128, :], osb[tt % 2][:], f"D_st{tt % 2}")
                rO.release(tt, f"D_st{tt % 2}", sv)
            p.wait("sync", "D_st0", p.val("D_st0"))
            p.wait("sync", "D_st1", p.val("D_st1"))
            if debug:
                p.wait("sync", "D_sd", p.val("D_sd"))

        with nc.Block() as block:
            p.finalize(block)
    return nc


_NC_CACHE = {}


def get_nc(debug=False):
    key = bool(debug)
    if key not in _NC_CACHE:
        _NC_CACHE[key] = build(debug)
    return _NC_CACHE[key]


def prep_inputs(L0_attn_x, L0_attn_w_qkv, L0_attn_b_qkv, L0_attn_w_out, L0_attn_b_out,
                L0_ffn_gate_w, L0_ffn_w1, L0_ffn_w2, L0_ffn_w3,
                L0_ffn_ws1, L0_ffn_ws2, L0_ffn_ws3):
    x = np.ascontiguousarray(np.asarray(L0_attn_x, dtype=np.float32))
    w_qkv = np.asarray(L0_attn_w_qkv, dtype=np.float32)
    b_qkv = np.ascontiguousarray(np.asarray(L0_attn_b_qkv, dtype=np.float32))
    w_out = np.asarray(L0_attn_w_out, dtype=np.float32)
    b_outv = np.ascontiguousarray(np.asarray(L0_attn_b_out, dtype=np.float32))
    gate_w = np.asarray(L0_ffn_gate_w, dtype=np.float32)
    w1 = np.asarray(L0_ffn_w1, dtype=np.float32)
    w2 = np.asarray(L0_ffn_w2, dtype=np.float32)
    w3 = np.asarray(L0_ffn_w3, dtype=np.float32)
    ws1 = np.asarray(L0_ffn_ws1, dtype=np.float32)
    ws2 = np.asarray(L0_ffn_ws2, dtype=np.float32)
    ws3 = np.asarray(L0_ffn_ws3, dtype=np.float32)

    shared = {
        "wqkv_t": np.ascontiguousarray(
            w_qkv.reshape(3, H, HD, 16, 128).transpose(1, 4, 0, 3, 2)),
        "bq_col": np.ascontiguousarray(b_qkv.reshape(3, H, HD).transpose(2, 0, 1).reshape(128, 48)),
        "b_qkv": b_qkv,
        "w_outT": np.ascontiguousarray(w_out.T.reshape(16, 128, 4, 512).transpose(2, 1, 0, 3)),
        "b_out": b_outv,
        "gate_wT": np.ascontiguousarray(gate_w.T),
        "w1h": np.ascontiguousarray(
            w1.transpose(0, 2, 1).reshape(NE, 16, 128, 8, 128).transpose(0, 3, 2, 1, 4)),
        "w3h": np.ascontiguousarray(
            w3.transpose(0, 2, 1).reshape(NE, 16, 128, 8, 128).transpose(0, 3, 2, 1, 4)),
        "w2h": np.ascontiguousarray(
            w2.transpose(0, 2, 1).reshape(NE, 8, 128, D).transpose(0, 2, 1, 3)),
        "ws1h": np.ascontiguousarray(
            ws1.T.reshape(16, 128, 8, 128).transpose(2, 1, 0, 3)),
        "ws3h": np.ascontiguousarray(
            ws3.T.reshape(16, 128, 8, 128).transpose(2, 1, 0, 3)),
        "ws2h": np.ascontiguousarray(ws2.T.reshape(8, 128, D).transpose(1, 0, 2)),
    }
    in_maps = []
    for c in range(8):
        b, j = divmod(c, 2)
        m = dict(shared)
        m["x_own"] = np.ascontiguousarray(x[b, j * OWN:(j + 1) * OWN, :])
        m["x_oth"] = np.ascontiguousarray(x[b, (1 - j) * OWN:(2 - j) * OWN, :])
        bias = np.full((128, 18), EXPSH, dtype=np.float32)
        if j == 0:
            bias[:, 0:16] = -1e30
        bias[:, 16] = EXPSH
        bias[:, 17] = EXPSH - BIGM * SCL
        m["expb"] = bias
        in_maps.append(m)
    return in_maps


def kernel(L0_attn_x, L0_attn_w_qkv, L0_attn_b_qkv, L0_attn_w_out, L0_attn_b_out,
           L0_ffn_gate_w, L0_ffn_w1, L0_ffn_w2, L0_ffn_w3,
           L0_ffn_ws1, L0_ffn_ws2, L0_ffn_ws3):
    in_maps = prep_inputs(L0_attn_x, L0_attn_w_qkv, L0_attn_b_qkv, L0_attn_w_out,
                          L0_attn_b_out, L0_ffn_gate_w, L0_ffn_w1, L0_ffn_w2,
                          L0_ffn_w3, L0_ffn_ws1, L0_ffn_ws2, L0_ffn_ws3)
    nc = get_nc(debug=False)
    res = run_bass_kernel_spmd(nc, in_maps, core_ids=list(range(8)))
    out = np.empty((B, S, D), dtype=np.float32)
    for c in range(8):
        b, j = divmod(c, 2)
        out[b, j * OWN:(j + 1) * OWN, :] = res.results[c]["out"]
    return out


# version 20 (final)
# speedup vs baseline: 1.0509x; 1.0210x over previous
import math
import numpy as np
import ml_dtypes
from contextlib import ExitStack

import concourse.bass as bass
import concourse.mybir as mybir
from concourse.bass_utils import run_bass_kernel_spmd

F32 = mybir.dt.float32
BF16 = mybir.dt.bfloat16
AF = mybir.ActivationFunctionType
ALU = mybir.AluOpType
AX = mybir.AxisListType

D = 2048
H = 16
HD = 128
S = 2048
B = 4
OWN = 1024
NHID = 1024
NE = 8
SCL = 1.0 / math.sqrt(HD)
BIGM = 600.0
EXPSH = -4.0
EPS = 1e-5

ENGINES = ("tensor", "vector", "scalar", "gpsimd", "sync")


class Prog:

    def __init__(self, nc, ctx):
        self.nc = nc
        self.ctx = ctx
        self.ops = {e: [] for e in ENGINES}
        self.semh = {}
        self.semc = {}

    def sem(self, name):
        if name not in self.semh:
            self.semh[name] = self.ctx.enter_context(self.nc.semaphore(name))
            self.semc[name] = 0
        return self.semh[name]

    def val(self, name):
        self.sem(name)
        return self.semc[name]

    def op(self, eng, f, sem=None, n=1):
        if sem is None:
            self.ops[eng].append(f)
            return None
        h = self.sem(sem)
        self.semc[sem] += n
        self.ops[eng].append(lambda e, f=f, h=h, n=n: f(e).then_inc(h, n))
        return self.semc[sem]

    def wait(self, eng, sem, v=None):
        h = self.sem(sem)
        if v is None:
            v = self.semc[sem]
        if v > 0:
            self.ops[eng].append(lambda e, h=h, v=v: e.wait_ge(h, v))

    def dma(self, eng, out, in_, sem):
        return self.op(eng, lambda e, o=out, i=in_: e.dma_start(out=o, in_=i), sem, 16)

    def barrier(self, tag):
        vals = [(n, self.semc[n]) for n in list(self.semc) if self.semc[n] > 0]
        for eng in ENGINES:
            for n, v in vals:
                self.wait(eng, n, v)

    def finalize(self, block):
        def run(engine, ops):
            for f in ops:
                f(engine)
        block.tensor(lambda e: run(e, self.ops["tensor"]))
        block.vector(lambda e: run(e, self.ops["vector"]))
        block.scalar(lambda e: run(e, self.ops["scalar"]))
        block.gpsimd(lambda e: run(e, self.ops["gpsimd"]))
        block.sync(lambda e: run(e, self.ops["sync"]))


class Rot:

    def __init__(self, n):
        self.n = n
        self.free = [None] * n

    def acquire(self, p, eng, i):
        w = self.free[i % self.n]
        if w is not None:
            p.wait(eng, w[0], w[1])

    def release(self, i, sem, val):
        self.free[i % self.n] = (sem, val)


def build(debug=False, stage=3):
    nc = bass.Bass("TRN2", target_bir_lowering=False, debug=False)

    x_own = nc.dram_tensor("x_own", [OWN, D], F32, kind="ExternalInput").ap()
    x_oth = nc.dram_tensor("x_oth", [OWN, D], F32, kind="ExternalInput").ap()
    expb = nc.dram_tensor("expb", [128, 18], F32, kind="ExternalInput").ap()
    wqkv_t = nc.dram_tensor("wqkv_t", [H, 128, 3, 16, HD], F32, kind="ExternalInput").ap()
    bq_col = nc.dram_tensor("bq_col", [128, 48], F32, kind="ExternalInput").ap()
    b_qkv = nc.dram_tensor("b_qkv", [3 * D], F32, kind="ExternalInput").ap()
    w_outT = nc.dram_tensor("w_outT", [4, 128, 16, 512], F32, kind="ExternalInput").ap()
    b_out = nc.dram_tensor("b_out", [D], F32, kind="ExternalInput").ap()
    gate_wT = nc.dram_tensor("gate_wT", [D, NE], F32, kind="ExternalInput").ap()
    w1h = nc.dram_tensor("w1h", [NE, 8, 128, 16, 128], F32, kind="ExternalInput").ap()
    w3h = nc.dram_tensor("w3h", [NE, 8, 128, 16, 128], F32, kind="ExternalInput").ap()
    w2h = nc.dram_tensor("w2h", [NE, 128, 8, D], F32, kind="ExternalInput").ap()
    ws1h = nc.dram_tensor("ws1h", [8, 128, 16, 128], F32, kind="ExternalInput").ap()
    ws3h = nc.dram_tensor("ws3h", [8, 128, 16, 128], F32, kind="ExternalInput").ap()
    ws2h = nc.dram_tensor("ws2h", [128, 8, D], F32, kind="ExternalInput").ap()
    out_c = nc.dram_tensor("out", [OWN, D], F32, kind="ExternalOutput").ap()
    if debug:
        dbg_xn = nc.dram_tensor("dbg_xn", [128, 2 * 16 * OWN], BF16, kind="ExternalOutput").ap()
        dbg_attn = nc.dram_tensor("dbg_attn", [128, 16 * OWN], BF16, kind="ExternalOutput").ap()
        dbg_w = nc.dram_tensor("dbg_w", [128, 64], BF16, kind="ExternalOutput").ap()
        dbg_R = nc.dram_tensor("dbg_R", [128, 8 * D], F32, kind="ExternalOutput").ap()

    bf = ml_dtypes.bfloat16
    ident_d = nc.inline_tensor(np.eye(128, dtype=np.float32).astype(bf), name="identc").ap()
    ibig_d = nc.inline_tensor((BIGM * np.eye(128, dtype=np.float32)).astype(bf), name="ibigc").ap()
    tri = np.zeros((4, 128, 512), dtype=np.float32)
    for j2 in range(4):
        for k in range(128):
            tri[j2, k, j2 * 128 + k:] = 1.0
    tri_d = nc.inline_tensor(tri.astype(bf), name="tric").ap()
    selm = np.zeros((8, 8 * 128), dtype=np.float32)
    for e in range(8):
        selm[e, e * 128:(e + 1) * 128] = 1.0
    sel_d = nc.inline_tensor(selm.astype(bf), name="selc").ap()
    ones_d = nc.inline_tensor(np.ones((1, 128), dtype=np.float32).astype(bf), name="ones1c").ap()

    ctx = ExitStack()
    with ctx:
        p = Prog(nc, ctx)
        sb = lambda *a, **k: ctx.enter_context(nc.sbuf_tensor(*a, **k))
        xn_tok = sb("xn_tok", [128, 8 * D], BF16)
        w_sbp = sb("w_sbp", [128, 8 * 8], BF16)
        wf_sbp = sb("wf_sbp", [128, 8 * 8], F32)
        ident = sb("ident", [128, 128], BF16)
        ibig = sb("ibig", [128, 128], BF16)
        ones1 = sb("ones1", [1, 128], BF16)
        bqc_sb = sb("bqc_sb", [128, 48], F32)
        gw_sb = sb("gw_sb", [128, 16 * 8], BF16)
        expb_sb = sb("expb_sb", [128, 18], F32)
        s_xnF = ExitStack()
        ctx.enter_context(s_xnF)
        xn_own = s_xnF.enter_context(nc.sbuf_tensor("xn_own", [128, 16 * OWN], BF16, side="right"))
        s_attn = ExitStack()
        ctx.enter_context(s_attn)
        attn = s_attn.enter_context(nc.sbuf_tensor("attn", [128, 16 * OWN], BF16, side="right"))
        bor_sb = s_attn.enter_context(nc.sbuf_tensor("bor_sb", [1, D], BF16, side="right"))
        s_xnoth = ExitStack()
        ctx.enter_context(s_xnoth)
        xn_oth = s_xnoth.enter_context(nc.sbuf_tensor("xn_oth", [128, 16 * OWN], BF16, side="right"))
        tri_sb = s_xnoth.enter_context(nc.sbuf_tensor("tri_sb", [128, 4 * 512], BF16))

        p.dma("sync", ident[:], ident_d[:, :], "D_c")
        p.dma("sync", ibig[:], ibig_d[:, :], "D_c")
        p.dma("sync", tri_sb[:].rearrange("p (m q) -> p m q", m=4),
              tri_d[:, :, :].rearrange("m p q -> p m q"), "D_c")
        p.dma("sync", ones1[:], ones_d[:, :], "D_c")
        p.dma("sync", bqc_sb[:], bq_col[:, :], "D_c")
        p.dma("sync", expb_sb[:], expb[:, :], "D_c")
        p.dma("gpsimd", bor_sb[:], b_out[:].rearrange("(a n) -> a n", a=1), "D_cg")
        p.dma("gpsimd", gw_sb[:].rearrange("p (c e) -> p c e", c=16),
              gate_wT.rearrange("(c p) e -> p c e", p=128), "D_cg")
        cst_v = p.val("D_c")
        cstg_v = p.val("D_cg")

        with ExitStack() as ph0:
            sb0 = lambda *a, **k: ph0.enter_context(nc.sbuf_tensor(*a, **k))
            ps0 = [ph0.enter_context(nc.psum_tensor(f"p0_{i}", [128, 1024], BF16)) for i in range(6)]
            xt = [sb0(f"xt{i}", [128, D], F32) for i in range(3)]
            sqj = sb0("sqj", [128, D], BF16)
            sqj2 = sb0("sqj2", [128, D], BF16)
            st1 = [sb0(f"st1{i}", [128, 8], F32) for i in range(3)]
            xntk = [sb0(f"xntk{i}", [128, D], BF16) for i in range(3)]
            xc = [sb0(f"xc{i}", [128, D], F32) for i in range(3)]
            epscol = sb0("epscol", [128, 1], F32)
            rcol = [sb0(f"rcol{i}", [128, 2], F32) for i in range(3)]
            p0rot = Rot(6)
            xtk_cnt = [0]
            pend0 = []
            p.op("vector", lambda e: e.memset(epscol[:], EPS), "v2")

            def emit_tile_transposes(tt, ss, ttl, dst, lnv):
                p.wait("tensor", "lnA", lnv)
                if tt == 0:
                    p.wait("tensor", "D_c", cst_v)
                for dc in range(16):
                    p0rot.acquire(p, "tensor", tt * 16 + dc)
                    tv = p.op("tensor", lambda e, s=ss, dc=dc, r=(tt * 16 + dc) % 6: e.transpose(
                        ps0[r][:, 0:128], xntk[s][:, dc * 128:(dc + 1) * 128], ident[:]), "xt")
                    eng = "vector" if dc % 2 == 0 else "scalar"
                    p.wait(eng, "xt", tv)
                    if eng == "vector":
                        ev2 = p.op(eng, lambda e, d=dst, dc=dc, ttl=ttl, r=(tt * 16 + dc) % 6: e.tensor_copy(
                            d[:, dc * OWN + ttl * 128: dc * OWN + (ttl + 1) * 128], ps0[r][:, 0:128]), "xe")
                        p0rot.release(tt * 16 + dc, "xe", ev2)
                    else:
                        ev2 = p.op(eng, lambda e, d=dst, dc=dc, ttl=ttl, r=(tt * 16 + dc) % 6: e.activation(
                            d[:, dc * OWN + ttl * 128: dc * OWN + (ttl + 1) * 128], ps0[r][:, 0:128],
                            AF.Copy, bias=0.0, scale=1.0), "xe2")
                        p0rot.release(tt * 16 + dc, "xe2", ev2)

            for tt in range(16):
                own = tt < 8
                src = (x_own if own else x_oth)
                dst = (xn_own if own else xn_oth)
                ttl = tt % 8
                b3 = tt % 3
                ss = tt % 3
                xcs = tt % 3
                if tt >= 3:
                    p.wait("sync", "lnA", tt - 3 + 1)
                xsem = f"D_x{tt % 3}"
                xv = p.dma("sync", xt[b3][:], src[ttl * 128:(ttl + 1) * 128, :], xsem)
                p.wait("vector", xsem, xv)
                if tt >= 3:
                    p.wait("vector", "lnA", tt - 3 + 1)
                p.wait("scalar", xsem, xv)
                if tt >= 3:
                    p.wait("scalar", "lnA", tt - 3 + 1)
                v1 = p.op("scalar", lambda e, b=b3, s=ss: e.activation(
                    sqj2[:], xt[b][:], AF.Copy, bias=0.0, scale=1.0,
                    accum_out=st1[s][:, 0:1]), "v1")
                s2v = p.op("scalar", lambda e, b=b3, s=ss: e.activation(
                    sqj[:], xt[b][:], AF.Square, bias=0.0, scale=1.0,
                    accum_out=st1[s][:, 1:2]), "sq")
                p.wait("vector", "v1", v1)
                mv = p.op("vector", lambda e, s=ss: e.tensor_scalar(
                    st1[s][:, 2:3], st1[s][:, 0:1], -1.0 / D, None, ALU.mult, ALU.bypass), "v2")
                p.wait("vector", "v2", mv)
                m2v = p.op("vector", lambda e, s=ss: e.tensor_scalar(
                    st1[s][:, 4:5], st1[s][:, 2:3], st1[s][:, 2:3], None, ALU.mult, ALU.bypass), "v2")
                p.wait("vector", "sq", s2v)
                p.wait("vector", "v2", m2v)
                vpev = p.op("vector", lambda e, s=ss: e.scalar_tensor_tensor(
                    st1[s][:, 5:6], st1[s][:, 1:2], 1.0 / D, st1[s][:, 4:5],
                    ALU.mult, ALU.subtract), "v2")
                p.wait("scalar", "v2", vpev)
                sdv = p.op("scalar", lambda e, s=ss: e.activation(
                    st1[s][:, 6:7], st1[s][:, 5:6], AF.Sqrt, bias=epscol[:, 0:1], scale=1.0), "s2")
                if tt >= 2:
                    p.wait("vector", "lnA", tt - 2 + 1)
                xcv = p.op("vector", lambda e, b=b3, s=ss, xs=xcs: e.tensor_scalar(
                    xc[xs][:], xt[b][:], st1[s][:, 2:3], None, ALU.add, ALU.bypass), "xcv")
                p.wait("vector", "s2", sdv)
                rsv = p.op("vector", lambda e, s=ss: e.reciprocal(rcol[s][:, 0:1], st1[s][:, 6:7]), "v3")
                p.wait("scalar", "v3", rsv)
                p.wait("scalar", "xcv", xcv)
                if tt >= 3:
                    p.wait("scalar", "xt", 16 * (tt - 2))
                if tt >= 3:
                    p.wait("scalar", "xtk", max(0, xtk_cnt[0] - 2))
                lnv = p.op("scalar", lambda e, s=ss, xs=xcs: e.activation(
                    xntk[s][:], xc[xs][:], AF.Copy, bias=0.0, scale=rcol[s][:, 0:1]), "lnA")
                if own:
                    p.wait("vector", "lnA", lnv)
                    p.op("vector", lambda e, s=ss, ttl=ttl: e.tensor_copy(
                        xn_tok[:, ttl * D:(ttl + 1) * D], xntk[s][:]), "xtk")
                    xtk_cnt[0] += 1
                pend0.append((tt, ss, ttl, dst, lnv))
                if len(pend0) > 1:
                    emit_tile_transposes(*pend0.pop(0))
                if tt == 15:
                    while pend0:
                        emit_tile_transposes(*pend0.pop(0))

        p.barrier("b01")
        if debug:
            p.dma("sync", dbg_xn[:, 0:16 * OWN], xn_own[:], "D_sd")
            p.dma("sync", dbg_xn[:, 16 * OWN:], xn_oth[:], "D_sd")

        if stage >= 1:
         with ExitStack() as ph1:
            sb1 = lambda *a, **k: ph1.enter_context(nc.sbuf_tensor(*a, **k))
            pP = [ph1.enter_context(nc.psum_tensor(f"pP{i}", [128, 1024], F32)) for i in range(2)]
            pA = [ph1.enter_context(nc.psum_tensor(f"pA{i}", [128, 512], F32)) for i in range(2)]
            pT = [ph1.enter_context(nc.psum_tensor(f"pT{i}", [128, 1024], BF16)) for i in range(2)]
            qkvw = [sb1("qkvw0", [128, 3 * 16 * 128], BF16)]
            K_sb = [sb1("K0", [128, S], BF16)]
            Q_sb = [sb1(f"Q{i}", [128, OWN], BF16) for i in range(2)]
            V_sb = [sb1("V0", [128, S], BF16)]
            VT_sb = [sb1(f"VT{i}", [128, 16 * 129], BF16) for i in range(2)]
            PT_sb = [sb1(f"PT{i}", [128, 16 * 512], BF16) for i in range(2)]
            AT_sb = [sb1(f"AT{i}", [128, 128], BF16) for i in range(2)]
            rc_sb = [sb1(f"rc{i}", [128, 1], F32) for i in range(2)]
            rS = Rot(2)
            rQp = Rot(4)
            rAp = Rot(2)
            rTp = Rot(2)
            rAT = Rot(2)

            for i in range(2):
                p.op("vector", lambda e, i=i: e.memset(
                    VT_sb[i][:].rearrange("p (kc c) -> p kc c", c=129)[:, :, 128:129], 1.0), "vte")

            def xn_rhs(dc, c0, n):
                if c0 < OWN:
                    assert c0 + n <= OWN
                    return xn_own[:, dc * OWN + c0: dc * OWN + c0 + n]
                assert c0 + n <= 2 * OWN
                return xn_oth[:, dc * OWN + c0 - OWN: dc * OWN + c0 - OWN + n]

            sched = {0: [(kc, 'diag', kc) for kc in range(4)] + [(kc, 'data', None) for kc in range(8, 16)],
                     1: [(kc, 'plain', None) for kc in range(4)] + [(kc, 'diag', kc - 4) for kc in range(4, 8)]
                        + [(kc, 'data', None) for kc in range(8, 16)]}

            qknt = 0
            unit = -1
            avdone = {}
            ptv_maps = {}
            pending_post = []

            def emit_qkv(h, b):
                nonlocal qknt
                wv = wq_v[h]
                p.wait("tensor", "D_wq0", wv)
                p.wait("tensor", "pt", p.val("pt"))
                for which, nchunks, dstl in ((1, 4, 'K'), (2, 4, 'V'), (0, 2, 'Q')):
                    for c4 in range(nchunks):
                        rQp.acquire(p, "tensor", qknt)
                        t2, hf = (qknt // 2) % 2, qknt % 2
                        for dc in range(16):
                            p.op("tensor", lambda e, b=b, w=which, dc=dc, c4=c4, t2=t2, hf=hf: e.matmul(
                                pP[t2][:, hf * 512:(hf + 1) * 512],
                                qkvw[0][:, (w * 16 + dc) * 128:(w * 16 + dc + 1) * 128],
                                xn_rhs(dc, c4 * 512, 512),
                                start=(dc == 0), stop=(dc == 15)),
                                "mmQ" if dc == 15 else None, 1)
                        mv = p.val("mmQ")
                        p.wait("vector", "mmQ", mv)
                        bcol = bqc_sb[:, which * 16 + h: which * 16 + h + 1]
                        if dstl == 'K':
                            dst = K_sb[0][:, c4 * 512:(c4 + 1) * 512]
                        elif dstl == 'V':
                            dst = V_sb[0][:, c4 * 512:(c4 + 1) * 512]
                        else:
                            dst = Q_sb[b][:, c4 * 512:(c4 + 1) * 512]
                        ev = p.op("vector", lambda e, dst=dst, t2=t2, hf=hf, bcol=bcol: e.tensor_scalar(
                            dst, pP[t2][:, hf * 512:(hf + 1) * 512], bcol, None, ALU.add, ALU.bypass), "kqv")
                        rQp.release(qknt, "kqv", ev)
                        qknt += 1
                relw[h] = p.val("mmQ")
                kv = p.val("kqv")
                p.wait("tensor", "kqv", kv)
                if h >= 2:
                    p.wait("vector", "av", av_after.get(h - 2, 0))
                for kc in range(16):
                    rTp.acquire(p, "tensor", trn[0])
                    tv = p.op("tensor", lambda e, b=b, kc=kc, r=trn[0] % 2: e.transpose(
                        pT[r][:, 0:128], V_sb[0][:, kc * 128:(kc + 1) * 128], ident[:]), "trp")
                    p.wait("vector", "trp", tv)
                    ev = p.op("vector", lambda e, b=b, kc=kc, r=trn[0] % 2: e.tensor_copy(
                        VT_sb[b][:, kc * 129: kc * 129 + 128], pT[r][:, 0:128]), "vte")
                    rTp.release(trn[0], "vte", ev)
                    trn[0] += 1
                relvs[h] = p.val("trp")
                vte_after[h] = p.val("vte")

            def emit_scores(h, qb, b, u):
                ptb = u % 2
                if u >= 2:
                    p.wait("scalar", "av", avdone[u - 2])
                ptv_map = {}
                p.wait("tensor", "kqv", p.val("kqv"))
                chunks = sched[qb]
                for pi in range(0, len(chunks), 2):
                    pair = chunks[pi:pi + 2]
                    rS.acquire(p, "tensor", sck[0])
                    pr = sck[0] % 2
                    smv = None
                    for hf, (kc, kind, j2) in enumerate(pair):
                        fin = (hf == 1 and kind != 'diag')
                        smv = p.op("tensor", lambda e, b=b, kc=kc, qb=qb, pr=pr, hf=hf,
                             last=(kind != 'diag'): e.matmul(
                            pP[pr][:, hf * 512:(hf + 1) * 512], K_sb[0][:, kc * 128:(kc + 1) * 128],
                            Q_sb[b][:, qb * 512:(qb + 1) * 512],
                            start=True, stop=last), "mmS" if fin else None)
                        if kind == 'diag':
                            smv = p.op("tensor", lambda e, j2=j2, pr=pr, hf=hf: e.matmul(
                                pP[pr][:, hf * 512:(hf + 1) * 512], ibig[:],
                                tri_sb[:, j2 * 512:(j2 + 1) * 512],
                                start=False, stop=True), "mmS" if hf == 1 else None)
                    p.wait("scalar", "mmS", smv)
                    kc0, kind0, _ = pair[0]
                    if kind0 == 'diag':
                        bias = expb_sb[:, 17:18]
                    elif kind0 == 'plain':
                        bias = expb_sb[:, 16:17]
                    else:
                        bias = expb_sb[:, qb * 8 + (kc0 - 8): qb * 8 + (kc0 - 8) + 1]
                    ev = p.op("scalar", lambda e, ptb=ptb, kc0=kc0, pr=pr, bias=bias: e.activation(
                        PT_sb[ptb][:, kc0 * 512:(kc0 + 2) * 512], pP[pr][:], AF.Exp,
                        bias=bias, scale=SCL), "pt")
                    rS.release(sck[0], "pt", ev)
                    for (kc, _, _) in pair:
                        ptv_map[kc] = ev
                    sck[0] += 1
                ptv_maps[u] = ptv_map

            def post_av(h, qb, ql, av_idx, avv):
                pr = av_idx % 2
                p.wait("vector", "av", avv)
                rcv = p.op("vector", lambda e, pr=pr, r=av_idx % 2: e.reciprocal(
                    rc_sb[r][:], pA[pr][:, 128:129]), "rc")
                p.wait("scalar", "rc", rcv)
                rAT.acquire(p, "scalar", av_idx)
                atv = p.op("scalar", lambda e, pr=pr, r=av_idx % 2: e.activation(
                    AT_sb[r][:], pA[pr][:, 0:128], AF.Copy, bias=0.0, scale=rc_sb[r][:, 0:1]), "at")
                rAp.release(av_idx, "at", atv)
                p.wait("tensor", "at", atv)
                rTp.acquire(p, "tensor", trn[0])
                tv = p.op("tensor", lambda e, r=av_idx % 2, tr=trn[0] % 2: e.transpose(
                    pT[tr][:, 0:128], AT_sb[r][:], ident[:]), "trp")
                rAT.release(av_idx, "trp", tv)
                p.wait("scalar", "trp", tv)
                col = h * OWN + qb * 512 + ql * 128
                ev = p.op("scalar", lambda e, col=col, tr=trn[0] % 2: e.activation(
                    attn[:, col:col + 128], pT[tr][:, 0:128],
                    AF.Copy, bias=0.0, scale=1.0), "ao")
                rTp.release(trn[0], "ao", ev)
                trn[0] += 1

            def emit_av(h, qb, b, u):
                ptb = u % 2
                p.wait("tensor", "vte", vte_after[h])
                pend = pending_post
                for ql in range(4):
                    rAp.acquire(p, "tensor", avk[0])
                    pr = avk[0] % 2
                    chunks = [c[0] for c in sched[qb]]
                    avv = None
                    for i, kc in enumerate(chunks):
                        if ql == 0:
                            p.wait("tensor", "pt", ptv_maps[u][kc])
                        avv = p.op("tensor", lambda e, ptb=ptb, kc=kc, ql=ql, b=b, pr=pr,
                             st=(i == 0), sp=(i == len(chunks) - 1): e.matmul(
                            pA[pr][:, 0:129], PT_sb[ptb][:, kc * 512 + ql * 128: kc * 512 + (ql + 1) * 128],
                            VT_sb[b][:, kc * 129: kc * 129 + 129],
                            start=st, stop=sp), "av" if i == len(chunks) - 1 else None)
                    if pend:
                        post_av(*pend.pop(0))
                    pend.append((h, qb, ql, avk[0], avv))
                    avk[0] += 1
                avdone[u] = p.val("av")

            wq_v = {}

            def emit_wq_dma(h):
                if h >= 1:
                    p.wait("gpsimd", "mmQ", relw[h - 1])
                wq_v[h] = p.dma("gpsimd",
                                qkvw[0][:].rearrange("p (w dc hd) -> p w dc hd", w=3, dc=16),
                                wqkv_t[h], "D_wq0")

            sck = [0]
            avk = [0]
            trn = [0]
            relw = {}
            relvs = {}
            vte_after = {}
            ms_after = {}
            av_after = {}
            emit_wq_dma(0)
            for h in range(H):
                b = h % 2
                if h >= 1:
                    p.wait("vector", "mmS", ms_after.get(h - 1, 0))
                    p.wait("vector", "trp", relvs.get(h - 1, 0))
                if h >= 2:
                    p.wait("vector", "mmS", ms_after.get(h - 2, 0))
                emit_qkv(h, b)
                if h + 1 < H:
                    emit_wq_dma(h + 1)
                for qb in range(2):
                    unit += 1
                    emit_scores(h, qb, b, unit)
                    emit_av(h, qb, b, unit)
                ms_after[h] = p.val("mmS")
                av_after[h] = p.val("av")
            while pending_post:
                post_av(*pending_post.pop(0))

        s_xnoth.close()
        p.barrier("b12")
        if debug and stage >= 1:
            p.dma("sync", dbg_attn[:], attn[:], "D_sd")

        R = sb("R", [128, 8 * D], F32)
        wT_sb = sb("wT_sb", [8, OWN], BF16)
        if stage >= 2:
         with ExitStack() as ph2a:
            sb2 = lambda *a, **k: ph2a.enter_context(nc.sbuf_tensor(*a, **k))
            pR = [ph2a.enter_context(nc.psum_tensor(f"pR{i}", [128, 512], F32)) for i in range(2)]
            pGm = ph2a.enter_context(nc.psum_tensor("pGm", [128, 512], F32))
            pW8 = ph2a.enter_context(nc.psum_tensor("pW8", [8, 512], F32))
            wo_sb = [sb2(f"wo{i}", [128, 16 * 512], BF16) for i in range(2)]
            sg_sb = sb2("sg", [128, 8 * 8], F32)
            tg_sb = sb2("tg", [128, 8 * 8 * 3], F32)
            rg_sb = sb2("rg", [128, 8 * 2], F32)
            w_sb = w_sbp
            rR = Rot(2)
            rwo = Rot(2)

            for tt in range(8):
                for dc in range(16):
                    p.op("tensor", lambda e, tt=tt, dc=dc: e.matmul(
                        pGm[:, tt * 8:(tt + 1) * 8],
                        xn_own[:, dc * OWN + tt * 128: dc * OWN + (tt + 1) * 128],
                        gw_sb[:, dc * 8:(dc + 1) * 8],
                        start=(dc == 0), stop=(dc == 15)),
                        "mmG" if (tt == 7 and dc == 15) else None, 1)
            gv = p.val("mmG")
            p.wait("scalar", "mmG", gv)
            p.op("scalar", lambda e: e.activation(
                sg_sb[:], pGm[:, 0:64], AF.Sigmoid, bias=0.0, scale=1.0), "sgm")
            sgv = p.val("sgm")
            p.wait("vector", "sgm", sgv)
            for tt in range(8):
                p.op("vector", lambda e, tt=tt: e.reduce_max(
                    rg_sb[:, tt * 2: tt * 2 + 1], sg_sb[:, tt * 8:(tt + 1) * 8], axis=AX.X), "g1")
            p.wait("vector", "g1", p.val("g1"))
            for tt in range(8):
                p.op("vector", lambda e, tt=tt: e.tensor_scalar(
                    tg_sb[:, tt * 8:(tt + 1) * 8], sg_sb[:, tt * 8:(tt + 1) * 8],
                    rg_sb[:, tt * 2: tt * 2 + 1], -1e30, ALU.is_ge, ALU.mult), "g1")
            p.wait("vector", "g1", p.val("g1"))
            for tt in range(8):
                p.op("vector", lambda e, tt=tt: e.tensor_add(
                    tg_sb[:, 64 + tt * 8: 64 + (tt + 1) * 8], tg_sb[:, tt * 8:(tt + 1) * 8],
                    sg_sb[:, tt * 8:(tt + 1) * 8]), "g1")
            p.wait("vector", "g1", p.val("g1"))
            for tt in range(8):
                p.op("vector", lambda e, tt=tt: e.reduce_max(
                    rg_sb[:, tt * 2 + 1: tt * 2 + 2], tg_sb[:, 64 + tt * 8: 64 + (tt + 1) * 8], axis=AX.X), "g1")
            p.wait("vector", "g1", p.val("g1"))
            for tt in range(8):
                p.op("vector", lambda e, tt=tt: e.tensor_scalar(
                    tg_sb[:, 128 + tt * 8: 128 + (tt + 1) * 8], sg_sb[:, tt * 8:(tt + 1) * 8],
                    rg_sb[:, tt * 2 + 1: tt * 2 + 2], None, ALU.is_ge, ALU.bypass), "g1")
            p.wait("vector", "g1", p.val("g1"))
            for tt in range(8):
                p.op("vector", lambda e, tt=tt: e.tensor_mul(
                    w_sb[:, tt * 8:(tt + 1) * 8], tg_sb[:, 128 + tt * 8: 128 + (tt + 1) * 8],
                    sg_sb[:, tt * 8:(tt + 1) * 8]), "g2")
            p.wait("vector", "g2", p.val("g2"))
            p.op("vector", lambda e: e.tensor_copy(wf_sbp[:], w_sbp[:]), "g2")
            g2v = p.val("g2")
            if debug:
                p.wait("sync", "g2", g2v)
                p.dma("sync", dbg_w[:], w_sb[:], "D_sd")
            p.wait("tensor", "g2", g2v)
            for half in range(2):
                for i in range(4):
                    tt = half * 4 + i
                    p.op("tensor", lambda e, tt=tt, i=i: e.matmul(
                        pW8[0:8, i * 128:(i + 1) * 128], w_sb[:, tt * 8:(tt + 1) * 8],
                        ident[:], start=True, stop=True),
                        "mmW" if i == 3 else None, 1)
                wv = p.val("mmW")
                p.wait("scalar", "mmW", wv)
                sv = p.op("scalar", lambda e, half=half: e.activation(
                    wT_sb[0:8, half * 512:(half + 1) * 512], pW8[0:8, 0:512],
                    AF.Copy, bias=0.0, scale=1.0), "wTe")
                p.wait("tensor", "wTe", sv)

            ork = [0]
            wo_v = {}

            def emit_wo_dma(dc):
                if dc >= 2:
                    p.wait("gpsimd", "mmO", 8 * (dc - 1))
                wo_v[dc] = p.dma("gpsimd",
                                 wo_sb[dc % 2][:].rearrange("p (dcin n) -> p dcin n", dcin=16),
                                 w_outT[dc], f"D_wo{dc % 2}")

            emit_wo_dma(0)
            emit_wo_dma(1)
            for dc in range(4):
                if dc + 2 < 4:
                    emit_wo_dma(dc + 2)
                p.wait("tensor", f"D_wo{dc % 2}", wo_v[dc])
                for tt in range(8):
                    rR.acquire(p, "tensor", ork[0])
                    pr = ork[0] % 2
                    p.op("tensor", lambda e, dc=dc, pr=pr: e.matmul(
                        pR[pr][:], ones1[0:1, :], bor_sb[0:1, dc * 512:(dc + 1) * 512],
                        start=True, stop=False), None)
                    for dcin in range(16):
                        p.op("tensor", lambda e, dcin=dcin, tt=tt, dc=dc, pr=pr: e.matmul(
                            pR[pr][:], attn[:, dcin * OWN + tt * 128: dcin * OWN + (tt + 1) * 128],
                            wo_sb[dc % 2][:, dcin * 512:(dcin + 1) * 512],
                            start=False, stop=(dcin == 15)),
                            "mmO" if dcin == 15 else None, 1)
                    ov = p.val("mmO")
                    p.wait("scalar", "mmO", ov)
                    ev = p.op("scalar", lambda e, tt=tt, dc=dc, pr=pr: e.activation(
                        R[:, tt * D + dc * 512: tt * D + (dc + 1) * 512], pR[pr][:],
                        AF.Copy, bias=0.0, scale=1.0), "rop")
                    rR.release(ork[0], "rop", ev)
                    ork[0] += 1

        s_attn.close()
        p.barrier("b2ab")

        CAP = 384
        if stage >= 3:
         with ExitStack() as ph2o:
            sbo = lambda *a, **k: ph2o.enter_context(nc.sbuf_tensor(*a, **k))
            w13 = [sbo(f"w13_{i}", [128, 2 * 16 * 128], BF16) for i in range(2)]
            w2s = sbo("w2s", [128, 8 * D], BF16)
            ts1 = [sbo(f"ts1{i}", [128, 512], F32) for i in range(2)]
            mm3_after = {}
            mm2_after = {}
            w13_v = {}
            w2_v = {}

            def w13_src(ee, hc, which):
                if ee == 0:
                    return (ws1h if which == 0 else ws3h)[hc]
                e = ee - 1
                return (w1h if which == 0 else w3h)[e, hc]

            def w2_src(ee):
                if ee == 0:
                    return ws2h[:, :, :]
                return w2h[ee - 1]

            def emit_w13_dma(ee, hc):
                idx = ee * 8 + hc
                if idx >= 2:
                    p.wait("gpsimd", "mm3", mm3_after[idx - 2])
                for which in range(2):
                    v = p.dma("gpsimd",
                              w13[idx % 2][:, which * 2048:(which + 1) * 2048].rearrange(
                                  "p (dc h) -> p dc h", dc=16),
                              w13_src(ee, hc, which),
                              f"D_w13{idx % 2}")
                w13_v[idx] = v

            def emit_w2_chunk(ee, c):
                if ee >= 1:
                    p.wait("gpsimd", "mm2", mm2_after[ee - 1])
                w2_v[ee] = p.dma("gpsimd",
                                 w2s[:, c * 2 * D:(c + 1) * 2 * D].rearrange(
                                     "p (hc n) -> p hc n", hc=2),
                                 w2_src(ee)[:, c * 2:(c + 1) * 2, :],
                                 "D_w2")

            emit_w13_dma(0, 0)
            emit_w13_dma(0, 1)

            with ExitStack() as ph2s:
                sbs = lambda *a, **k: ph2s.enter_context(nc.sbuf_tensor(*a, **k))
                pG1 = [ph2s.enter_context(nc.psum_tensor(f"pG1{i}", [128, 512], F32)) for i in range(2)]
                pG3 = [ph2s.enter_context(nc.psum_tensor(f"pG3{i}", [128, 512], F32)) for i in range(2)]
                pR2 = [ph2s.enter_context(nc.psum_tensor(f"pR2{i}", [128, 512], F32)) for i in range(2)]
                Hsh = sbs("Hsh", [128, 8 * OWN], BF16)
                rG1 = Rot(2)
                rG3 = Rot(2)
                rR2 = Rot(2)
                rTmp = Rot(2)
                gck = [0]
                w2ck = [0]
                for hc in range(8):
                    idx = hc
                    wb13 = idx % 2
                    p.wait("tensor", f"D_w13{idx % 2}", w13_v[idx])
                    for tc in range(2):
                        rG1.acquire(p, "tensor", gck[0])
                        rG3.acquire(p, "tensor", gck[0])
                        pr = gck[0] % 2
                        for dc in range(16):
                            p.op("tensor", lambda e, wb13=wb13, dc=dc, tc=tc, pr=pr: e.matmul(
                                pG1[pr][:], w13[wb13][:, dc * 128:(dc + 1) * 128],
                                xn_own[:, dc * OWN + tc * 512: dc * OWN + (tc + 1) * 512],
                                start=(dc == 0), stop=(dc == 15)),
                                "mm1" if dc == 15 else None, 1)
                        m1v = p.val("mm1")
                        for dc in range(16):
                            p.op("tensor", lambda e, wb13=wb13, dc=dc, tc=tc, pr=pr: e.matmul(
                                pG3[pr][:], w13[wb13][:, 2048 + dc * 128:2048 + (dc + 1) * 128],
                                xn_own[:, dc * OWN + tc * 512: dc * OWN + (tc + 1) * 512],
                                start=(dc == 0), stop=(dc == 15)),
                                "mm3" if dc == 15 else None, 1)
                        m3v = p.val("mm3")
                        rTmp.acquire(p, "scalar", gck[0])
                        tb = gck[0] % 2
                        p.wait("scalar", "mm1", m1v)
                        s1v = p.op("scalar", lambda e, pr=pr, tb=tb: e.activation(
                            ts1[tb][:], pG1[pr][:], AF.Silu, bias=0.0, scale=1.0), "si")
                        rG1.release(gck[0], "si", s1v)
                        p.wait("vector", "mm3", m3v)
                        p.wait("vector", "si", s1v)
                        hv = p.op("vector", lambda e, pr=pr, tb=tb, hc=hc, tc=tc: e.tensor_mul(
                            Hsh[:, hc * OWN + tc * 512: hc * OWN + (tc + 1) * 512],
                            ts1[tb][:], pG3[pr][:]), "hh")
                        rG3.release(gck[0], "hh", hv)
                        rTmp.release(gck[0], "hh", hv)
                        gck[0] += 1
                    mm3_after[idx] = p.val("mm3")
                    if idx + 2 < 72:
                        emit_w13_dma(*divmod(idx + 2, 8))
                    if hc % 2 == 0:
                        emit_w2_chunk(0, hc // 2)
                hhv = p.val("hh")
                p.wait("tensor", "D_w2", w2_v[0])
                p.wait("tensor", "hh", hhv)
                for tt in range(8):
                    for dc in range(4):
                        rR2.acquire(p, "tensor", w2ck[0])
                        pr = w2ck[0] % 2
                        for hc in range(8):
                            p.op("tensor", lambda e, hc=hc, tt=tt, dc=dc, pr=pr: e.matmul(
                                pR2[pr][:], Hsh[:, hc * OWN + tt * 128: hc * OWN + (tt + 1) * 128],
                                w2s[:, hc * D + dc * 512: hc * D + (dc + 1) * 512],
                                start=(hc == 0), stop=(hc == 7)),
                                "mm2" if hc == 7 else None, 1)
                        m2v = p.val("mm2")
                        p.wait("vector", "mm2", m2v)
                        av = p.op("vector", lambda e, tt=tt, dc=dc, pr=pr: e.tensor_add(
                            R[:, tt * D + dc * 512: tt * D + (dc + 1) * 512],
                            R[:, tt * D + dc * 512: tt * D + (dc + 1) * 512], pR2[pr][:]), "rad")
                        rR2.release(w2ck[0], "rad", av)
                        w2ck[0] += 1
                mm2_after[0] = p.val("mm2")
                for c in range(4):
                    emit_w2_chunk(1, c)

            s_xnF.close()
            p.barrier("b2bc")

            iota_cap_d = nc.inline_tensor(
                np.tile(np.arange(1, CAP + 1, dtype=np.float32), (128, 1)), name="iotacapc")
            identf_d = nc.inline_tensor(np.eye(8, dtype=np.float32), name="identfc")
            with ExitStack() as ph2e:
                sbe = lambda *a, **k: ph2e.enter_context(nc.sbuf_tensor(*a, **k))
                pG1e = [ph2e.enter_context(nc.psum_tensor(f"pG1eb{i}", [128, 512], F32)) for i in range(2)]
                pXS = pG1e
                pG3e = [ph2e.enter_context(nc.psum_tensor(f"pG3eb{i}", [128, 512], F32)) for i in range(2)]
                pT2 = [ph2e.enter_context(nc.psum_tensor(f"pT2{i}", [128, 1024], BF16)) for i in range(2)]
                pD = [ph2e.enter_context(nc.psum_tensor(f"pD{i}", [128, 512], F32)) for i in range(2)]
                xs_sb = sbe("xs_sb", [128, 16 * CAP], BF16)
                Hsp = sbe("Hsp", [128, 8 * CAP], BF16)
                rout = sbe("rout", [128, 3 * D], BF16)
                P_sb = sbe("P_sb", [128, 8 * CAP], BF16)
                PTs = [sbe(f"PTs{i}", [128, 3 * OWN], BF16) for i in range(2)]
                posm = sbe("posm", [128, OWN], F32)
                P01 = [sbe(f"P01_{i}", [128, CAP], BF16) for i in range(2)]
                poscol = sbe("poscol", [128, 8 * 8], F32)
                mcol_sb = sbe("mcol_sb", [128, 8 * 8], F32)
                iota_cap = sbe("iota_cap", [128, CAP], F32)
                identf = sbe("identf", [8, 8], F32)
                p.dma("sync", iota_cap[:], iota_cap_d[:, :], "D_c2")
                p.dma("sync", identf[:], identf_d[:, :], "D_c2")
                ic_v = p.val("D_c2")

                wtv = p.val("wTe")
                p.wait("vector", "wTe", wtv)
                mv = p.op("vector", lambda e: e.tensor_scalar(
                    posm[32:40, :], wT_sb[0:8, :], 0.0, None, ALU.is_gt, ALU.bypass), "pv")
                p.wait("vector", "pv", mv)
                cv3 = p.op("vector", lambda e: e.tensor_tensor_scan(
                    posm[0:8, :], posm[32:40, :], posm[32:40, :], 0.0, ALU.add, ALU.bypass), "pv")
                p.op("vector", lambda e: e.tensor_scalar(
                    mcol_sb[:], wf_sbp[:], 0.0, None, ALU.is_gt, ALU.bypass), "pv")
                p.wait("tensor", "pv", p.val("pv"))
                p.wait("tensor", "D_c2", ic_v)
                for tc in range(8):
                    if tc >= 1:
                        p.wait("tensor", "pce", tc)
                    tv = p.op("tensor", lambda e, tc=tc: e.transpose(
                        pXS[0][:, 0:8], posm[0:8, tc * 128:(tc + 1) * 128], identf[:]), "ptr")
                    p.wait("vector", "ptr", tv)
                    pcv = p.op("vector", lambda e, tc=tc: e.tensor_copy(
                        poscol[:, tc * 8:(tc + 1) * 8], pXS[0][:, 0:8]), "pce")
                p.wait("tensor", "pce", p.val("pce"))
                p.wait("tensor", "D_c2", ic_v)

                rXS = Rot(2)
                rG1e_shared = rXS
                rPTrot = Rot(2)
                ptk = [0]
                poolk = [0]
                rG1e = Rot(2)
                rG3e = Rot(2)
                rD = Rot(2)
                rTmpE = Rot(2)
                xsk = [0]
                gck = [0]
                dk = [0]
                mmxs_after = {}
                mm2b_after = {}
                mmsc_after = {}

                pb_after = {}
                ptb_after = {}

                def emit_ptbuild(ex):
                    ptbuf = ex % 2
                    if ex >= 2:
                        p.wait("vector", "mmSC", mmsc_after[ex - 2])
                    for tc in range(8):
                        if ptk[0] >= 2:
                            p.wait("vector", "pttr", p.val("pttr") - 1)
                        pv01 = p.op("vector", lambda e, tc=tc, ex=ex, pb2=tc % 2: e.tensor_scalar(
                            P01[pb2][:], iota_cap[:],
                            poscol[:, tc * 8 + ex: tc * 8 + ex + 1],
                            mcol_sb[:, tc * 8 + ex: tc * 8 + ex + 1],
                            ALU.is_equal, ALU.mult), "p01")
                        p.wait("tensor", "p01", pv01)
                        for sc in range(3):
                            rPTrot.acquire(p, "tensor", ptk[0])
                            tv2 = p.op("tensor", lambda e, sc=sc, r=ptk[0] % 2, pb2=tc % 2: e.transpose(
                                pT2[r][:, 0:128], P01[pb2][:, sc * 128:(sc + 1) * 128], ident[:]), "pttr")
                            p.wait("scalar", "pttr", tv2)
                            cpv = p.op("scalar", lambda e, sc=sc, tc=tc, r=ptk[0] % 2, ptbuf=ptbuf: e.activation(
                                PTs[ptbuf][:, sc * OWN + tc * 128:(sc * OWN + (tc + 1) * 128)],
                                pT2[r][:, 0:128], AF.Copy, bias=0.0, scale=1.0), "ptb")
                            rPTrot.release(ptk[0], "ptb", cpv)
                            ptk[0] += 1
                    ptb_after[ex] = p.val("ptb")

                def emit_pbuild(ex):
                    if ex >= 1:
                        p.wait("vector", "mmXS", mmxs_after[ex - 1])
                    p.wait("vector", "pce", p.val("pce"))
                    for tc in range(8):
                        p.op("vector", lambda e, tc=tc, ex=ex: e.tensor_scalar(
                            P_sb[:, tc * CAP:(tc + 1) * CAP], iota_cap[:],
                            poscol[:, tc * 8 + ex: tc * 8 + ex + 1],
                            wf_sbp[:, 8 * tc + ex: 8 * tc + ex + 1],
                            ALU.is_equal, ALU.mult), "pb")
                    pb_after[ex] = p.val("pb")

                emit_pbuild(0)
                emit_ptbuild(0)
                for ex in range(8):
                    ee = ex + 1
                    p.wait("tensor", "pb", pb_after[ex])
                    if ex >= 1:
                        p.wait("vector", "mm1", p.val("mm1"))
                    for dc in range(16):
                        rXS.acquire(p, "tensor", poolk[0])
                        prx = poolk[0] % 2
                        for tc in range(8):
                            p.op("tensor", lambda e, dc=dc, tc=tc, prx=prx: e.matmul(
                                pXS[prx][:, 0:CAP], xn_tok[:, tc * D + dc * 128: tc * D + (dc + 1) * 128],
                                P_sb[:, tc * CAP:(tc + 1) * CAP],
                                start=(tc == 0), stop=(tc == 7)),
                                "mmXS" if tc == 7 else None, 1)
                        xv = p.val("mmXS")
                        p.wait("scalar", "mmXS", xv)
                        ev = p.op("scalar", lambda e, dc=dc, prx=prx: e.activation(
                            xs_sb[:, dc * CAP:(dc + 1) * CAP], pXS[prx][:, 0:CAP],
                            AF.Copy, bias=0.0, scale=1.0), "xse")
                        rXS.release(poolk[0], "xse", ev)
                        poolk[0] += 1
                    mmxs_after[ex] = p.val("mmXS")
                    xse_base = p.val("xse") - 16
                    ptbv = ptb_after[ex]
                    for hc in range(8):
                        idx = ee * 8 + hc
                        wb13 = idx % 2
                        p.wait("tensor", f"D_w13{idx % 2}", w13_v[idx])
                        rXS.acquire(p, "tensor", poolk[0])
                        rG3e.acquire(p, "tensor", gck[0])
                        prg = poolk[0] % 2
                        prg3 = gck[0] % 2
                        for dc in range(16):
                            if hc == 0:
                                p.wait("tensor", "xse", xse_base + dc + 1)
                            p.op("tensor", lambda e, wb13=wb13, dc=dc, prg=prg: e.matmul(
                                pG1e[prg][:, 0:CAP], w13[wb13][:, dc * 128:(dc + 1) * 128],
                                xs_sb[:, dc * CAP:(dc + 1) * CAP],
                                start=(dc == 0), stop=(dc == 15)),
                                "mm1" if dc == 15 else None, 1)
                        m1v = p.val("mm1")
                        for dc in range(16):
                            p.op("tensor", lambda e, wb13=wb13, dc=dc, prg3=prg3: e.matmul(
                                pG3e[prg3][:, 0:CAP], w13[wb13][:, 2048 + dc * 128:2048 + (dc + 1) * 128],
                                xs_sb[:, dc * CAP:(dc + 1) * CAP],
                                start=(dc == 0), stop=(dc == 15)),
                                "mm3" if dc == 15 else None, 1)
                        m3v = p.val("mm3")
                        rTmpE.acquire(p, "scalar", gck[0])
                        tb = gck[0] % 2
                        p.wait("scalar", "mm1", m1v)
                        if ex >= 1 and hc == 0:
                            p.wait("scalar", "mm2", mm2b_after[ex - 1])
                        s1v = p.op("scalar", lambda e, prg=prg, tb=tb: e.activation(
                            ts1[tb][:, 0:CAP], pG1e[prg][:, 0:CAP], AF.Silu, bias=0.0, scale=1.0), "si")
                        rXS.release(poolk[0], "si", s1v)
                        p.wait("vector", "mm3", m3v)
                        p.wait("vector", "si", s1v)
                        if ex >= 1 and hc == 0:
                            p.wait("vector", "mm2", mm2b_after[ex - 1])
                        hv = p.op("vector", lambda e, prg3=prg3, tb=tb, hc=hc: e.tensor_mul(
                            Hsp[:, hc * CAP:(hc + 1) * CAP],
                            ts1[tb][:, 0:CAP], pG3e[prg3][:, 0:CAP]), "hh")
                        rG3e.release(gck[0], "hh", hv)
                        rTmpE.release(gck[0], "hh", hv)
                        gck[0] += 1
                        poolk[0] += 1
                        mm3_after[idx] = p.val("mm3")
                        if idx + 2 < 72:
                            emit_w13_dma(*divmod(idx + 2, 8))

                    hhv = p.val("hh")
                    if ex + 1 < 8:
                        emit_pbuild(ex + 1)
                        emit_ptbuild(ex + 1)
                    p.wait("tensor", "D_w2", w2_v[ee])
                    p.wait("tensor", "hh", hhv)
                    if ex >= 1:
                        p.wait("scalar", "mmSC", mmsc_after[ex - 1])
                    for sc in range(3):
                        for dc in range(4):
                            rD.acquire(p, "tensor", dk[0])
                            prd = dk[0] % 2
                            for hc in range(8):
                                p.op("tensor", lambda e, hc=hc, sc=sc, dc=dc, prd=prd: e.matmul(
                                    pD[prd][:], Hsp[:, hc * CAP + sc * 128: hc * CAP + sc * 128 + 128],
                                    w2s[:, hc * D + dc * 512: hc * D + (dc + 1) * 512],
                                    start=(hc == 0), stop=(hc == 7)),
                                    "mm2" if hc == 7 else None, 1)
                            m2v = p.val("mm2")
                            p.wait("scalar", "mm2", m2v)
                            rv = p.op("scalar", lambda e, sc=sc, dc=dc, prd=prd: e.activation(
                                rout[:, sc * D + dc * 512: sc * D + (dc + 1) * 512], pD[prd][:],
                                AF.Copy, bias=0.0, scale=1.0), "rte")
                            rD.release(dk[0], "rte", rv)
                            dk[0] += 1
                    mm2b_after[ex] = p.val("mm2")
                    mm2_after[ee] = p.val("mm2")
                    if ee + 1 <= 8:
                        for c in range(4):
                            emit_w2_chunk(ee + 1, c)
                    rtev = p.val("rte")
                    p.wait("tensor", "rte", rtev)
                    p.wait("tensor", "ptb", ptbv)
                    for tt in range(8):
                        for dc in range(4):
                            rD.acquire(p, "tensor", dk[0])
                            prd = dk[0] % 2
                            for sc in range(3):
                                p.op("tensor", lambda e, sc=sc, tt=tt, dc=dc, prd=prd, ptbuf=ex % 2: e.matmul(
                                    pD[prd][:], PTs[ptbuf][:, sc * OWN + tt * 128: sc * OWN + (tt + 1) * 128],
                                    rout[:, sc * D + dc * 512: sc * D + (dc + 1) * 512],
                                    start=(sc == 0), stop=(sc == 2)),
                                    "mmSC" if sc == 2 else None, 1)
                            scv = p.val("mmSC")
                            p.wait("vector", "mmSC", scv)
                            av = p.op("vector", lambda e, tt=tt, dc=dc, prd=prd: e.tensor_add(
                                R[:, tt * D + dc * 512: tt * D + (dc + 1) * 512],
                                R[:, tt * D + dc * 512: tt * D + (dc + 1) * 512], pD[prd][:]), "rad")
                            rD.release(dk[0], "rad", av)
                            dk[0] += 1
                    mmsc_after[ex] = p.val("mmSC")

        if debug and stage >= 2:
            p.wait("sync", "rad", p.val("rad"))
            p.dma("sync", dbg_R[:], R[:], "D_sd")
        p.barrier("b2f")

        if stage < 2:
            p.op("vector", lambda e: e.memset(R[:], 0.0), "rad")
        with ExitStack() as phf:
            sbf = lambda *a, **k: phf.enter_context(nc.sbuf_tensor(*a, **k))
            xrl = [sbf(f"xrl{i}", [128, D], F32) for i in range(2)]
            osb = [sbf(f"osb{i}", [128, D], F32) for i in range(2)]
            rX = Rot(2)
            rO = Rot(2)
            p.wait("vector", "rad", p.val("rad"))
            for tt in range(8):
                rX.acquire(p, "gpsimd", tt)
                xv = p.dma("gpsimd", xrl[tt % 2][:], x_own[tt * 128:(tt + 1) * 128, :], f"D_xr{tt % 2}")
                p.wait("vector", f"D_xr{tt % 2}", xv)
                rO.acquire(p, "vector", tt)
                fv = p.op("vector", lambda e, tt=tt: e.tensor_add(
                    osb[tt % 2][:], R[:, tt * D:(tt + 1) * D], xrl[tt % 2][:]), "fin")
                rX.release(tt, "fin", fv)
                p.wait("sync", "fin", fv)
                sv = p.dma("sync", out_c[tt * 128:(tt + 1) * 128, :], osb[tt % 2][:], f"D_st{tt % 2}")
                rO.release(tt, f"D_st{tt % 2}", sv)
            p.wait("sync", "D_st0", p.val("D_st0"))
            p.wait("sync", "D_st1", p.val("D_st1"))
            if debug:
                p.wait("sync", "D_sd", p.val("D_sd"))

        with nc.Block() as block:
            p.finalize(block)
    return nc


_NC_CACHE = {}


def get_nc(debug=False):
    key = bool(debug)
    if key not in _NC_CACHE:
        _NC_CACHE[key] = build(debug)
    return _NC_CACHE[key]


def prep_inputs(L0_attn_x, L0_attn_w_qkv, L0_attn_b_qkv, L0_attn_w_out, L0_attn_b_out,
                L0_ffn_gate_w, L0_ffn_w1, L0_ffn_w2, L0_ffn_w3,
                L0_ffn_ws1, L0_ffn_ws2, L0_ffn_ws3):
    x = np.ascontiguousarray(np.asarray(L0_attn_x, dtype=np.float32))
    w_qkv = np.asarray(L0_attn_w_qkv, dtype=np.float32)
    b_qkv = np.ascontiguousarray(np.asarray(L0_attn_b_qkv, dtype=np.float32))
    w_out = np.asarray(L0_attn_w_out, dtype=np.float32)
    b_outv = np.ascontiguousarray(np.asarray(L0_attn_b_out, dtype=np.float32))
    gate_w = np.asarray(L0_ffn_gate_w, dtype=np.float32)
    w1 = np.asarray(L0_ffn_w1, dtype=np.float32)
    w2 = np.asarray(L0_ffn_w2, dtype=np.float32)
    w3 = np.asarray(L0_ffn_w3, dtype=np.float32)
    ws1 = np.asarray(L0_ffn_ws1, dtype=np.float32)
    ws2 = np.asarray(L0_ffn_ws2, dtype=np.float32)
    ws3 = np.asarray(L0_ffn_ws3, dtype=np.float32)

    shared = {
        "wqkv_t": np.ascontiguousarray(
            w_qkv.reshape(3, H, HD, 16, 128).transpose(1, 4, 0, 3, 2)),
        "bq_col": np.ascontiguousarray(b_qkv.reshape(3, H, HD).transpose(2, 0, 1).reshape(128, 48)),
        "b_qkv": b_qkv,
        "w_outT": np.ascontiguousarray(w_out.T.reshape(16, 128, 4, 512).transpose(2, 1, 0, 3)),
        "b_out": b_outv,
        "gate_wT": np.ascontiguousarray(gate_w.T),
        "w1h": np.ascontiguousarray(
            w1.transpose(0, 2, 1).reshape(NE, 16, 128, 8, 128).transpose(0, 3, 2, 1, 4)),
        "w3h": np.ascontiguousarray(
            w3.transpose(0, 2, 1).reshape(NE, 16, 128, 8, 128).transpose(0, 3, 2, 1, 4)),
        "w2h": np.ascontiguousarray(
            w2.transpose(0, 2, 1).reshape(NE, 8, 128, D).transpose(0, 2, 1, 3)),
        "ws1h": np.ascontiguousarray(
            ws1.T.reshape(16, 128, 8, 128).transpose(2, 1, 0, 3)),
        "ws3h": np.ascontiguousarray(
            ws3.T.reshape(16, 128, 8, 128).transpose(2, 1, 0, 3)),
        "ws2h": np.ascontiguousarray(ws2.T.reshape(8, 128, D).transpose(1, 0, 2)),
    }
    in_maps = []
    for c in range(8):
        b, j = divmod(c, 2)
        m = dict(shared)
        m["x_own"] = np.ascontiguousarray(x[b, j * OWN:(j + 1) * OWN, :])
        m["x_oth"] = np.ascontiguousarray(x[b, (1 - j) * OWN:(2 - j) * OWN, :])
        bias = np.full((128, 18), EXPSH, dtype=np.float32)
        if j == 0:
            bias[:, 0:16] = -1e30
        bias[:, 16] = EXPSH
        bias[:, 17] = EXPSH - BIGM * SCL
        m["expb"] = bias
        in_maps.append(m)
    return in_maps


def kernel(L0_attn_x, L0_attn_w_qkv, L0_attn_b_qkv, L0_attn_w_out, L0_attn_b_out,
           L0_ffn_gate_w, L0_ffn_w1, L0_ffn_w2, L0_ffn_w3,
           L0_ffn_ws1, L0_ffn_ws2, L0_ffn_ws3):
    in_maps = prep_inputs(L0_attn_x, L0_attn_w_qkv, L0_attn_b_qkv, L0_attn_w_out,
                          L0_attn_b_out, L0_ffn_gate_w, L0_ffn_w1, L0_ffn_w2,
                          L0_ffn_w3, L0_ffn_ws1, L0_ffn_ws2, L0_ffn_ws3)
    nc = get_nc(debug=False)
    res = run_bass_kernel_spmd(nc, in_maps, core_ids=list(range(8)))
    out = np.empty((B, S, D), dtype=np.float32)
    for c in range(8):
        b, j = divmod(c, 2)
        out[b, j * OWN:(j + 1) * OWN, :] = res.results[c]["out"]
    return out
